# version 06
# baseline (speedup 1.0000x reference)
import math
import numpy as np
import ml_dtypes
import concourse.bass as bass
import concourse.mybir as mybir
from concourse.bass_utils import run_bass_kernel_spmd

F32 = mybir.dt.float32
BF16 = mybir.dt.bfloat16
I32 = mybir.dt.int32
AF = mybir.ActivationFunctionType
ALU = mybir.AluOpType
AX = mybir.AxisListType

B, N, E = 2, 4096, 2048
T = B * N
NC = 8
TPC = T // NC
NTT = TPC // 128
HEADS, DH = 16, 128
KSEL, S = 512, 1024
IF = 8192
LN_EPS = 1e-5
RMS_EPS = 1e-6
SELW = 2056
BIG = 1.0e6
MAGIC = 12582912.0
INV2PI = 1.0 / (2.0 * math.pi)
TWOPI = 2.0 * math.pi
ISQD = 1.0 / math.sqrt(DH)

import os
DEBUG = [s for s in os.environ.get("KDEBUG", "").split(",") if s]


ENGINES = ("tensor", "vector", "scalar", "gpsimd", "sync")


class _Op:
    __slots__ = ("idx", "eng", "emit", "kind", "sem", "val", "deps", "flag",
                 "eidx")

    def __init__(self, idx, eng, emit, kind):
        self.idx, self.eng, self.emit, self.kind = idx, eng, emit, kind
        self.sem = None
        self.val = None
        self.deps = []
        self.flag = False
        self.eidx = None


class Buf:
    __slots__ = ("name", "last_writer", "readers", "dma_sem", "dma_count")

    def __init__(self, name):
        self.name = name
        self.last_writer = None
        self.readers = []
        self.dma_sem = None
        self.dma_count = 0


class Stream:
    __slots__ = ("name", "sem", "count")

    def __init__(self, name, sem):
        self.name, self.sem, self.count = name, sem, 0


class Prog:
    def __init__(self, nc):
        self.nc = nc
        self.ops = []
        self.eng_ops = {e: [] for e in ENGINES}
        self.eng_sem = {e: nc.alloc_semaphore(f"esem_{e}") for e in
                        ("tensor", "vector", "scalar", "gpsimd")}
        self.streams = {}
        self.final_waits = []
        self.same_engine_implicit = False

    def stream(self, name):
        if name not in self.streams:
            self.streams[name] = Stream(name, self.nc.alloc_semaphore("ss_" + name))
        return self.streams[name]

    def stream_dep(self, name):
        st = self.stream(name)
        return ("rawsem", st.sem, 16 * st.count)

    def _record(self, eng, emit, kind, reads, writes, extra_deps):
        op = _Op(len(self.ops), eng, emit, kind)
        self.ops.append(op)
        self.eng_ops[eng].append(op)
        op.eidx = len(self.eng_ops[eng]) - 1
        for d in extra_deps:
            op.deps.append(d)
        for b in reads:
            if b.last_writer is not None:
                op.deps.append(b.last_writer)
            b.readers.append(op)
        for b in writes:
            if b.last_writer is not None and b.last_writer is not op:
                op.deps.append(b.last_writer)
            for r in b.readers:
                if r is not op:
                    op.deps.append(r)
            b.readers = []
            b.last_writer = op
        for i, d in enumerate(op.deps):
            if (not isinstance(d, tuple) and d.kind == "dma"
                    and isinstance(d.sem, Stream)):
                if d.sem.name == "trp":
                    op.deps[i] = ("rawsem", d.sem.sem, d.val)
                else:
                    op.deps[i] = ("rawsem", d.sem.sem, 16 * d.sem.count)
        return op

    def op(self, eng, emit, reads=(), writes=(), deps=()):
        assert eng != "sync"
        return self._record(eng, emit, "eng", reads, writes, deps)

    def dma(self, eng, emit, reads=(), writes=(), stream=None, deps=()):
        op = self._record(eng, emit, "dma", reads, writes, deps)
        if stream is not None:
            st = self.stream(stream)
            st.count += 1
            op.sem = st
            op.val = 16 * st.count
        else:
            assert len(writes) >= 1
            b = writes[0]
            if b.dma_sem is None:
                b.dma_sem = self.nc.alloc_semaphore("ds_" + b.name)
            if b.dma_count > 0:
                op.deps.append(("rawsem", b.dma_sem, 16 * b.dma_count))
            b.dma_count += 1
            op.sem = b
            op.val = 16 * b.dma_count
        return op

    def coll(self, emit, reads=(), writes=(), deps=()):
        op = self._record("gpsimd", emit, "coll", reads, writes, deps)
        op.sem = self.nc.alloc_semaphore(f"cs_{op.idx}")
        op.val = 1
        return op

    def final_wait_streams(self, names):
        for n in names:
            st = self.stream(n)
            self.final_waits.append((st.sem, 16 * st.count))

    def _completion(self, dep):
        if isinstance(dep, tuple):
            return dep[1], dep[2]
        if dep.kind == "eng":
            dep.flag = True
            return None, dep
        if dep.kind == "coll":
            return dep.sem, dep.val
        if isinstance(dep.sem, Stream):
            return dep.sem.sem, dep.val
        return dep.sem.dma_sem, dep.val

    def finish(self):
        nc = self.nc
        waitlists = []
        for op in self.ops:
            wl = []
            for dep in op.deps:
                if not isinstance(dep, tuple):
                    assert dep.idx < op.idx
                    if (dep.kind == "eng" and dep.eng == op.eng
                            and (self.same_engine_implicit
                                 or op.eng == "tensor")):
                        continue
                wl.append(self._completion(dep))
            waitlists.append(wl)
        cnt = {e: 0 for e in ENGINES}
        for op in self.ops:
            if op.kind == "eng" and op.flag:
                cnt[op.eng] += 1
                op.val = cnt[op.eng]
                op.sem = self.eng_sem[op.eng]
        resolved = []
        for wl in waitlists:
            resolved.append([(val.sem, val.val) if sem is None else (sem, val)
                             for sem, val in wl])
        know = {e: {} for e in ENGINES}
        for op in self.ops:
            k = know[op.eng]
            wl = []
            for sem, val in resolved[op.idx]:
                key = sem.num if hasattr(sem, "num") else id(sem)
                if k.get(key, -1) >= val:
                    continue
                k[key] = val
                wl.append((sem, val))
            resolved[op.idx] = wl
        with nc.Block() as block:
            def mk(engname):
                def run(eng):
                    for op in self.eng_ops[engname]:
                        for sem, val in resolved[op.idx]:
                            eng.wait_ge(sem, val)
                        inst = op.emit(eng)
                        if op.kind == "eng":
                            if op.flag:
                                inst.then_inc(op.sem, 1)
                        elif op.kind == "coll":
                            inst.then_inc(op.sem, 1)
                        else:
                            sem = (op.sem.sem if isinstance(op.sem, Stream)
                                   else op.sem.dma_sem)
                            inst.then_inc(sem, 16)
                    if engname == "sync":
                        for sem, val in self.final_waits:
                            eng.wait_ge(sem, val)
                return run
            block.tensor(mk("tensor"))
            block.vector(mk("vector"))
            block.scalar(mk("scalar"))
            block.gpsimd(mk("gpsimd"))
            block.sync(mk("sync"))


class Arena:

    def __init__(self, nc, base, size):
        self.nc = nc
        self.base = base
        self.size = size
        self.live = {}
        self.freed = []
        self.n = 0

    def alloc(self, name, shape, dtype, nbufs=1, align=512):
        per = int(np.prod(shape[1:])) * mybir.dt.size(dtype)
        per = (per + align - 1) // align * align
        spans = sorted((o, e) for (o, e, _) in self.live.values())
        off = 0
        for (s, e) in spans:
            if off + per <= s:
                break
            off = max(off, e)
        assert off + per <= self.size, \
            f"arena overflow: {name} needs {per}B at {off}, cap {self.size}"
        war = []
        keep = []
        for (fo, fe, ops) in self.freed:
            if fo < off + per and off < fe:
                war.extend(ops)
                if fo < off:
                    keep.append((fo, off, ops))
                if fe > off + per:
                    keep.append((off + per, fe, ops))
            else:
                keep.append((fo, fe, ops))
        self.freed = keep
        bufs = [Buf(f"{name}_{i}") for i in range(nbufs)]
        for b in bufs:
            b.readers = list(war)
        self.n += 1
        h = self.nc.alloc_sbuf_tensor_at(f"ar{self.n}_{name}", list(shape),
                                         dtype, offset=self.base + off)
        self.live[name] = (off, off + per, bufs)
        return h, bufs

    def free(self, *names):
        for nm in names:
            off, end, bufs = self.live.pop(nm)
            ops = []
            for b in bufs:
                ops.extend(b.readers)
                if b.last_writer is not None:
                    ops.append(b.last_writer)
            self.freed.append((off, end, ops))


def _host_constants(core):
    b = core // 4
    rb = core % 4
    cst = {}
    cst["c_identf"] = np.eye(128, dtype=np.float32)
    cst["c_identb"] = np.eye(128, dtype=ml_dtypes.bfloat16)
    cst["c_sut"] = np.triu(np.ones((128, 128), np.float32), 1)
    cst["c_tril"] = np.where(np.tril(np.ones((128, 128), bool)), 0.0,
                             -1.0e30).astype(np.float32)
    P = np.zeros((128, 128), np.float32)
    for i in range(64):
        P[i, i + 64] = -1.0
        P[i + 64, i] = 1.0
    cst["c_permT"] = np.ascontiguousarray(P.T)
    cst["c_iota"] = np.arange(1, 129, dtype=np.float32).reshape(128, 1)
    cst["c_onecol"] = np.ones((128, 1), np.float32)
    inv = 1.0 / (10000.0 ** (np.arange(0, DH, 2, dtype=np.float32) / DH))
    invd = np.concatenate([inv, inv]).reshape(128, 1)
    cst["c_invf"] = (invd * INV2PI).astype(np.float32)
    cst["c_ones"] = np.ones((1, 512), np.float32)
    sbv = np.zeros((128, 2), np.float32)
    sbv[:, 0] = 1.0 - b
    sbv[:, 1] = b
    cst["c_selb"] = sbv
    f0 = b * 32 + rb * 8
    bm = np.zeros((1, 64), np.float32)
    bm[0, b * 32:f0] = 1.0
    cst["c_beforem"] = bm
    tl = np.arange(TPC)
    pos = rb * TPC + tl
    pe = np.zeros((128, NTT, 2), np.float32)
    pe[:, :, 0] = (pos // 64).reshape(NTT, 128).T
    pe[:, :, 1] = (pos % 64).reshape(NTT, 128).T
    cst["c_posex"] = pe.astype(ml_dtypes.bfloat16)
    cst["c_b512"] = np.full((128, 1), 512.0 * b, np.float32)
    return cst


_CONST_SPECS = None


def _const_specs():
    global _CONST_SPECS
    if _CONST_SPECS is None:
        c = _host_constants(0)
        _CONST_SPECS = {k: (list(v.shape),
                            BF16 if v.dtype == ml_dtypes.bfloat16 else F32)
                        for k, v in c.items()}
    return _CONST_SPECS


def build_program():
    nc = bass.Bass("TRN2", target_bir_lowering=False)
    p = Prog(nc)

    din = {}

    def dt_in(name, shape, dtype=F32):
        din[name] = nc.dram_tensor(name, list(shape), dtype,
                                   kind="ExternalInput").ap()
        return din[name]

    x_in = dt_in("x", [TPC, E])
    Wqk = dt_in("Wqk", [E, E]); dt_in("bqk", [1, E])
    Wv = dt_in("Wv", [E, E]); dt_in("bv", [1, E])
    Wu = dt_in("Wu", [E, E]); dt_in("bu", [1, E])
    Wo = dt_in("Wo", [E, E]); dt_in("bo", [1, E])
    ln_g = dt_in("ln_g", [1, E]); ln_b = dt_in("ln_b", [1, E])
    w_router = dt_in("w_router", [1, E])
    wq = dt_in("wq", [E, 256]); dt_in("fbq", [1, 256])
    wk = dt_in("wk", [E, 128]); dt_in("fbk", [1, 128])
    wv2 = dt_in("wv2", [E, 128]); dt_in("fbv", [1, 128])
    wo2 = dt_in("wo2", [256, E])
    wgate = dt_in("wgate", [E, 1024])
    wup = dt_in("wup", [E, 1024])
    wdown = dt_in("wdown", [1024, E])
    ln1w = dt_in("ln1w", [1, E]); ln2w = dt_in("ln2w", [1, E])
    for cname, (cshape, cdt) in _const_specs().items():
        dt_in(cname, cshape, cdt)

    y_out = nc.dram_tensor("y", [TPC, E], F32, kind="ExternalOutput").ap()

    def dscr(name, shape, dtype):
        return nc.dram_tensor(name, list(shape), dtype, kind="Internal").ap()

    tr_dram = dscr("tr_dram", [TPC, E], BF16)
    tr_dram2 = dscr("tr_dram2", [TPC, E], BF16)
    kh_dram = dscr("kh_dram", [TPC, E], BF16)
    u_dram = dscr("u_dram", [TPC, E], F32)
    y2_dram = dscr("y2_dram", [TPC, E], F32)
    kv_in = dscr("kv_in", [128, 2 * E], F32)
    kv_out = dscr("kv_out", [128, 2 * E], F32)
    sc_in = dscr("sc_in", [NTT, 128], F32)
    sc_out = dscr("sc_out", [NC * NTT, 128], F32)
    xres_dram = dscr("xres_dram", [TPC, E], F32)
    hln_dram = dscr("hln_dram", [TPC, E], BF16)
    sel_in = dscr("sel_in", [S, SELW], BF16)
    sel_out = dscr("sel_out", [S, SELW], BF16)
    pos_dram = dscr("pos_dram", [NTT, 128], F32)
    ar2_in = dscr("ar2_in", [S, E], BF16)
    ar2_out = dscr("ar2_out", [S, E], BF16)
    ar3_in = dscr("ar3_in", [S, E], BF16)
    ar3_out = dscr("ar3_out", [S, E], BF16)
    delta_dram = dscr("delta_dram", [S, E], BF16)

    dbg_outs = {}

    def dbg_dram(name, dram_ap, shape, dtype, stream_name=None, reads=()):
        if name not in DEBUG:
            return
        d = nc.dram_tensor("dbg_" + name, list(shape), dtype,
                           kind="ExternalOutput").ap()
        dbg_outs[name] = d
        deps = [p.stream_dep(stream_name)] if stream_name else []
        p.dma("scalar", (lambda dd=d, s=dram_ap: lambda e: e.dma_start(
            out=dd, in_=s))(), reads=list(reads), stream="dbg", deps=deps)

    def dbg_store(name, shape, src_ap, reads, dtype=F32):
        if name not in DEBUG:
            return
        d = nc.dram_tensor("dbg_" + name, list(shape), dtype,
                           kind="ExternalOutput").ap()
        dbg_outs[name] = d
        p.dma("sync", (lambda dd=d, s=src_ap: lambda e: e.dma_start(
            out=dd, in_=s))(), reads=list(reads), stream="dbg")

    sbt = {}

    def salloc(name, shape, dtype):
        sbt[name] = nc.alloc_sbuf_tensor("s_" + name, list(shape), dtype)
        return sbt[name]

    consts = {}
    CB = {}
    for cname, (cshape, cdt) in _const_specs().items():
        hh = salloc(cname, cshape, cdt)
        consts[cname] = hh
        CB[cname] = Buf(cname)
        q = "gpsimd" if cdt == BF16 else "sync"
        p.dma(q, (lambda h=hh, a=din[cname]:
                  lambda e: e.dma_start(out=h.ap(), in_=a))(),
              writes=[CB[cname]], stream="cload")
    def bcast_load(nm, srcrow, dt_):
        hh, (bb,) = A.alloc(nm, [128, E], dt_)
        q = "gpsimd" if dt_ == BF16 else "sync"
        p.dma(q, (lambda h=hh, a=srcrow:
                  lambda e: e.dma_start(out=h.ap(),
                                        in_=a.to_broadcast([128, E])))(),
              writes=[bb])
        return hh, bb
    for nm, src, w in (("r_fbq", din["fbq"], 256), ("r_fbk", din["fbk"], 128),
                       ("r_fbv", din["fbv"], 128)):
        hh = salloc(nm, [1, w], F32)
        consts[nm] = hh
        CB[nm] = Buf(nm)
        p.dma("sync", (lambda h=hh, a=src:
                       lambda e: e.dma_start(out=h.ap(), in_=a))(),
              writes=[CB[nm]], stream="cload")
    bias_slot = salloc("bias_slot", [1, 4 * 512], F32)
    bias_slot_b = [Buf(f"bias{i}") for i in range(4)]

    salloc("stats", [128, 2, 8], F32)
    salloc("wsum", [128, 1], F32)
    salloc("scores", [128, NTT], F32)
    _stap = sbt["stats"].ap()

    def ST(sl):
        a = _stap[:, sl, :]
        return {"mu": a[:, 0:1], "rstd": a[:, 1:2], "t1": a[:, 2:3],
                "t2": a[:, 3:4], "t3": a[:, 4:5]}
    stat_bs = [Buf("stat0"), Buf("stat1")]
    st_mu = ST(0)["mu"]; st_rstd = ST(0)["rstd"]
    st_t1 = ST(0)["t1"]; st_t2 = ST(0)["t2"]; st_t3 = ST(0)["t3"]
    stat_b = stat_bs[0]
    scsb = sbt["scores"].ap()
    sc_b = Buf("scores")

    ones = consts["c_ones"].ap()
    onecol = consts["c_onecol"].ap()
    selb = consts["c_selb"].ap()
    identb = consts["c_identb"].ap()
    identf = consts["c_identf"].ap()


    ps = {}
    ps["A"] = nc.alloc_psum_tensor("psA", [128, 1024], F32)
    for nm in ("B", "C", "D", "E2"):
        ps[nm] = nc.alloc_psum_tensor("ps" + nm, [128, 512], F32)
    ps["T"] = nc.alloc_psum_tensor("psT", [128, 512], BF16)
    ps["S"] = nc.alloc_psum_tensor("psS", [128, 512], F32)
    pbuf = {k: Buf("ps" + k) for k in ps}
    PSN4 = ("B", "C", "D", "E2")

    avail = nc.sbuf_bytes_remaining
    arena_size = (avail - 12 * 1024) // 512 * 512
    backing = nc.alloc_sbuf_tensor("arena_backing", [128, arena_size],
                                   mybir.dt.uint8)
    arena_base = nc.lookup_mloc(backing).addr
    A = Arena(nc, arena_base, arena_size)

    def t_mm(psname, cols, lhsT, rhs, start, stop, reads, rows=(0, 128)):
        o = ps[psname].ap()[rows[0]:rows[1], cols[0]:cols[1]]
        p.op("tensor",
             (lambda o=o, l=lhsT, r=rhs, st=start, sp=stop:
              lambda e: e.matmul(o, l, r, start=st, stop=sp))(),
             reads=list(reads), writes=[pbuf[psname]])

    def t_transpose(psname, rows, cols, in_ap, ident_ap, reads, iread):
        o = ps[psname].ap()[rows[0]:rows[1], cols[0]:cols[1]]
        p.op("tensor",
             (lambda o=o, i=in_ap, idn=ident_ap:
              lambda e: e.transpose(o, i, idn))(),
             reads=list(reads) + [iread], writes=[pbuf[psname]])


    def emit_transposes(dst_ap_fn, dst_bufs, src_dram, nchunks, dep):
        st = p.stream("trp")
        for kc in range(nchunks):
            deps = [dep, ("rawsem", st.sem, 16 * st.count)]
            p.dma("sync", (lambda k=kc: lambda e: e.dma_start(
                out=dst_ap_fn(k), in_=src_dram[:, k * 128:(k + 1) * 128],
                transpose=True))(),
                writes=[dst_bufs[kc]], stream="trp", deps=deps)

    zt, (zt_b,) = A.alloc("ztile", [128, SELW], BF16)
    p.op("vector", lambda e: e.memset(zt.ap(), 0.0), writes=[zt_b])
    for j in range(NTT):
        p.dma("scalar", (lambda jj=j: lambda e: e.dma_start(
            out=sel_in[jj * 128:(jj + 1) * 128, :], in_=zt.ap()))(),
            reads=[zt_b], stream="selz")
    A.free("ztile")

    xin, xin_b = A.alloc("xin", [128, 2, E], F32, nbufs=2)
    xln, xln_b = A.alloc("xln", [128, 2, E], BF16, nbufs=2)
    sq, sq_bs = A.alloc("sq", [128, 2, E], F32, nbufs=2)

    def ln_stats(src_ap, src_bufs, eps, sl=0):
        st = ST(sl); sb = stat_bs[sl]; sqv = sq.ap()[:, sl, :]
        sq_b = sq_bs[sl]
        p.op("vector", (lambda s=src_ap, o=st["t1"]: lambda e: e.reduce_sum(
            o, s, axis=AX.X))(), reads=list(src_bufs), writes=[sb])
        p.op("vector", (lambda st=st: lambda e: e.tensor_scalar_mul(
            st["mu"], st["t1"], 1.0 / E))(), reads=[sb], writes=[sb])
        p.op("scalar", (lambda s=src_ap, o=sqv: lambda e: e.activation(
            o, s, AF.Square))(), reads=list(src_bufs), writes=[sq_b])
        p.op("vector", (lambda st=st, i=sqv: lambda e: e.reduce_sum(
            st["t2"], i, axis=AX.X))(), reads=[sq_b], writes=[sb])
        p.op("vector", (lambda st=st, ep=eps: lambda e: e.tensor_scalar(
            st["t2"], st["t2"], 1.0 / E, ep, ALU.mult, ALU.add))(),
             reads=[sb], writes=[sb])
        p.op("vector", (lambda st=st: lambda e: e.tensor_mul(
            st["t1"], st["mu"], st["mu"]))(), reads=[sb], writes=[sb])
        p.op("vector", (lambda st=st: lambda e: e.tensor_sub(
            st["t2"], st["t2"], st["t1"]))(), reads=[sb], writes=[sb])
        p.op("scalar", (lambda st=st: lambda e: e.activation(
            st["t1"], st["t2"], AF.Sqrt))(), reads=[sb], writes=[sb])
        p.op("vector", (lambda st=st: lambda e: e.reciprocal(
            st["rstd"], st["t1"]))(), reads=[sb], writes=[sb])
        p.op("vector", (lambda st=st: lambda e: e.tensor_mul(
            st["t2"], st["mu"], st["rstd"]))(), reads=[sb], writes=[sb])
        return st, sb

    for j in range(NTT):
        sl = j % 2
        p.dma("sync", (lambda s=sl, jj=j: lambda e: e.dma_start(
            out=xin.ap()[:, s, :], in_=x_in[jj * 128:(jj + 1) * 128, :]))(),
            writes=[xin_b[sl]])
        st, sb = ln_stats(xin.ap()[:, sl, :], [xin_b[sl]], LN_EPS, sl)
        p.op("vector", (lambda s=sl, st=st: lambda e: e.tensor_scalar(
            xln.ap()[:, s, :], xin.ap()[:, s, :], st["rstd"], st["t2"],
            ALU.mult, ALU.subtract))(),
             reads=[xin_b[sl], sb], writes=[xln_b[sl]])
        p.dma("sync", (lambda s=sl, jj=j: lambda e: e.dma_start(
            out=tr_dram[jj * 128:(jj + 1) * 128, :], in_=xln.ap()[:, s, :]))(),
            reads=[xln_b[sl]], stream="xlnw")

    dbg_dram("xln", tr_dram, [TPC, E], BF16, "xlnw")
    xT, xT_b = A.alloc("xT", [128, 16, TPC], BF16, nbufs=16)
    emit_transposes(lambda k: xT.ap()[:, k, :], xT_b, tr_dram, 16,
                    p.stream_dep("xlnw"))
    A.free("xin", "xln")

    wqk_buf, wqk_b = A.alloc("wqk", [128, 2, 16, 512], BF16, nbufs=2)

    def load_wquarter(slot, W, oc):
        src = W[:, oc * 512:(oc + 1) * 512].rearrange("(kc p) o -> p kc o",
                                                      p=128)
        p.dma("gpsimd", (lambda s=slot, ss=src: lambda e: e.dma_start(
            out=wqk_buf.ap()[:, s, :, :], in_=ss))(), writes=[wqk_b[slot]])

    def load_biasq(slot, brow, oc):
        p.dma("sync", (lambda s=slot, b=brow, occ=oc: lambda e: e.dma_start(
            out=bias_slot.ap()[0:1, s * 512:(s + 1) * 512],
            in_=b[0:1, occ * 512:(occ + 1) * 512]))(),
            writes=[bias_slot_b[slot]])

    def proj_pass(W, brow, evac):
        for oc in range(4):
            slot = oc % 2
            load_wquarter(slot, W, oc)
            load_biasq(oc % 4, brow, oc)
            for j in range(NTT):
                psn = "B" if j % 2 == 0 else "C"
                for kc in range(16):
                    t_mm(psn, (0, 512),
                         xT.ap()[:, kc, j * 128:(j + 1) * 128],
                         wqk_buf.ap()[:, slot, kc, :],
                         kc == 0, False, [xT_b[kc], wqk_b[slot]])
                p.op("tensor", (lambda o=ps[psn].ap()[:, 0:512], s=oc % 4:
                                lambda e: e.matmul(
                                    o, ones[0:1, 0:128],
                                    bias_slot.ap()[0:1, s * 512:(s + 1) * 512],
                                    start=False, stop=True))(),
                     reads=[bias_slot_b[oc % 4], CB["c_ones"]],
                     writes=[pbuf[psn]])
                evac(psn, j, oc)

    kh, kh_b = A.alloc("kh", [128, NTT, E], BF16, nbufs=NTT)
    escr, (escr_b,) = A.alloc("escr", [128, 2, 512], F32, nbufs=1)
    escr2, (escr2_b,) = A.alloc("escr2", [128, 2, 512], F32, nbufs=1)

    def evac_qk(psn, j, oc):
        sl = j % 2
        p.op("vector", (lambda s=ps[psn].ap()[:, 0:512], q=sl:
                        lambda e: e.tensor_scalar(
                            escr.ap()[:, q, :], s, 0.0, None, ALU.min))(),
             reads=[pbuf[psn]], writes=[escr_b])
        p.op("scalar", (lambda q=sl: lambda e: e.activation(
            escr2.ap()[:, q, :], escr.ap()[:, q, :], AF.Exp))(),
             reads=[escr_b], writes=[escr2_b])
        p.op("vector", (lambda o=kh.ap()[:, j, oc * 512:(oc + 1) * 512],
                        s=ps[psn].ap()[:, 0:512], q=sl:
                        lambda e: e.scalar_tensor_tensor(
                            o, escr2.ap()[:, q, :], -1.0, s,
                            ALU.add, ALU.max))(),
             reads=[escr2_b, pbuf[psn]], writes=[kh_b[j]])

    proj_pass(Wqk, din["bqk"], evac_qk)
    A.free("escr", "escr2")
    for j in range(NTT):
        p.dma("sync", (lambda jj=j: lambda e: e.dma_start(
            out=kh_dram[jj * 128:(jj + 1) * 128, :], in_=kh.ap()[:, jj, :]))(),
            reads=[kh_b[j]], stream="khw")

    dbg_dram("kh", kh_dram, [TPC, E], BF16, "khw")
    vt, vt_b = A.alloc("vt", [128, NTT, E], BF16, nbufs=NTT)

    def evac_v(psn, j, oc):
        p.op("scalar", (lambda o=vt.ap()[:, j, oc * 512:(oc + 1) * 512],
                        s=ps[psn].ap()[:, 0:512]:
                        lambda e: e.activation(o, s, AF.Silu))(),
             reads=[pbuf[psn]], writes=[vt_b[j]])

    proj_pass(Wv, din["bv"], evac_v)

    kvsb, (kvsb_b,) = A.alloc("kvsb", [128, E], F32)
    kmsk, (kmsk_b,) = A.alloc("kmsk", [128, E], F32)
    for hg in range(4):
        psn = PSN4[hg]
        for hh in range(4):
            h = hg * 4 + hh
            for j in range(NTT):
                t_mm(psn, (hh * 128, hh * 128 + 128),
                     kh.ap()[:, j, h * 128:(h + 1) * 128],
                     vt.ap()[:, j, h * 128:(h + 1) * 128],
                     j == 0, j == NTT - 1, [kh_b[j], vt_b[j]])
        p.op("vector", (lambda o=kvsb.ap()[:, hg * 512:(hg + 1) * 512],
                        s=ps[psn].ap(): lambda e: e.tensor_copy(o, s))(),
             reads=[pbuf[psn]], writes=[kvsb_b])
    p.op("vector", lambda e: e.tensor_scalar(kmsk.ap(), kvsb.ap(),
                                             selb[:, 0:1], None, ALU.mult),
         reads=[kvsb_b, CB["c_selb"]], writes=[kmsk_b])
    p.dma("sync", lambda e: e.dma_start(out=kv_in[:, 0:E], in_=kmsk.ap()),
          reads=[kmsk_b], stream="kvw")
    p.op("vector", lambda e: e.tensor_scalar(kmsk.ap(), kvsb.ap(),
                                             selb[:, 1:2], None, ALU.mult),
         reads=[kvsb_b, CB["c_selb"]], writes=[kmsk_b])
    p.dma("sync", lambda e: e.dma_start(out=kv_in[:, E:2 * E], in_=kmsk.ap()),
          reads=[kmsk_b], stream="kvw")
    kv_out_b = Buf("kv_out")
    p.coll(lambda e: e.collective_compute(
        "AllReduce", ALU.add, replica_groups=[list(range(NC))],
        ins=[kv_in.opt()], outs=[kv_out.opt()]),
        writes=[kv_out_b], deps=[p.stream_dep("kvw")])
    A.free("kvsb", "kmsk")

    A.free("vt")
    khT, khT_b = A.alloc("khT", [128, 16, TPC], BF16, nbufs=16)
    emit_transposes(lambda k: khT.ap()[:, k, :], khT_b, kh_dram, 16,
                    p.stream_dep("khw"))

    A.free("kh")
    usl, usl_b = A.alloc("usl", [128, 2, E], F32, nbufs=2)

    def evac_u(psn, j, oc):
        sl = j % 2
        p.op("scalar", (lambda o=usl.ap()[:, sl, 0:512],
                        s=ps[psn].ap()[:, 0:512]:
                        lambda e: e.activation(o, s, AF.Silu))(),
             reads=[pbuf[psn]], writes=[usl_b[sl]])
        p.dma("sync", (lambda s=sl, jj=j, occ=oc: lambda e: e.dma_start(
            out=u_dram[jj * 128:(jj + 1) * 128, occ * 512:(occ + 1) * 512],
            in_=usl.ap()[:, s, 0:512]))(),
            reads=[usl_b[sl]], stream="uw")

    proj_pass(Wu, din["bu"], evac_u)

    dbg_dram("u", u_dram, [TPC, E], F32, "uw")
    kvf, (kvf_b,) = A.alloc("kvf", [128, E], F32)
    kvlo, (kvlo_b,) = A.alloc("kvlo", [128, E], F32)
    kvbf, (kvbf_b,) = A.alloc("kvbf", [128, E], BF16)
    p.dma("sync", lambda e: e.dma_start(out=kvlo.ap(), in_=kv_out[:, 0:E]),
          reads=[kv_out_b], writes=[kvlo_b])
    p.dma("sync", lambda e: e.dma_start(out=kvf.ap(), in_=kv_out[:, E:2 * E]),
          reads=[kv_out_b], writes=[kvf_b])
    p.op("vector", lambda e: e.tensor_scalar(kvlo.ap(), kvlo.ap(),
                                             selb[:, 0:1], None, ALU.mult),
         reads=[kvlo_b, CB["c_selb"]], writes=[kvlo_b])
    p.op("vector", lambda e: e.tensor_scalar(kvf.ap(), kvf.ap(),
                                             selb[:, 1:2], None, ALU.mult),
         reads=[kvf_b, CB["c_selb"]], writes=[kvf_b])
    p.op("vector", lambda e: e.tensor_add(kvf.ap(), kvf.ap(), kvlo.ap()),
         reads=[kvf_b, kvlo_b], writes=[kvf_b])
    p.op("vector", lambda e: e.tensor_copy(kvbf.ap(), kvf.ap()),
         reads=[kvf_b], writes=[kvbf_b])
    dbg_store("kv", [128, E], kvf.ap(), [kvf_b])
    A.free("kvlo", "kvf")

    lnsc, (lnsc_b,) = A.alloc("lnsc", [128, E], F32)
    zsl, zsl_b = A.alloc("zsl", [128, 2, E], BF16, nbufs=2)
    g_bh, g_bb = bcast_load("g_b", ln_g, F32)
    b_bh, b_bb = bcast_load("b_b", ln_b, F32)
    g_b = g_bh.ap(); b_b = b_bh.ap()
    uw_dep = p.stream_dep("uw")
    for j in range(NTT):
        sl = j % 2
        for hg in range(4):
            psn = PSN4[hg]
            for hh in range(4):
                h = hg * 4 + hh
                t_mm(psn, (hh * 128, hh * 128 + 128),
                     khT.ap()[:, h, j * 128:(j + 1) * 128],
                     kvbf.ap()[:, h * 128:(h + 1) * 128],
                     True, True, [khT_b[h], kvbf_b])
            p.op("scalar", (lambda o=lnsc.ap()[:, hg * 512:(hg + 1) * 512],
                            s=ps[psn].ap(): lambda e: e.copy(o, s))(),
                 reads=[pbuf[psn]], writes=[lnsc_b])
        st, sb = ln_stats(lnsc.ap(), [lnsc_b], LN_EPS, sl)
        p.op("vector", (lambda st=st: lambda e: e.tensor_scalar(
            lnsc.ap(), lnsc.ap(), st["rstd"], st["t2"], ALU.mult,
            ALU.subtract))(),
             reads=[lnsc_b, sb], writes=[lnsc_b])
        p.op("gpsimd", lambda e: e.tensor_mul(lnsc.ap(), lnsc.ap(), g_b),
             reads=[g_bb], writes=[lnsc_b])
        p.op("gpsimd", lambda e: e.tensor_add(lnsc.ap(), lnsc.ap(), b_b),
             reads=[b_bb], writes=[lnsc_b])
        p.dma("sync", (lambda s=sl, jj=j: lambda e: e.dma_start(
            out=usl.ap()[:, s, :], in_=u_dram[jj * 128:(jj + 1) * 128, :]))(),
            writes=[usl_b[sl]], deps=[uw_dep])
        p.op("vector", (lambda s=sl: lambda e: e.tensor_mul(
            zsl.ap()[:, s, :], lnsc.ap(), usl.ap()[:, s, :]))(),
             reads=[lnsc_b, usl_b[sl]], writes=[zsl_b[sl]])
        p.dma("sync", (lambda s=sl, jj=j: lambda e: e.dma_start(
            out=tr_dram2[jj * 128:(jj + 1) * 128, :], in_=zsl.ap()[:, s, :]))(),
            reads=[zsl_b[sl]], stream="zw")

    dbg_dram("z", tr_dram2, [TPC, E], BF16, "zw")
    A.free("xT", "khT", "usl", "zsl", "lnsc", "kvbf", "g_b", "b_b")
    zT, zT_b = A.alloc("zT", [128, 16, TPC], BF16, nbufs=16)
    emit_transposes(lambda k: zT.ap()[:, k, :], zT_b, tr_dram2, 16,
                    p.stream_dep("zw"))

    y2sl, y2sl_b = A.alloc("y2sl", [128, 2, 512], F32, nbufs=2)

    def evac_y2(psn, j, oc):
        sl = j % 2
        p.op("vector", (lambda o=y2sl.ap()[:, sl, :],
                        s=ps[psn].ap()[:, 0:512]:
                        lambda e: e.tensor_copy(o, s))(),
             reads=[pbuf[psn]], writes=[y2sl_b[sl]])
        p.dma("sync", (lambda s=sl, jj=j, occ=oc: lambda e: e.dma_start(
            out=y2_dram[jj * 128:(jj + 1) * 128, occ * 512:(occ + 1) * 512],
            in_=y2sl.ap()[:, s, :]))(),
            reads=[y2sl_b[sl]], stream="y2w")

    for oc in range(4):
        slot = oc % 2
        load_wquarter(slot, Wo, oc)
        load_biasq(oc % 4, din["bo"], oc)
        for j in range(NTT):
            psn = "B" if j % 2 == 0 else "C"
            for kc in range(16):
                t_mm(psn, (0, 512),
                     zT.ap()[:, kc, j * 128:(j + 1) * 128],
                     wqk_buf.ap()[:, slot, kc, :],
                     kc == 0, False, [zT_b[kc], wqk_b[slot]])
            p.op("tensor", (lambda o=ps[psn].ap()[:, 0:512], s=oc % 4:
                            lambda e: e.matmul(
                                o, ones[0:1, 0:128],
                                bias_slot.ap()[0:1, s * 512:(s + 1) * 512],
                                start=False, stop=True))(),
                 reads=[bias_slot_b[oc % 4], CB["c_ones"]], writes=[pbuf[psn]])
            evac_y2(psn, j, oc)

    dbg_dram("y2", y2_dram, [TPC, E], F32, "y2w")
    A.free("zT", "wqk", "y2sl")
    hln, hln_b = A.alloc("hln", [128, NTT, SELW], BF16, nbufs=NTT)
    xin2, xin2_b = A.alloc("xin2", [128, 2, E], F32, nbufs=2)
    lnsc2, lnsc2_bs = A.alloc("lnsc2", [128, 2, E], F32, nbufs=2)
    xr, xr_b = A.alloc("xr", [128, 2, E], F32, nbufs=2)
    posex_b = hln_b[0]
    p.dma("gpsimd", lambda e: e.dma_start(
        out=hln.ap()[:, :, 2048:2050], in_=din["c_posex"]),
        writes=list(hln_b))
    p.op("vector", lambda e: e.memset(hln.ap()[:, :, 2051:SELW], 0.0),
         writes=list(hln_b))
    hlnpad_op = p.ops[-1]
    wr_bh, wr_bb = bcast_load("wr_b", w_router, F32)
    wr_b = wr_bh.ap()
    p.op("vector", lambda e: e.reduce_sum(sbt["wsum"].ap(), wr_b, axis=AX.X),
         reads=[wr_bb], writes=[stat_b])
    y2_dep = p.stream_dep("y2w")
    for j in range(NTT):
        sl = j % 2
        p.dma("sync", (lambda s=sl, jj=j: lambda e: e.dma_start(
            out=xin2.ap()[:, s, :], in_=x_in[jj * 128:(jj + 1) * 128, :]))(),
            writes=[xin2_b[sl]])
        p.dma("scalar", (lambda s=sl, jj=j: lambda e: e.dma_start(
            out=xr.ap()[:, s, :], in_=y2_dram[jj * 128:(jj + 1) * 128, :]))(),
            writes=[xr_b[sl]], deps=[y2_dep])
        p.op("vector", (lambda s=sl: lambda e: e.tensor_add(
            xr.ap()[:, s, :], xr.ap()[:, s, :], xin2.ap()[:, s, :]))(),
             reads=[xr_b[sl], xin2_b[sl]], writes=[xr_b[sl]])
        st, sb = ln_stats(xr.ap()[:, sl, :], [xr_b[sl]], LN_EPS, sl)
        p.op("vector", (lambda s=sl, jj=j, st=st: lambda e: e.tensor_scalar(
            hln.ap()[:, jj, 0:E], xr.ap()[:, s, :], st["rstd"], st["t2"],
            ALU.mult, ALU.subtract))(),
             reads=[xr_b[sl], sb], writes=[hln_b[j]])
        p.op("gpsimd", (lambda s=sl, o=lnsc2.ap()[:, sl, :]:
                        lambda e: e.tensor_mul(o, xr.ap()[:, s, :], wr_b))(),
             reads=[xr_b[sl], wr_bb], writes=[lnsc2_bs[sl]])
        p.op("vector", (lambda st=st, i=lnsc2.ap()[:, sl, :]:
                        lambda e: e.reduce_sum(st["t1"], i, axis=AX.X))(),
             reads=[lnsc2_bs[sl]], writes=[sb])
        p.op("vector", (lambda st=st: lambda e: e.tensor_mul(
            st["t3"], st["mu"], sbt["wsum"].ap()))(), reads=[sb], writes=[sb])
        p.op("vector", (lambda st=st: lambda e: e.tensor_sub(
            st["t1"], st["t1"], st["t3"]))(), reads=[sb], writes=[sb])
        p.op("vector", (lambda jj=j, st=st: lambda e: e.tensor_mul(
            scsb[:, jj:jj + 1], st["t1"], st["rstd"]))(),
             reads=[sb], writes=[sc_b])
        p.op("vector", (lambda jj=j: lambda e: e.tensor_copy(
            hln.ap()[:, jj, 2050:2051], scsb[:, jj:jj + 1]))(),
             reads=[sc_b], writes=[hln_b[j]])
        p.dma("sync", (lambda s=sl, jj=j: lambda e: e.dma_start(
            out=xres_dram[jj * 128:(jj + 1) * 128, :], in_=xr.ap()[:, s, :]))(),
            reads=[xr_b[sl]], stream="xrw")
        p.dma("scalar", (lambda jj=j: lambda e: e.dma_start(
            out=hln_dram[jj * 128:(jj + 1) * 128, :],
            in_=hln.ap()[:, jj, 0:E]))(),
            reads=[hln_b[j]], stream="hlw")

    dbg_dram("xres", xres_dram, [TPC, E], F32, "xrw")
    dbg_dram("hlnd", hln_dram, [TPC, E], BF16, "hlw")
    scT, (scT_b,) = A.alloc("scT", [NTT, 128], F32)
    t_transpose("S", (0, NTT), (0, 128), scsb, identf, [sc_b], CB["c_identf"])
    p.op("vector", lambda e: e.tensor_copy(scT.ap(),
                                           ps["S"].ap()[0:NTT, 0:128]),
         reads=[pbuf["S"]], writes=[scT_b])
    p.dma("sync", lambda e: e.dma_start(out=sc_in, in_=scT.ap()),
          reads=[scT_b], stream="scw")
    scout_b = Buf("sc_out")
    p.coll(lambda e: e.collective_compute(
        "AllGather", ALU.bypass, replica_groups=[list(range(NC))],
        ins=[sc_in.opt()], outs=[sc_out.opt()]),
        writes=[scout_b], deps=[p.stream_dep("scw")])

    dbg_dram("scout", sc_out, [NC * NTT, 128], F32, None, reads=[scout_b])
    scb, (scb_b,) = A.alloc("scb", [128, T], F32)
    p.dma("sync", lambda e: e.dma_start(
        out=scb.ap(), in_=sc_out.rearrange("(o a) b -> o (a b)", o=1).to_broadcast(
            [128, T])), reads=[scout_b], writes=[scb_b])
    stab64, (stab64_b,) = A.alloc("stab64", [64, 128], F32)
    p.dma("scalar", lambda e: e.dma_start(out=stab64.ap(), in_=sc_out),
          reads=[scout_b], writes=[stab64_b])
    tab, (tab_b,) = A.alloc("tab", [128, 64], F32)
    t_transpose("S", (0, 128), (0, 64), stab64.ap(), identf[0:64, 0:64], [stab64_b], CB["c_identf"])
    p.op("vector", lambda e: e.tensor_copy(tab.ap(), ps["S"].ap()[:, 0:64]),
         reads=[pbuf["S"]], writes=[tab_b])

    gs, (gs_b,) = A.alloc("gs", [1, 16], F32)
    gsl = gs.ap()
    junk, (junk_b,) = A.alloc("junk", [128, 4096], BF16)
    cnt2, (cnt2_b,) = A.alloc("cnt2", [128, 2], F32)
    tcol, (tcol_b,) = A.alloc("tcol", [128, 4], F32)
    sq64, (sq64_b,) = A.alloc("sq64", [128, 64], F32)

    p.op("scalar", lambda e: e.activation(sq64.ap(), tab.ap(), AF.Square),
         reads=[tab_b], writes=[sq64_b])
    t_mm("S", (0, 64), onecol, sq64.ap(), True, True,
         [sq64_b, CB["c_onecol"]], rows=(0, 1))
    p.op("vector", lambda e: e.tensor_copy(sq64.ap()[0:1, 0:64],
                                           ps["S"].ap()[0:1, 0:64]),
         reads=[pbuf["S"]], writes=[sq64_b])
    p.op("vector", lambda e: e.reduce_sum(
        gsl[0:1, 8:10], sq64.ap()[0:1, 0:64].rearrange("a (b c) -> a b c", b=2),
        axis=AX.X), reads=[sq64_b], writes=[gs_b])
    p.op("scalar", lambda e: e.activation(gsl[0:1, 2:4], gsl[0:1, 8:10],
                                          AF.Sqrt), reads=[gs_b], writes=[gs_b])
    p.op("vector", lambda e: e.tensor_scalar_add(gsl[0:1, 2:4],
                                                 gsl[0:1, 2:4], 1.0),
         reads=[gs_b], writes=[gs_b])
    p.op("vector", lambda e: e.tensor_scalar_mul(gsl[0:1, 0:2],
                                                 gsl[0:1, 2:4], -1.0),
         reads=[gs_b], writes=[gs_b])
    p.op("vector", lambda e: e.tensor_scalar_mul(gsl[0:1, 2:4],
                                                 gsl[0:1, 2:4], 2.0 / 129.0),
         reads=[gs_b], writes=[gs_b])

    iota = consts["c_iota"].ap()
    for rnd in range(4):
        t_mm("S", (0, 4), ones[0:1, 0:128], gsl[0:1, 0:4], True, True,
             [gs_b, CB["c_ones"]])
        p.op("vector", lambda e: e.tensor_copy(tcol.ap(),
                                               ps["S"].ap()[:, 0:4]),
             reads=[pbuf["S"]], writes=[tcol_b])
        p.op("vector", lambda e: e.scalar_tensor_tensor(
            tcol.ap()[:, 0:1], tcol.ap()[:, 2:3], iota[:, 0:1],
            tcol.ap()[:, 0:1], ALU.mult, ALU.add),
             reads=[tcol_b, CB["c_iota"]], writes=[tcol_b])
        p.op("vector", lambda e: e.scalar_tensor_tensor(
            tcol.ap()[:, 1:2], tcol.ap()[:, 3:4], iota[:, 0:1],
            tcol.ap()[:, 1:2], ALU.mult, ALU.add),
             reads=[tcol_b, CB["c_iota"]], writes=[tcol_b])
        for bb in range(2):
            p.op("vector", (lambda b2=bb: lambda e: e.tensor_scalar(
                junk.ap(), scb.ap()[:, b2 * 4096:(b2 + 1) * 4096],
                tcol.ap()[:, b2:b2 + 1], None, ALU.is_gt))(),
                 reads=[scb_b, tcol_b], writes=[junk_b])
            p.op("vector", (lambda b2=bb: lambda e: e.reduce_sum(
                cnt2.ap()[:, b2:b2 + 1], junk.ap(), axis=AX.X))(),
                 reads=[junk_b], writes=[cnt2_b])
        p.op("vector", lambda e: e.tensor_scalar(
            cnt2.ap(), cnt2.ap(), float(KSEL), None, ALU.is_ge),
             reads=[cnt2_b], writes=[cnt2_b])
        t_mm("S", (0, 2), onecol, cnt2.ap(), True, True,
             [cnt2_b, CB["c_onecol"]], rows=(0, 1))
        p.op("vector", lambda e: e.tensor_copy(gsl[0:1, 4:6],
                                               ps["S"].ap()[0:1, 0:2]),
             reads=[pbuf["S"]], writes=[gs_b])
        p.op("vector", lambda e: e.tensor_mul(gsl[0:1, 6:8], gsl[0:1, 4:6],
                                              gsl[0:1, 2:4]),
             reads=[gs_b], writes=[gs_b])
        p.op("vector", lambda e: e.tensor_add(gsl[0:1, 0:2], gsl[0:1, 0:2],
                                              gsl[0:1, 6:8]),
             reads=[gs_b], writes=[gs_b])
        p.op("vector", lambda e: e.tensor_scalar_mul(gsl[0:1, 2:4],
                                                     gsl[0:1, 2:4],
                                                     1.0 / 129.0),
             reads=[gs_b], writes=[gs_b])

    t_mm("S", (0, 2), ones[0:1, 0:128], gsl[0:1, 0:2], True, True,
         [gs_b, CB["c_ones"]])
    p.op("vector", lambda e: e.tensor_copy(tcol.ap()[:, 0:2],
                                           ps["S"].ap()[:, 0:2]),
         reads=[pbuf["S"]], writes=[tcol_b])
    p.op("vector", lambda e: e.tensor_mul(tcol.ap()[:, 2:4],
                                          tcol.ap()[:, 0:2], selb),
         reads=[tcol_b, CB["c_selb"]], writes=[tcol_b])
    p.op("vector", lambda e: e.reduce_sum(
        st_t1, tcol.ap()[:, 2:4], axis=AX.X), reads=[tcol_b], writes=[stat_b])

    mloc, (mloc_b,) = A.alloc("mloc", [128, NTT], F32)
    pexc, (pexc_b,) = A.alloc("pexc", [128, NTT], F32)
    lrow, (lrow_b,) = A.alloc("lrow", [1, 4 * NTT], F32)
    sidx, (sidx_b,) = A.alloc("sidx", [128, NTT], F32)
    sidx_i, (sidxi_b,) = A.alloc("sidxi", [128, NTT], I32)
    p.op("vector", lambda e: e.tensor_scalar(mloc.ap(), scsb, st_t1, None,
                                             ALU.is_gt),
         reads=[sc_b, stat_b], writes=[mloc_b])
    t_mm("S", (0, NTT), consts["c_sut"].ap(), mloc.ap(), True, True,
         [mloc_b, CB["c_sut"]])
    p.op("vector", lambda e: e.tensor_copy(pexc.ap(),
                                           ps["S"].ap()[:, 0:NTT]),
         reads=[pbuf["S"]], writes=[pexc_b])
    t_mm("S", (64, 64 + NTT), onecol, mloc.ap(), True, True,
         [mloc_b, CB["c_onecol"]], rows=(0, 1))
    p.op("vector", lambda e: e.tensor_copy(lrow.ap()[0:1, 0:NTT],
                                           ps["S"].ap()[0:1, 64:64 + NTT]),
         reads=[pbuf["S"]], writes=[lrow_b])
    p.op("vector", lambda e: e.tensor_copy(lrow.ap()[0:1, 8:16],
                                           lrow.ap()[0:1, 0:8]),
         reads=[lrow_b], writes=[lrow_b])
    p.op("vector", lambda e: e.tensor_add(lrow.ap()[0:1, 9:16],
                                          lrow.ap()[0:1, 9:16],
                                          lrow.ap()[0:1, 0:7]),
         reads=[lrow_b], writes=[lrow_b])
    p.op("vector", lambda e: e.tensor_copy(lrow.ap()[0:1, 16:24],
                                           lrow.ap()[0:1, 8:16]),
         reads=[lrow_b], writes=[lrow_b])
    p.op("vector", lambda e: e.tensor_add(lrow.ap()[0:1, 18:24],
                                          lrow.ap()[0:1, 18:24],
                                          lrow.ap()[0:1, 8:14]),
         reads=[lrow_b], writes=[lrow_b])
    p.op("vector", lambda e: e.tensor_copy(lrow.ap()[0:1, 24:32],
                                           lrow.ap()[0:1, 16:24]),
         reads=[lrow_b], writes=[lrow_b])
    p.op("vector", lambda e: e.tensor_add(lrow.ap()[0:1, 28:32],
                                          lrow.ap()[0:1, 28:32],
                                          lrow.ap()[0:1, 16:20]),
         reads=[lrow_b], writes=[lrow_b])
    p.op("vector", lambda e: e.tensor_sub(lrow.ap()[0:1, 24:32],
                                          lrow.ap()[0:1, 24:32],
                                          lrow.ap()[0:1, 0:8]),
         reads=[lrow_b], writes=[lrow_b])
    gmask = sq64
    gmask_b = sq64_b
    for bb in range(2):
        p.op("vector", (lambda b2=bb: lambda e: e.tensor_scalar(
            gmask.ap()[:, b2 * 32:(b2 + 1) * 32],
            tab.ap()[:, b2 * 32:(b2 + 1) * 32],
            tcol.ap()[:, b2:b2 + 1], None, ALU.is_gt))(),
             reads=[tab_b, tcol_b], writes=[gmask_b])
    t_mm("S", (0, 64), onecol, gmask.ap(), True, True,
         [gmask_b, CB["c_onecol"]], rows=(0, 1))
    p.op("vector", lambda e: e.tensor_copy(gmask.ap()[0:1, 0:64],
                                           ps["S"].ap()[0:1, 0:64]),
         reads=[pbuf["S"]], writes=[gmask_b])
    p.op("vector", lambda e: e.tensor_mul(gmask.ap()[0:1, 0:64],
                                          gmask.ap()[0:1, 0:64],
                                          consts["c_beforem"].ap()),
         reads=[gmask_b, CB["c_beforem"]], writes=[gmask_b])
    p.op("vector", lambda e: e.reduce_sum(lrow.ap()[0:1, 8:9],
                                          gmask.ap()[0:1, 0:64], axis=AX.X),
         reads=[gmask_b], writes=[lrow_b])
    p.op("vector", lambda e: e.tensor_scalar(lrow.ap()[0:1, 24:32],
                                             lrow.ap()[0:1, 24:32],
                                             lrow.ap()[0:1, 8:9], None,
                                             ALU.add),
         reads=[lrow_b], writes=[lrow_b])
    t_mm("S", (0, NTT), ones[0:1, 0:128], lrow.ap()[0:1, 24:32], True, True,
         [lrow_b, CB["c_ones"]])
    p.op("vector", lambda e: e.tensor_add(sidx.ap(), pexc.ap(),
                                          ps["S"].ap()[:, 0:NTT]),
         reads=[pexc_b, pbuf["S"]], writes=[sidx_b])
    p.op("vector", lambda e: e.tensor_scalar(pexc.ap(), sidx.ap(),
                                             float(KSEL), None, ALU.is_lt),
         reads=[sidx_b], writes=[pexc_b])
    p.op("vector", lambda e: e.tensor_mul(mloc.ap(), mloc.ap(), pexc.ap()),
         reads=[mloc_b, pexc_b], writes=[mloc_b])
    p.op("vector", lambda e: e.tensor_scalar(sidx.ap(), sidx.ap(),
                                             consts["c_b512"].ap()[:, 0:1],
                                             None, ALU.add),
         reads=[sidx_b, CB["c_b512"]], writes=[sidx_b])
    p.op("vector", lambda e: e.tensor_scalar(pexc.ap(), mloc.ap(), -BIG, BIG,
                                             ALU.mult, ALU.add),
         reads=[mloc_b], writes=[pexc_b])
    p.op("vector", lambda e: e.tensor_add(sidx.ap(), sidx.ap(), pexc.ap()),
         reads=[sidx_b, pexc_b], writes=[sidx_b])
    p.op("vector", lambda e: e.tensor_copy(sidx_i.ap(), sidx.ap()),
         reads=[sidx_b], writes=[sidxi_b])
    dbg_store("sidx", [128, NTT], sidx.ap(), [sidx_b])

    selz_dep = p.stream_dep("selz")
    for j in range(NTT):
        p.dma("gpsimd", (lambda jj=j: lambda e: e.indirect_dma_start(
            out=sel_in, out_offset=bass.IndirectOffsetOnAxis(
                sidx_i.ap()[:, jj:jj + 1], 0),
            in_=hln.ap()[:, jj, :], in_offset=None,
            bounds_check=S - 1, oob_is_err=False))(),
            reads=[hln_b[j], sidxi_b], stream="selw",
            deps=[selz_dep])
    selout_b = Buf("sel_out")
    p.coll(lambda e: e.collective_compute(
        "AllReduce", ALU.add, replica_groups=[list(range(NC))],
        ins=[sel_in.opt()], outs=[sel_out.opt()]),
        writes=[selout_b], deps=[p.stream_dep("selw")])

    dbg_dram("selout", sel_out, [S, SELW], BF16, None, reads=[selout_b])
    A.free("scb", "junk", "stab64", "sq64", "xin2", "xr", "lnsc2", "wr_b",
           "sq")
    selr, selr_b = A.alloc("selr", [128, NTT, SELW], BF16, nbufs=NTT)
    p.dma("sync", lambda e: e.dma_start(
        out=selr.ap(), in_=sel_out.rearrange("(j p) c -> p j c", p=128)),
        reads=[selout_b], writes=list(selr_b))
    gates, (gates_b,) = A.alloc("gates", [128, NTT, 1], F32)
    p.op("scalar", lambda e: e.activation(gates.ap(), selr.ap()[:, :, 2050:2051],
                                          AF.Sigmoid),
         reads=list(selr_b), writes=[gates_b])
    posf, (posf_b,) = A.alloc("posf", [128, NTT], F32)
    pos2, (pos2_b,) = A.alloc("pos2", [128, NTT], F32)
    p.op("vector", lambda e: e.tensor_copy(
        posf.ap().rearrange("p (a o) -> p a o", o=1), selr.ap()[:, :, 2048:2049]),
         reads=list(selr_b), writes=[posf_b])
    p.op("vector", lambda e: e.tensor_copy(
        pos2.ap().rearrange("p (a o) -> p a o", o=1), selr.ap()[:, :, 2049:2050]),
         reads=list(selr_b), writes=[pos2_b])
    p.op("vector", lambda e: e.scalar_tensor_tensor(
        posf.ap(), posf.ap(), 64.0, pos2.ap(), ALU.mult, ALU.add),
         reads=[posf_b, pos2_b], writes=[posf_b])
    t_transpose("S", (0, NTT), (0, 128), posf.ap(), identf, [posf_b],
                CB["c_identf"])
    p.op("vector", lambda e: e.tensor_copy(scT.ap(),
                                           ps["S"].ap()[0:NTT, 0:128]),
         reads=[pbuf["S"]], writes=[scT_b])
    p.dma("sync", lambda e: e.dma_start(out=pos_dram, in_=scT.ap()),
          reads=[scT_b], stream="posw")
    posb, (posb_b,) = A.alloc("posb", [128, S], F32)
    p.dma("sync", lambda e: e.dma_start(
        out=posb.ap(), in_=pos_dram.rearrange("(o a) b -> o (a b)", o=1).to_broadcast(
            [128, S])), writes=[posb_b], deps=[p.stream_dep("posw")])
    cosf, (cosf_b,) = A.alloc("cosf", [128, S], F32)
    sinf, (sinf_b,) = A.alloc("sinf", [128, S], F32)
    acyc, (acyc_b,) = A.alloc("acyc", [128, S], F32)
    rr, (rr_b,) = A.alloc("rr", [128, S], F32)
    p.op("vector", lambda e: e.tensor_scalar(
        acyc.ap(), posb.ap(), consts["c_invf"].ap()[:, 0:1], None, ALU.mult),
         reads=[posb_b, CB["c_invf"]], writes=[acyc_b])
    p.op("vector", lambda e: e.tensor_scalar(rr.ap(), acyc.ap(), MAGIC, MAGIC,
                                             ALU.add, ALU.subtract),
         reads=[acyc_b], writes=[rr_b])
    p.op("vector", lambda e: e.tensor_sub(rr.ap(), acyc.ap(), rr.ap()),
         reads=[acyc_b, rr_b], writes=[rr_b])
    p.op("scalar", lambda e: e.activation(sinf.ap(), rr.ap(), AF.Sin,
                                          scale=TWOPI),
         reads=[rr_b], writes=[sinf_b])
    p.op("vector", lambda e: e.tensor_scalar_add(acyc.ap(), acyc.ap(), 0.25),
         reads=[acyc_b], writes=[acyc_b])
    p.op("vector", lambda e: e.tensor_scalar(rr.ap(), acyc.ap(), MAGIC, MAGIC,
                                             ALU.add, ALU.subtract),
         reads=[acyc_b], writes=[rr_b])
    p.op("vector", lambda e: e.tensor_sub(rr.ap(), acyc.ap(), rr.ap()),
         reads=[acyc_b, rr_b], writes=[rr_b])
    p.op("scalar", lambda e: e.activation(cosf.ap(), rr.ap(), AF.Sin,
                                          scale=TWOPI),
         reads=[rr_b], writes=[cosf_b])
    A.free("acyc", "rr", "posb", "posf", "pos2")

    hrow, hrow_b = A.alloc("hrow", [128, 2, E], BF16, nbufs=2)
    sqf, sqf_bs = A.alloc("sqf", [128, 2, E], F32, nbufs=2)
    ln1bh, ln1bb = bcast_load("ln1_b", ln1w, BF16)
    ln1b = ln1bh.ap()

    def rms_stats(src_ap, src_bufs, sl=0):
        st = ST(sl); sb = stat_bs[sl]
        sqv = sqf.ap()[:, sl, :]; sq_b = sqf_bs[sl]
        p.op("scalar", (lambda s=src_ap, o=sqv: lambda e: e.activation(
            o, s, AF.Square))(), reads=list(src_bufs), writes=[sq_b])
        p.op("vector", (lambda st=st, i=sqv: lambda e: e.reduce_sum(
            st["t1"], i, axis=AX.X))(), reads=[sq_b], writes=[sb])
        p.op("vector", (lambda st=st: lambda e: e.tensor_scalar(
            st["t1"], st["t1"], 1.0 / E, RMS_EPS, ALU.mult, ALU.add))(),
             reads=[sb], writes=[sb])
        p.op("scalar", (lambda st=st: lambda e: e.activation(
            st["t2"], st["t1"], AF.Sqrt))(), reads=[sb], writes=[sb])
        p.op("vector", (lambda st=st: lambda e: e.reciprocal(
            st["rstd"], st["t2"]))(), reads=[sb], writes=[sb])
        return st, sb

    for j in range(NTT):
        sl = j % 2
        st, sb = rms_stats(selr.ap()[:, j, 0:E], [selr_b[j]], sl)
        p.op("vector", (lambda s=sl, jj=j, st=st: lambda e: e.tensor_scalar(
            hrow.ap()[:, s, :], selr.ap()[:, jj, 0:E], st["rstd"], None,
            ALU.mult))(),
             reads=[selr_b[j], sb], writes=[hrow_b[sl]])
        p.op("gpsimd", (lambda s=sl: lambda e: e.tensor_mul(
            hrow.ap()[:, s, :], hrow.ap()[:, s, :], ln1b))(),
             reads=[ln1bb], writes=[hrow_b[sl]])
        p.dma("sync", (lambda s=sl, jj=j: lambda e: e.dma_start(
            out=tr_dram[jj * 128:(jj + 1) * 128, :], in_=hrow.ap()[:, s, :]))(),
            reads=[hrow_b[sl]], stream="hw", deps=[p.stream_dep("trp")])
    A.free("ln1_b", "hrow")
    hT, hT_b = A.alloc("hT", [128, 16, S], BF16, nbufs=16)
    emit_transposes(lambda k: hT.ap()[:, k, :], hT_b, tr_dram, 16,
                    p.stream_dep("hw"))

    wq_sb, (wq_b,) = A.alloc("wq_sb", [128, 16, 256], BF16)
    wk_sb, (wk_b,) = A.alloc("wk_sb", [128, 16, 128], BF16)
    wvs_sb, (wvs_b,) = A.alloc("wvs_sb", [128, 16, 128], BF16)
    for hh, W, bobj in ((wq_sb, wq, wq_b), (wk_sb, wk, wk_b),
                        (wvs_sb, wv2, wvs_b)):
        p.dma("gpsimd", (lambda h=hh, WW=W: lambda e: e.dma_start(
            out=h.ap(), in_=WW.rearrange("(kc p) o -> p kc o", p=128)))(),
            writes=[bobj])
    qf, (qf_b,) = A.alloc("qf", [128, 2, S], F32)
    qbf, (qbf_b,) = A.alloc("qbf", [128, 2, S], BF16)
    kkf, (kkf_b,) = A.alloc("kkf", [128, S], F32)
    kkbf, (kkbf_b,) = A.alloc("kkbf", [128, S], BF16)
    rsc1, (rsc1_b,) = A.alloc("rsc1", [128, 512], F32)
    rsc2, (rsc2_b,) = A.alloc("rsc2", [128, 512], F32)
    permT = consts["c_permT"].ap()

    def rope_fm(dst_ap, dst_buf, src_ap, src_buf):
        for th in range(2):
            cols = (th * 512, (th + 1) * 512)
            t_mm("S", (0, 512), permT, src_ap[:, cols[0]:cols[1]],
                 True, True, [src_buf, CB["c_permT"]])
            p.op("vector", (lambda c=cols: lambda e: e.tensor_mul(
                rsc1.ap(), src_ap[:, c[0]:c[1]], cosf.ap()[:, c[0]:c[1]]))(),
                 reads=[src_buf, cosf_b], writes=[rsc1_b])
            p.op("vector", (lambda c=cols: lambda e: e.tensor_mul(
                rsc2.ap(), ps["S"].ap(), sinf.ap()[:, c[0]:c[1]]))(),
                 reads=[pbuf["S"], sinf_b], writes=[rsc2_b])
            p.op("vector", (lambda c=cols: lambda e: e.tensor_add(
                dst_ap[:, c[0]:c[1]], rsc1.ap(), rsc2.ap()))(),
                 reads=[rsc1_b, rsc2_b], writes=[dst_buf])

    for oh in range(2):
        for th in range(2):
            psn = PSN4[th]
            for kc in range(16):
                t_mm(psn, (0, 512),
                     wq_sb.ap()[:, kc, oh * 128:(oh + 1) * 128],
                     hT.ap()[:, kc, th * 512:(th + 1) * 512],
                     kc == 0, False, [wq_b, hT_b[kc]])
            p.op("tensor", (lambda o=ps[psn].ap()[:, 0:512], ohh=oh:
                            lambda e: e.matmul(
                                o, consts["r_fbq"].ap()[0:1,
                                                        ohh * 128:(ohh + 1) * 128],
                                ones[0:1, 0:512], start=False, stop=True))(),
                 reads=[CB["r_fbq"], CB["c_ones"]], writes=[pbuf[psn]])
            p.op("vector", (lambda o=qf.ap()[:, oh, th * 512:(th + 1) * 512],
                            s=ps[psn].ap(): lambda e: e.tensor_copy(o, s))(),
                 reads=[pbuf[psn]], writes=[qf_b])
        rope_fm(qbf.ap()[:, oh, :], qbf_b, qf.ap()[:, oh, :], qf_b)
    for th in range(2):
        psn = PSN4[th]
        for kc in range(16):
            t_mm(psn, (0, 512), wk_sb.ap()[:, kc, :],
                 hT.ap()[:, kc, th * 512:(th + 1) * 512],
                 kc == 0, False, [wk_b, hT_b[kc]])
        p.op("tensor", (lambda o=ps[psn].ap()[:, 0:512]:
                        lambda e: e.matmul(
                            o, consts["r_fbk"].ap()[0:1, 0:128],
                            ones[0:1, 0:512], start=False, stop=True))(),
             reads=[CB["r_fbk"], CB["c_ones"]], writes=[pbuf[psn]])
        p.op("vector", (lambda o=kkf.ap()[:, th * 512:(th + 1) * 512],
                        s=ps[psn].ap(): lambda e: e.tensor_copy(o, s))(),
             reads=[pbuf[psn]], writes=[kkf_b])
    rope_fm(kkbf.ap(), kkbf_b, kkf.ap(), kkf_b)
    vbf, vbf_b = A.alloc("vbf", [128, NTT, 128], BF16, nbufs=NTT)
    for j in range(NTT):
        psn = "B" if j % 2 == 0 else "C"
        for kc in range(16):
            t_mm(psn, (0, 128), hT.ap()[:, kc, j * 128:(j + 1) * 128],
                 wvs_sb.ap()[:, kc, :], kc == 0, False, [wvs_b, hT_b[kc]])
        p.op("tensor", (lambda o=ps[psn].ap()[:, 0:128]:
                        lambda e: e.matmul(
                            o, ones[0:1, 0:128],
                            consts["r_fbv"].ap()[0:1, 0:128],
                            start=False, stop=True))(),
             reads=[CB["r_fbv"], CB["c_ones"]], writes=[pbuf[psn]])
        p.op("vector", (lambda o=vbf.ap()[:, j, :], s=ps[psn].ap()[:, 0:128]:
                        lambda e: e.tensor_copy(o, s))(),
             reads=[pbuf[psn]], writes=[vbf_b[j]])

    A.free("wq_sb", "wk_sb", "wvs_sb", "rsc1", "rsc2", "qf", "kkf")
    wo_sb, (wo_b,) = A.alloc("wo_sb", [128, 2, E], BF16)
    p.dma("gpsimd", lambda e: e.dma_start(
        out=wo_sb.ap(), in_=wo2.rearrange("(h p) o -> p h o", p=128)),
        writes=[wo_b])
    ao, (ao_b,) = A.alloc("ao", [128, 2, S], BF16)
    pex, (pex_b,) = A.alloc("pex", [128, S], BF16)
    attT, attT_b = A.alloc("attT", [128, NTT, 128], BF16, nbufs=NTT)
    tril = consts["c_tril"].ap()
    for qh in range(2):
        for r in range(NTT):
            nk = (r + 1) * 128
            for kb in range(r + 1):
                t_mm("A", (kb * 128, kb * 128 + 128),
                     qbf.ap()[:, qh, r * 128:(r + 1) * 128],
                     kkbf.ap()[:, kb * 128:(kb + 1) * 128],
                     True, True, [qbf_b, kkbf_b])
            p.op("vector", (lambda rr2=r: lambda e: e.tensor_add(
                ps["A"].ap()[:, rr2 * 128:(rr2 + 1) * 128],
                ps["A"].ap()[:, rr2 * 128:(rr2 + 1) * 128], tril))(),
                 reads=[CB["c_tril"], pbuf["A"]], writes=[pbuf["A"]])
            p.op("vector", (lambda n=nk: lambda e: e.reduce_max(
                st_t1, ps["A"].ap()[:, 0:n], axis=AX.X))(),
                 reads=[pbuf["A"]], writes=[stat_b])
            p.op("vector", lambda e: e.tensor_scalar_mul(st_t2, st_t1, -ISQD),
                 reads=[stat_b], writes=[stat_b])
            p.op("scalar", (lambda n=nk: lambda e: e.activation(
                pex.ap()[:, 0:n], ps["A"].ap()[:, 0:n], AF.Exp,
                bias=st_t2, scale=ISQD))(),
                 reads=[pbuf["A"], stat_b], writes=[pex_b])
            p.op("vector", (lambda n=nk: lambda e: e.reduce_sum(
                st_t1, pex.ap()[:, 0:n], axis=AX.X))(),
                 reads=[pex_b], writes=[stat_b])
            p.op("vector", lambda e: e.reciprocal(st_t3, st_t1),
                 reads=[stat_b], writes=[stat_b])
            p.op("vector", (lambda n=nk: lambda e: e.tensor_scalar(
                pex.ap()[:, 0:n], pex.ap()[:, 0:n], st_t3, None, ALU.mult))(),
                 reads=[pex_b, stat_b], writes=[pex_b])
            for kb in range(r + 1):
                t_transpose("T", (0, 128),
                            ((kb % 4) * 128, (kb % 4) * 128 + 128),
                            pex.ap()[:, kb * 128:(kb + 1) * 128], identb,
                            [pex_b], CB["c_identb"])
                p.op("vector", (lambda kbb=kb: lambda e: e.tensor_copy(
                    attT.ap()[:, kbb, :],
                    ps["T"].ap()[:, (kbb % 4) * 128:(kbb % 4) * 128 + 128]))(),
                     reads=[pbuf["T"]], writes=[attT_b[kb]])
            psn = "B"
            for kb in range(r + 1):
                t_mm(psn, (0, 128), vbf.ap()[:, kb, :], attT.ap()[:, kb, :],
                     kb == 0, kb == r, [vbf_b[kb], attT_b[kb]])
            p.op("vector", (lambda o=ao.ap()[:, qh, r * 128:(r + 1) * 128],
                            s=ps[psn].ap()[:, 0:128]:
                            lambda e: e.tensor_copy(o, s))(),
                 reads=[pbuf[psn]], writes=[ao_b])

    apbf, apbf_b = A.alloc("apbf", [128, 2, E], BF16, nbufs=2)
    for j in range(NTT):
        sl = j % 2
        for ec in range(4):
            psn = PSN4[ec]
            for h in range(2):
                t_mm(psn, (0, 512), ao.ap()[:, h, j * 128:(j + 1) * 128],
                     wo_sb.ap()[:, h, ec * 512:(ec + 1) * 512],
                     h == 0, h == 1, [ao_b, wo_b])
            p.op("vector", (lambda o=apbf.ap()[:, sl, ec * 512:(ec + 1) * 512],
                            s=ps[psn].ap(): lambda e: e.tensor_copy(o, s))(),
                 reads=[pbuf[psn]], writes=[apbf_b[sl]])
        p.dma("sync", (lambda s=sl, jj=j: lambda e: e.dma_start(
            out=ar2_in[jj * 128:(jj + 1) * 128, :], in_=apbf.ap()[:, s, :]))(),
            reads=[apbf_b[sl]], stream="a2w")
    ar2out_b = Buf("ar2_out")
    p.coll(lambda e: e.collective_compute(
        "AllReduce", ALU.add, replica_groups=[list(range(NC))],
        ins=[ar2_in.opt()], outs=[ar2_out.opt()]),
        writes=[ar2out_b], deps=[p.stream_dep("a2w")])
    A.free("wo_sb", "apbf")

    dbg_dram("a2", ar2_out, [S, E], BF16, None, reads=[ar2out_b])
    A.free("hT", "qbf", "kkbf", "vbf", "ao", "pex", "attT",
           "cosf", "sinf", "hln", "tab", "gs", "cnt2", "tcol", "mloc",
           "pexc", "lrow", "sidx", "scT")
    a2r, a2r_b = A.alloc("a2r", [128, NTT, E], BF16, nbufs=NTT)
    p.dma("sync", lambda e: e.dma_start(
        out=a2r.ap(), in_=ar2_out.rearrange("(j p) c -> p j c", p=128)),
        reads=[ar2out_b], writes=list(a2r_b))
    x1bf, x1bf_b = A.alloc("x1bf", [128, NTT, E], BF16, nbufs=NTT)
    x1f, x1f_bs = A.alloc("x1f", [128, 2, E], F32, nbufs=2)
    h2row, h2row_b = A.alloc("h2row", [128, 2, E], BF16, nbufs=2)
    ln2bh, ln2bb = bcast_load("ln2_b", ln2w, BF16)
    ln2b = ln2bh.ap()
    for j in range(NTT):
        sl = j % 2
        p.op("vector", (lambda jj=j, s=sl: lambda e: e.tensor_add(
            x1f.ap()[:, s, :], a2r.ap()[:, jj, :], selr.ap()[:, jj, 0:E]))(),
             reads=[a2r_b[j], selr_b[j]], writes=[x1f_bs[sl]])
        p.op("vector", (lambda jj=j, s=sl: lambda e: e.tensor_copy(
            x1bf.ap()[:, jj, :], x1f.ap()[:, s, :]))(),
             reads=[x1f_bs[sl]], writes=[x1bf_b[j]])
        st, sb = rms_stats(x1f.ap()[:, sl, :], [x1f_bs[sl]], sl)
        p.op("vector", (lambda s=sl, st=st: lambda e: e.tensor_scalar(
            h2row.ap()[:, s, :], x1f.ap()[:, s, :], st["rstd"], None,
            ALU.mult))(),
             reads=[x1f_bs[sl], sb], writes=[h2row_b[sl]])
        p.op("gpsimd", (lambda s=sl: lambda e: e.tensor_mul(
            h2row.ap()[:, s, :], h2row.ap()[:, s, :], ln2b))(),
             reads=[ln2bb], writes=[h2row_b[sl]])
        p.dma("sync", (lambda s=sl, jj=j: lambda e: e.dma_start(
            out=tr_dram2[jj * 128:(jj + 1) * 128, :],
            in_=h2row.ap()[:, s, :]))(),
            reads=[h2row_b[sl]], stream="h2w", deps=[p.stream_dep("trp")])
    A.free("a2r", "x1f", "sqf", "h2row", "ln2_b")
    h2T, h2T_b = A.alloc("h2T", [128, 16, S], BF16, nbufs=16)
    emit_transposes(lambda k: h2T.ap()[:, k, :], h2T_b, tr_dram2, 16,
                    p.stream_dep("h2w"))

    wg_sb, wg_b = A.alloc("wg_sb", [128, 2, 16, 128], BF16, nbufs=2)
    wu_sb, wu_b = A.alloc("wu_sb", [128, 2, 16, 128], BF16, nbufs=2)
    mbf, (mbf_b,) = A.alloc("mbf", [128, NTT, S], BF16)
    gsc, (gsc_b,) = A.alloc("gsc", [128, 512], F32)
    for oc in range(NTT):
        slot = oc % 2
        p.dma("gpsimd", (lambda s=slot, occ=oc: lambda e: e.dma_start(
            out=wg_sb.ap()[:, s, :, :],
            in_=wgate[:, occ * 128:(occ + 1) * 128].rearrange(
                "(kc p) o -> p kc o", p=128)))(), writes=[wg_b[slot]])
        p.dma("gpsimd", (lambda s=slot, occ=oc: lambda e: e.dma_start(
            out=wu_sb.ap()[:, s, :, :],
            in_=wup[:, occ * 128:(occ + 1) * 128].rearrange(
                "(kc p) o -> p kc o", p=128)))(), writes=[wu_b[slot]])
        for th in range(2):
            for kc in range(16):
                t_mm("B", (0, 512), wg_sb.ap()[:, slot, kc, :],
                     h2T.ap()[:, kc, th * 512:(th + 1) * 512],
                     kc == 0, kc == 15, [wg_b[slot], h2T_b[kc]])
                t_mm("C", (0, 512), wu_sb.ap()[:, slot, kc, :],
                     h2T.ap()[:, kc, th * 512:(th + 1) * 512],
                     kc == 0, kc == 15, [wu_b[slot], h2T_b[kc]])
            p.op("scalar", lambda e: e.activation(gsc.ap(), ps["B"].ap(),
                                                  AF.Silu),
                 reads=[pbuf["B"]], writes=[gsc_b])
            p.op("vector", (lambda o=mbf.ap()[:, oc, th * 512:(th + 1) * 512]:
                            lambda e: e.tensor_mul(o, gsc.ap(),
                                                   ps["C"].ap()))(),
                 reads=[gsc_b, pbuf["C"]], writes=[mbf_b])

    A.free("h2T", "wg_sb", "wu_sb", "gsc")
    wd_sb, wd_b = A.alloc("wd_sb", [128, 2, NTT, 512], BF16, nbufs=2)
    mpq, mpq_b = A.alloc("mpq", [128, 2, 512], BF16, nbufs=2)
    for ec in range(4):
        slot = ec % 2
        p.dma("gpsimd", (lambda s=slot, e2=ec: lambda e: e.dma_start(
            out=wd_sb.ap()[:, s, :, :],
            in_=wdown[:, e2 * 512:(e2 + 1) * 512].rearrange(
                "(ic p) o -> p ic o", p=128)))(), writes=[wd_b[slot]])
        for j in range(NTT):
            psn = "B" if j % 2 == 0 else "C"
            for ic in range(NTT):
                t_mm(psn, (0, 512), mbf.ap()[:, ic, j * 128:(j + 1) * 128],
                     wd_sb.ap()[:, slot, ic, :],
                     ic == 0, ic == NTT - 1, [mbf_b, wd_b[slot]])
            msl = j % 2
            p.op("vector", (lambda o=mpq.ap()[:, msl, :], s=ps[psn].ap():
                            lambda e: e.tensor_copy(o, s))(),
                 reads=[pbuf[psn]], writes=[mpq_b[msl]])
            p.dma("sync", (lambda s=msl, jj=j, e2=ec: lambda e: e.dma_start(
                out=ar3_in[jj * 128:(jj + 1) * 128,
                           e2 * 512:(e2 + 1) * 512],
                in_=mpq.ap()[:, s, :]))(),
                reads=[mpq_b[msl]], stream="a3w")
    ar3out_b = Buf("ar3_out")
    p.coll(lambda e: e.collective_compute(
        "AllReduce", ALU.add, replica_groups=[list(range(NC))],
        ins=[ar3_in.opt()], outs=[ar3_out.opt()]),
        writes=[ar3out_b], deps=[p.stream_dep("a3w")])

    dbg_dram("a3", ar3_out, [S, E], BF16, None, reads=[ar3out_b])
    A.free("mbf", "wd_sb", "mpq")
    a3r, a3r_b = A.alloc("a3r", [128, NTT, E], BF16, nbufs=NTT)
    p.dma("sync", lambda e: e.dma_start(
        out=a3r.ap(), in_=ar3_out.rearrange("(j p) c -> p j c", p=128)),
        reads=[ar3out_b], writes=list(a3r_b))
    procf, (procf_b,) = A.alloc("procf", [128, E], F32)
    dlb, dlb_b = A.alloc("dlb", [128, 2, E], BF16, nbufs=2)
    for j in range(NTT):
        sl = j % 2
        p.op("vector", (lambda jj=j: lambda e: e.tensor_add(
            procf.ap(), a3r.ap()[:, jj, :], x1bf.ap()[:, jj, :]))(),
             reads=[a3r_b[j], x1bf_b[j]], writes=[procf_b])
        p.op("vector", (lambda jj=j: lambda e: e.tensor_sub(
            procf.ap(), procf.ap(), selr.ap()[:, jj, 0:E]))(),
             reads=[procf_b, selr_b[j]], writes=[procf_b])
        p.op("vector", (lambda s=sl, jj=j: lambda e: e.tensor_scalar(
            dlb.ap()[:, s, :], procf.ap(), gates.ap()[:, jj, 0:1], None,
            ALU.mult))(),
             reads=[procf_b, gates_b], writes=[dlb_b[sl]])
        p.dma("sync", (lambda s=sl, jj=j: lambda e: e.dma_start(
            out=delta_dram[jj * 128:(jj + 1) * 128, :],
            in_=dlb.ap()[:, s, :]))(),
            reads=[dlb_b[sl]], stream="dlw")
    dl_dep = p.stream_dep("dlw")
    A.free("a3r", "x1bf", "selr", "procf", "dlb")
    xrr, xrr_b = A.alloc("xrr", [128, 2, E], F32, nbufs=2)
    hlr, hlr_b = A.alloc("hlr", [128, 2, E], BF16, nbufs=2)
    dgt, dgt_b = A.alloc("dgt", [128, 2, E], BF16, nbufs=2)
    outf, outf_b = A.alloc("outf", [128, 2, E], F32, nbufs=2)
    dbg_dram("delta", delta_dram, [S, E], BF16, "dlw")
    xr_dep = p.stream_dep("xrw")
    hl_dep = p.stream_dep("hlw")
    for j in range(NTT):
        sl = j % 2
        p.dma("sync", (lambda s=sl, jj=j: lambda e: e.dma_start(
            out=xrr.ap()[:, s, :], in_=xres_dram[jj * 128:(jj + 1) * 128, :]))(),
            writes=[xrr_b[sl]], deps=[xr_dep])
        p.dma("scalar", (lambda s=sl, jj=j: lambda e: e.dma_start(
            out=hlr.ap()[:, s, :], in_=hln_dram[jj * 128:(jj + 1) * 128, :]))(),
            writes=[hlr_b[sl]], deps=[hl_dep])
        p.op("vector", (lambda s=sl: lambda e: e.memset(
            dgt.ap()[:, s, :], 0.0))(), writes=[dgt_b[sl]])
        p.dma("gpsimd", (lambda s=sl, jj=j: lambda e: e.indirect_dma_start(
            out=dgt.ap()[:, s, :], out_offset=None,
            in_=delta_dram, in_offset=bass.IndirectOffsetOnAxis(
                sidx_i.ap()[:, jj:jj + 1], 0),
            bounds_check=S - 1, oob_is_err=False))(),
            reads=[sidxi_b], writes=[dgt_b[sl]], deps=[dl_dep])
        p.op("vector", (lambda s=sl: lambda e: e.tensor_add(
            outf.ap()[:, s, :], xrr.ap()[:, s, :], hlr.ap()[:, s, :]))(),
             reads=[xrr_b[sl], hlr_b[sl]], writes=[outf_b[sl]])
        p.op("vector", (lambda s=sl: lambda e: e.tensor_add(
            outf.ap()[:, s, :], outf.ap()[:, s, :], dgt.ap()[:, s, :]))(),
             reads=[outf_b[sl], dgt_b[sl]], writes=[outf_b[sl]])
        p.dma("sync", (lambda s=sl, jj=j: lambda e: e.dma_start(
            out=y_out[jj * 128:(jj + 1) * 128, :], in_=outf.ap()[:, s, :]))(),
            reads=[outf_b[sl]], stream="yw")

    p.final_wait_streams(["yw"] + (["dbg"] if "dbg" in p.streams else []))
    p.finish()
    return nc, dbg_outs


def _shard_inputs(inputs):
    x = np.ascontiguousarray(np.asarray(inputs["L0_attn_x"],
                                        dtype=np.float32).reshape(T, E))
    per = []
    for c in range(NC):
        d = {
            "x": x[c * TPC:(c + 1) * TPC],
            "Wqk": inputs["L0_attn_Wqk"],
            "bqk": inputs["L0_attn_bqk"].reshape(1, E),
            "Wv": inputs["L0_attn_Wv"], "bv": inputs["L0_attn_bv"].reshape(1, E),
            "Wu": inputs["L0_attn_Wu"], "bu": inputs["L0_attn_bu"].reshape(1, E),
            "Wo": inputs["L0_attn_Wo"], "bo": inputs["L0_attn_bo"].reshape(1, E),
            "ln_g": inputs["L0_attn_ln_g"].reshape(1, E),
            "ln_b": inputs["L0_attn_ln_b"].reshape(1, E),
            "w_router": inputs["L0_ffn_w_router"].reshape(1, E),
            "wq": inputs["L0_ffn_wq"][:, 256 * c:256 * (c + 1)],
            "fbq": inputs["L0_ffn_bq"][256 * c:256 * (c + 1)].reshape(1, 256),
            "wk": inputs["L0_ffn_wk"][:, 128 * c:128 * (c + 1)],
            "fbk": inputs["L0_ffn_bk"][128 * c:128 * (c + 1)].reshape(1, 128),
            "wv2": inputs["L0_ffn_wv"][:, 128 * c:128 * (c + 1)],
            "fbv": inputs["L0_ffn_bv"][128 * c:128 * (c + 1)].reshape(1, 128),
            "wo2": inputs["L0_ffn_wo"][256 * c:256 * (c + 1), :],
            "wgate": inputs["L0_ffn_w_gate"][:, 1024 * c:1024 * (c + 1)],
            "wup": inputs["L0_ffn_w_up"][:, 1024 * c:1024 * (c + 1)],
            "wdown": inputs["L0_ffn_w_down"][1024 * c:1024 * (c + 1), :],
            "ln1w": inputs["L0_ffn_ln1_w"].reshape(1, E),
            "ln2w": inputs["L0_ffn_ln2_w"].reshape(1, E),
        }
        d = {k: np.ascontiguousarray(np.asarray(v, dtype=np.float32))
             for k, v in d.items()}
        d.update(_host_constants(c))
        per.append(d)
    return per


LAST_DBG = {}


def kernel(**inputs):
    nc, dbg = build_program()
    in_maps = _shard_inputs(inputs)
    res = run_bass_kernel_spmd(nc, in_maps, core_ids=list(range(NC)))
    global LAST_DBG
    LAST_DBG = {name: [np.asarray(res.results[c]["dbg_" + name])
                       for c in range(NC)] for name in dbg}
    out = np.concatenate([np.asarray(res.results[c]["y"]) for c in range(NC)],
                         axis=0)
    return out.reshape(B, N, E)


# version 07
# speedup vs baseline: 1.0833x; 1.0833x over previous
import math
import numpy as np
import ml_dtypes
import concourse.bass as bass
import concourse.mybir as mybir
from concourse.bass_utils import run_bass_kernel_spmd

F32 = mybir.dt.float32
BF16 = mybir.dt.bfloat16
I32 = mybir.dt.int32
AF = mybir.ActivationFunctionType
ALU = mybir.AluOpType
AX = mybir.AxisListType

B, N, E = 2, 4096, 2048
T = B * N
NC = 8
TPC = T // NC
NTT = TPC // 128
HEADS, DH = 16, 128
KSEL, S = 512, 1024
IF = 8192
LN_EPS = 1e-5
RMS_EPS = 1e-6
SELW = 2056
BIG = 1.0e6
MAGIC = 12582912.0
INV2PI = 1.0 / (2.0 * math.pi)
TWOPI = 2.0 * math.pi
ISQD = 1.0 / math.sqrt(DH)

import os
DEBUG = [s for s in os.environ.get("KDEBUG", "").split(",") if s]


ENGINES = ("tensor", "vector", "scalar", "gpsimd", "sync")


class _Op:
    __slots__ = ("idx", "eng", "emit", "kind", "sem", "val", "deps", "flag",
                 "eidx")

    def __init__(self, idx, eng, emit, kind):
        self.idx, self.eng, self.emit, self.kind = idx, eng, emit, kind
        self.sem = None
        self.val = None
        self.deps = []
        self.flag = False
        self.eidx = None


class Buf:
    __slots__ = ("name", "last_writer", "readers", "dma_sem", "dma_count")

    def __init__(self, name):
        self.name = name
        self.last_writer = None
        self.readers = []
        self.dma_sem = None
        self.dma_count = 0


class Stream:
    __slots__ = ("name", "sem", "count")

    def __init__(self, name, sem):
        self.name, self.sem, self.count = name, sem, 0


class Prog:
    def __init__(self, nc):
        self.nc = nc
        self.ops = []
        self.eng_ops = {e: [] for e in ENGINES}
        self.eng_sem = {e: nc.alloc_semaphore(f"esem_{e}") for e in
                        ("tensor", "vector", "scalar", "gpsimd")}
        self.streams = {}
        self.final_waits = []
        self.same_engine_implicit = False

    def stream(self, name):
        if name not in self.streams:
            self.streams[name] = Stream(name, self.nc.alloc_semaphore("ss_" + name))
        return self.streams[name]

    def stream_dep(self, name):
        st = self.stream(name)
        return ("rawsem", st.sem, 16 * st.count)

    def _record(self, eng, emit, kind, reads, writes, extra_deps):
        op = _Op(len(self.ops), eng, emit, kind)
        self.ops.append(op)
        self.eng_ops[eng].append(op)
        op.eidx = len(self.eng_ops[eng]) - 1
        for d in extra_deps:
            op.deps.append(d)
        for b in reads:
            if b.last_writer is not None:
                op.deps.append(b.last_writer)
            b.readers.append(op)
        for b in writes:
            if b.last_writer is not None and b.last_writer is not op:
                op.deps.append(b.last_writer)
            for r in b.readers:
                if r is not op:
                    op.deps.append(r)
            b.readers = []
            b.last_writer = op
        for i, d in enumerate(op.deps):
            if (not isinstance(d, tuple) and d.kind == "dma"
                    and isinstance(d.sem, Stream)):
                if d.sem.name == "trp":
                    op.deps[i] = ("rawsem", d.sem.sem, d.val)
                else:
                    op.deps[i] = ("rawsem", d.sem.sem, 16 * d.sem.count)
        return op

    def op(self, eng, emit, reads=(), writes=(), deps=()):
        assert eng != "sync"
        return self._record(eng, emit, "eng", reads, writes, deps)

    def dma(self, eng, emit, reads=(), writes=(), stream=None, deps=()):
        op = self._record(eng, emit, "dma", reads, writes, deps)
        if stream is not None:
            st = self.stream(stream)
            st.count += 1
            op.sem = st
            op.val = 16 * st.count
        else:
            assert len(writes) >= 1
            b = writes[0]
            if b.dma_sem is None:
                b.dma_sem = self.nc.alloc_semaphore("ds_" + b.name)
            if b.dma_count > 0:
                op.deps.append(("rawsem", b.dma_sem, 16 * b.dma_count))
            b.dma_count += 1
            op.sem = b
            op.val = 16 * b.dma_count
        return op

    def coll(self, emit, reads=(), writes=(), deps=()):
        op = self._record("gpsimd", emit, "coll", reads, writes, deps)
        op.sem = self.nc.alloc_semaphore(f"cs_{op.idx}")
        op.val = 1
        return op

    def final_wait_streams(self, names):
        for n in names:
            st = self.stream(n)
            self.final_waits.append((st.sem, 16 * st.count))

    def _completion(self, dep):
        if isinstance(dep, tuple):
            return dep[1], dep[2]
        if dep.kind == "eng":
            dep.flag = True
            return None, dep
        if dep.kind == "coll":
            return dep.sem, dep.val
        if isinstance(dep.sem, Stream):
            return dep.sem.sem, dep.val
        return dep.sem.dma_sem, dep.val

    def finish(self):
        nc = self.nc
        waitlists = []
        for op in self.ops:
            wl = []
            for dep in op.deps:
                if not isinstance(dep, tuple):
                    assert dep.idx < op.idx
                    if (dep.kind == "eng" and dep.eng == op.eng
                            and (self.same_engine_implicit
                                 or op.eng == "tensor")):
                        continue
                wl.append(self._completion(dep))
            waitlists.append(wl)
        cnt = {e: 0 for e in ENGINES}
        for op in self.ops:
            if op.kind == "eng" and op.flag:
                cnt[op.eng] += 1
                op.val = cnt[op.eng]
                op.sem = self.eng_sem[op.eng]
        resolved = []
        for wl in waitlists:
            resolved.append([(val.sem, val.val) if sem is None else (sem, val)
                             for sem, val in wl])
        know = {e: {} for e in ENGINES}
        for op in self.ops:
            k = know[op.eng]
            wl = []
            for sem, val in resolved[op.idx]:
                key = sem.num if hasattr(sem, "num") else id(sem)
                if k.get(key, -1) >= val:
                    continue
                k[key] = val
                wl.append((sem, val))
            resolved[op.idx] = wl
        with nc.Block() as block:
            def mk(engname):
                def run(eng):
                    for op in self.eng_ops[engname]:
                        for sem, val in resolved[op.idx]:
                            eng.wait_ge(sem, val)
                        inst = op.emit(eng)
                        if op.kind == "eng":
                            if op.flag:
                                inst.then_inc(op.sem, 1)
                        elif op.kind == "coll":
                            inst.then_inc(op.sem, 1)
                        else:
                            sem = (op.sem.sem if isinstance(op.sem, Stream)
                                   else op.sem.dma_sem)
                            inst.then_inc(sem, 16)
                    if engname == "sync":
                        for sem, val in self.final_waits:
                            eng.wait_ge(sem, val)
                return run
            block.tensor(mk("tensor"))
            block.vector(mk("vector"))
            block.scalar(mk("scalar"))
            block.gpsimd(mk("gpsimd"))
            block.sync(mk("sync"))


class Arena:

    def __init__(self, nc, base, size):
        self.nc = nc
        self.base = base
        self.size = size
        self.live = {}
        self.freed = []
        self.n = 0

    def alloc(self, name, shape, dtype, nbufs=1, align=512):
        per = int(np.prod(shape[1:])) * mybir.dt.size(dtype)
        per = (per + align - 1) // align * align
        spans = sorted((o, e) for (o, e, _) in self.live.values())
        holes = []
        cur = 0
        for (s, e) in spans:
            if s > cur:
                holes.append((cur, s))
            cur = max(cur, e)
        if cur < self.size:
            holes.append((cur, self.size))
        fit = [h for h in holes if h[1] - h[0] >= per]
        assert fit, \
            f"arena overflow: {name} needs {per}B, holes {holes}"
        off = min(fit, key=lambda h: h[1] - h[0])[0]
        war = []
        keep = []
        for (fo, fe, ops) in self.freed:
            if fo < off + per and off < fe:
                war.extend(ops)
                if fo < off:
                    keep.append((fo, off, ops))
                if fe > off + per:
                    keep.append((off + per, fe, ops))
            else:
                keep.append((fo, fe, ops))
        self.freed = keep
        bufs = [Buf(f"{name}_{i}") for i in range(nbufs)]
        for b in bufs:
            b.readers = list(war)
        self.n += 1
        h = self.nc.alloc_sbuf_tensor_at(f"ar{self.n}_{name}", list(shape),
                                         dtype, offset=self.base + off)
        self.live[name] = (off, off + per, bufs)
        return h, bufs

    def free(self, *names):
        for nm in names:
            off, end, bufs = self.live.pop(nm)
            ops = []
            for b in bufs:
                ops.extend(b.readers)
                if b.last_writer is not None:
                    ops.append(b.last_writer)
            self.freed.append((off, end, ops))


def _host_constants(core):
    b = core // 4
    rb = core % 4
    cst = {}
    cst["c_identf"] = np.eye(128, dtype=np.float32)
    cst["c_identb"] = np.eye(128, dtype=ml_dtypes.bfloat16)
    cst["c_sut"] = np.triu(np.ones((128, 128), np.float32), 1)
    cst["c_tril"] = np.where(np.tril(np.ones((128, 128), bool)), 0.0,
                             -1.0e30).astype(np.float32)
    P = np.zeros((128, 128), np.float32)
    for i in range(64):
        P[i, i + 64] = -1.0
        P[i + 64, i] = 1.0
    cst["c_permT"] = np.ascontiguousarray(P.T)
    cst["c_iota"] = np.arange(1, 129, dtype=np.float32).reshape(128, 1)
    cst["c_onecol"] = np.ones((128, 1), np.float32)
    inv = 1.0 / (10000.0 ** (np.arange(0, DH, 2, dtype=np.float32) / DH))
    invd = np.concatenate([inv, inv]).reshape(128, 1)
    cst["c_invf"] = (invd * INV2PI).astype(np.float32)
    cst["c_ones"] = np.ones((1, 512), np.float32)
    sbv = np.zeros((128, 2), np.float32)
    sbv[:, 0] = 1.0 - b
    sbv[:, 1] = b
    cst["c_selb"] = sbv
    f0 = b * 32 + rb * 8
    bm = np.zeros((1, 64), np.float32)
    bm[0, b * 32:f0] = 1.0
    cst["c_beforem"] = bm
    tl = np.arange(TPC)
    pos = rb * TPC + tl
    pe = np.zeros((128, NTT, 2), np.float32)
    pe[:, :, 0] = (pos // 64).reshape(NTT, 128).T
    pe[:, :, 1] = (pos % 64).reshape(NTT, 128).T
    cst["c_posex"] = pe.astype(ml_dtypes.bfloat16)
    cst["c_b512"] = np.full((128, 1), 512.0 * b, np.float32)
    return cst


_CONST_SPECS = None


def _const_specs():
    global _CONST_SPECS
    if _CONST_SPECS is None:
        c = _host_constants(0)
        _CONST_SPECS = {k: (list(v.shape),
                            BF16 if v.dtype == ml_dtypes.bfloat16 else F32)
                        for k, v in c.items()}
    return _CONST_SPECS


def build_program():
    nc = bass.Bass("TRN2", target_bir_lowering=False)
    p = Prog(nc)

    din = {}

    def dt_in(name, shape, dtype=F32):
        din[name] = nc.dram_tensor(name, list(shape), dtype,
                                   kind="ExternalInput").ap()
        return din[name]

    x_in = dt_in("x", [TPC, E])
    Wqk = dt_in("Wqk", [E, E]); dt_in("bqk", [1, E])
    Wv = dt_in("Wv", [E, E]); dt_in("bv", [1, E])
    Wu = dt_in("Wu", [E, E]); dt_in("bu", [1, E])
    Wo = dt_in("Wo", [E, E]); dt_in("bo", [1, E])
    ln_g = dt_in("ln_g", [1, E]); ln_b = dt_in("ln_b", [1, E])
    w_router = dt_in("w_router", [1, E])
    wq = dt_in("wq", [E, 256]); dt_in("fbq", [1, 256])
    wk = dt_in("wk", [E, 128]); dt_in("fbk", [1, 128])
    wv2 = dt_in("wv2", [E, 128]); dt_in("fbv", [1, 128])
    wo2 = dt_in("wo2", [256, E])
    wgate = dt_in("wgate", [E, 1024])
    wup = dt_in("wup", [E, 1024])
    wdown = dt_in("wdown", [1024, E])
    ln1w = dt_in("ln1w", [1, E]); ln2w = dt_in("ln2w", [1, E])
    for cname, (cshape, cdt) in _const_specs().items():
        dt_in(cname, cshape, cdt)

    y_out = nc.dram_tensor("y", [TPC, E], F32, kind="ExternalOutput").ap()

    def dscr(name, shape, dtype):
        return nc.dram_tensor(name, list(shape), dtype, kind="Internal").ap()

    tr_dram = dscr("tr_dram", [TPC, E], BF16)
    tr_dram2 = dscr("tr_dram2", [TPC, E], BF16)
    kh_dram = dscr("kh_dram", [TPC, E], BF16)
    u_dram = dscr("u_dram", [TPC, E], F32)
    y2_dram = dscr("y2_dram", [TPC, E], F32)
    kv_in = dscr("kv_in", [128, 2 * E], F32)
    kv_out = dscr("kv_out", [128, 2 * E], F32)
    sc_in = dscr("sc_in", [NTT, 128], F32)
    sc_out = dscr("sc_out", [NC * NTT, 128], F32)
    xres_dram = dscr("xres_dram", [TPC, E], F32)
    hln_dram = dscr("hln_dram", [TPC, E], BF16)
    sel_in = dscr("sel_in", [S, SELW], BF16)
    sel_out = dscr("sel_out", [S, SELW], BF16)
    pos_dram = dscr("pos_dram", [NTT, 128], F32)
    ar2_in = dscr("ar2_in", [S, E], BF16)
    ar2_out = dscr("ar2_out", [S, E], BF16)
    ar3_in = dscr("ar3_in", [S, E], BF16)
    ar3_out = dscr("ar3_out", [S, E], BF16)
    delta_dram = dscr("delta_dram", [S, E], BF16)

    dbg_outs = {}

    def dbg_dram(name, dram_ap, shape, dtype, stream_name=None, reads=()):
        if name not in DEBUG:
            return
        d = nc.dram_tensor("dbg_" + name, list(shape), dtype,
                           kind="ExternalOutput").ap()
        dbg_outs[name] = d
        deps = [p.stream_dep(stream_name)] if stream_name else []
        p.dma("scalar", (lambda dd=d, s=dram_ap: lambda e: e.dma_start(
            out=dd, in_=s))(), reads=list(reads), stream="dbg", deps=deps)

    def dbg_store(name, shape, src_ap, reads, dtype=F32):
        if name not in DEBUG:
            return
        d = nc.dram_tensor("dbg_" + name, list(shape), dtype,
                           kind="ExternalOutput").ap()
        dbg_outs[name] = d
        p.dma("sync", (lambda dd=d, s=src_ap: lambda e: e.dma_start(
            out=dd, in_=s))(), reads=list(reads), stream="dbg")

    sbt = {}

    def salloc(name, shape, dtype):
        sbt[name] = nc.alloc_sbuf_tensor("s_" + name, list(shape), dtype)
        return sbt[name]

    consts = {}
    CB = {}
    for cname, (cshape, cdt) in _const_specs().items():
        hh = salloc(cname, cshape, cdt)
        consts[cname] = hh
        CB[cname] = Buf(cname)
        q = "gpsimd" if cdt == BF16 else "sync"
        p.dma(q, (lambda h=hh, a=din[cname]:
                  lambda e: e.dma_start(out=h.ap(), in_=a))(),
              writes=[CB[cname]], stream="cload")
    def bcast_load(nm, srcrow, dt_):
        hh, (bb,) = A.alloc(nm, [128, E], dt_)
        q = "gpsimd" if dt_ == BF16 else "sync"
        p.dma(q, (lambda h=hh, a=srcrow:
                  lambda e: e.dma_start(out=h.ap(),
                                        in_=a.to_broadcast([128, E])))(),
              writes=[bb])
        return hh, bb
    for nm, src, w in (("r_fbq", din["fbq"], 256), ("r_fbk", din["fbk"], 128),
                       ("r_fbv", din["fbv"], 128)):
        hh = salloc(nm, [1, w], F32)
        consts[nm] = hh
        CB[nm] = Buf(nm)
        p.dma("sync", (lambda h=hh, a=src:
                       lambda e: e.dma_start(out=h.ap(), in_=a))(),
              writes=[CB[nm]], stream="cload")
    bias_slot = salloc("bias_slot", [1, 4 * 512], F32)
    bias_slot_b = [Buf(f"bias{i}") for i in range(4)]

    salloc("stats", [128, 2, 8], F32)
    salloc("wsum", [128, 1], F32)
    salloc("scores", [128, NTT], F32)
    _stap = sbt["stats"].ap()

    def ST(sl):
        a = _stap[:, sl, :]
        return {"mu": a[:, 0:1], "rstd": a[:, 1:2], "t1": a[:, 2:3],
                "t2": a[:, 3:4], "t3": a[:, 4:5]}
    stat_bs = [Buf("stat0"), Buf("stat1")]
    st_mu = ST(0)["mu"]; st_rstd = ST(0)["rstd"]
    st_t1 = ST(0)["t1"]; st_t2 = ST(0)["t2"]; st_t3 = ST(0)["t3"]
    stat_b = stat_bs[0]
    scsb = sbt["scores"].ap()
    sc_b = Buf("scores")

    ones = consts["c_ones"].ap()
    onecol = consts["c_onecol"].ap()
    selb = consts["c_selb"].ap()
    identb = consts["c_identb"].ap()
    identf = consts["c_identf"].ap()


    ps = {}
    ps["A"] = nc.alloc_psum_tensor("psA", [128, 1024], F32)
    for nm in ("B", "C", "D", "E2"):
        ps[nm] = nc.alloc_psum_tensor("ps" + nm, [128, 512], F32)
    ps["T"] = nc.alloc_psum_tensor("psT", [128, 1024], BF16)
    ps["S"] = nc.alloc_psum_tensor("psS", [128, 512], F32)
    pbuf = {k: Buf("ps" + k) for k in ps}
    PSN4 = ("B", "C", "D", "E2")

    avail = nc.sbuf_bytes_remaining
    arena_size = (avail - 12 * 1024) // 512 * 512
    backing = nc.alloc_sbuf_tensor("arena_backing", [128, arena_size],
                                   mybir.dt.uint8)
    arena_base = nc.lookup_mloc(backing).addr
    A = Arena(nc, arena_base, arena_size)

    def t_mm(psname, cols, lhsT, rhs, start, stop, reads, rows=(0, 128)):
        o = ps[psname].ap()[rows[0]:rows[1], cols[0]:cols[1]]
        p.op("tensor",
             (lambda o=o, l=lhsT, r=rhs, st=start, sp=stop:
              lambda e: e.matmul(o, l, r, start=st, stop=sp))(),
             reads=list(reads), writes=[pbuf[psname]])

    def t_transpose(psname, rows, cols, in_ap, ident_ap, reads, iread):
        o = ps[psname].ap()[rows[0]:rows[1], cols[0]:cols[1]]
        p.op("tensor",
             (lambda o=o, i=in_ap, idn=ident_ap:
              lambda e: e.transpose(o, i, idn))(),
             reads=list(reads) + [iread], writes=[pbuf[psname]])


    def pe_transpose_tile(dstT_ap, dst_buf, src_ap, src_bufs, j, nch=16):
        for g in range(nch // 8):
            for kk in range(8):
                kc = g * 8 + kk
                t_transpose("T", (0, 128), (kk * 128, kk * 128 + 128),
                            src_ap[:, kc * 128:(kc + 1) * 128], identb,
                            list(src_bufs), CB["c_identb"])
            p.op("vector", (lambda gg=g, d=dstT_ap, jj=j:
                            lambda e: e.tensor_copy(
                                d[:, gg * 8:(gg + 1) * 8,
                                  jj * 128:(jj + 1) * 128],
                                ps["T"].ap().rearrange(
                                    "p (a b) -> p a b", a=8)))(),
                 reads=[pbuf["T"]], writes=[dst_buf])

    def emit_transposes(dst_ap_fn, dst_bufs, src_dram, nchunks, dep):
        st = p.stream("trp")
        for kc in range(nchunks):
            deps = [dep, ("rawsem", st.sem, 16 * st.count)]
            p.dma("sync", (lambda k=kc: lambda e: e.dma_start(
                out=dst_ap_fn(k), in_=src_dram[:, k * 128:(k + 1) * 128],
                transpose=True))(),
                writes=[dst_bufs[kc]], stream="trp", deps=deps)

    zt, (zt_b,) = A.alloc("ztile", [128, SELW], BF16)
    p.op("vector", lambda e: e.memset(zt.ap(), 0.0), writes=[zt_b])
    for j in range(NTT):
        p.dma("scalar", (lambda jj=j: lambda e: e.dma_start(
            out=sel_in[jj * 128:(jj + 1) * 128, :], in_=zt.ap()))(),
            reads=[zt_b], stream="selz")
    A.free("ztile")

    xin, xin_b = A.alloc("xin", [128, 2, E], F32, nbufs=2)
    xln, xln_b = A.alloc("xln", [128, 2, E], BF16, nbufs=2)
    sq, sq_bs = A.alloc("sq", [128, 2, E], F32, nbufs=2)
    xT, xT_bt = A.alloc("xT", [128, 16, TPC], BF16, nbufs=NTT)

    def ln_stats(src_ap, src_bufs, eps, sl=0):
        st = ST(sl); sb = stat_bs[sl]; sqv = sq.ap()[:, sl, :]
        sq_b = sq_bs[sl]
        p.op("vector", (lambda s=src_ap, o=st["t1"]: lambda e: e.reduce_sum(
            o, s, axis=AX.X))(), reads=list(src_bufs), writes=[sb])
        p.op("vector", (lambda st=st: lambda e: e.tensor_scalar_mul(
            st["mu"], st["t1"], 1.0 / E))(), reads=[sb], writes=[sb])
        p.op("scalar", (lambda s=src_ap, o=sqv: lambda e: e.activation(
            o, s, AF.Square))(), reads=list(src_bufs), writes=[sq_b])
        p.op("vector", (lambda st=st, i=sqv: lambda e: e.reduce_sum(
            st["t2"], i, axis=AX.X))(), reads=[sq_b], writes=[sb])
        p.op("vector", (lambda st=st, ep=eps: lambda e: e.tensor_scalar(
            st["t2"], st["t2"], 1.0 / E, ep, ALU.mult, ALU.add))(),
             reads=[sb], writes=[sb])
        p.op("vector", (lambda st=st: lambda e: e.tensor_mul(
            st["t1"], st["mu"], st["mu"]))(), reads=[sb], writes=[sb])
        p.op("vector", (lambda st=st: lambda e: e.tensor_sub(
            st["t2"], st["t2"], st["t1"]))(), reads=[sb], writes=[sb])
        p.op("scalar", (lambda st=st: lambda e: e.activation(
            st["t1"], st["t2"], AF.Sqrt))(), reads=[sb], writes=[sb])
        p.op("vector", (lambda st=st: lambda e: e.reciprocal(
            st["rstd"], st["t1"]))(), reads=[sb], writes=[sb])
        p.op("vector", (lambda st=st: lambda e: e.tensor_mul(
            st["t2"], st["mu"], st["rstd"]))(), reads=[sb], writes=[sb])
        return st, sb

    for j in range(NTT):
        sl = j % 2
        p.dma("sync", (lambda s=sl, jj=j: lambda e: e.dma_start(
            out=xin.ap()[:, s, :], in_=x_in[jj * 128:(jj + 1) * 128, :]))(),
            writes=[xin_b[sl]])
        st, sb = ln_stats(xin.ap()[:, sl, :], [xin_b[sl]], LN_EPS, sl)
        p.op("vector", (lambda s=sl, st=st: lambda e: e.tensor_scalar(
            xln.ap()[:, s, :], xin.ap()[:, s, :], st["rstd"], st["t2"],
            ALU.mult, ALU.subtract))(),
             reads=[xin_b[sl], sb], writes=[xln_b[sl]])
        pe_transpose_tile(xT.ap(), xT_bt[j], xln.ap()[:, sl, :],
                          [xln_b[sl]], j)
    A.free("xin", "xln")

    wqk_buf, wqk_b = A.alloc("wqk", [128, 2, 16, 512], BF16, nbufs=2)

    def load_wquarter(slot, W, oc):
        src = W[:, oc * 512:(oc + 1) * 512].rearrange("(kc p) o -> p kc o",
                                                      p=128)
        p.dma("gpsimd", (lambda s=slot, ss=src: lambda e: e.dma_start(
            out=wqk_buf.ap()[:, s, :, :], in_=ss))(), writes=[wqk_b[slot]])

    def load_biasq(slot, brow, oc):
        p.dma("sync", (lambda s=slot, b=brow, occ=oc: lambda e: e.dma_start(
            out=bias_slot.ap()[0:1, s * 512:(s + 1) * 512],
            in_=b[0:1, occ * 512:(occ + 1) * 512]))(),
            writes=[bias_slot_b[slot]])

    def proj_pass(W, brow, evac):
        for oc in range(4):
            slot = oc % 2
            load_wquarter(slot, W, oc)
            load_biasq(oc % 4, brow, oc)
            for j in range(NTT):
                psn = "B" if j % 2 == 0 else "C"
                for kc in range(16):
                    t_mm(psn, (0, 512),
                         xT.ap()[:, kc, j * 128:(j + 1) * 128],
                         wqk_buf.ap()[:, slot, kc, :],
                         kc == 0, False, [xT_bt[j], wqk_b[slot]])
                p.op("tensor", (lambda o=ps[psn].ap()[:, 0:512], s=oc % 4:
                                lambda e: e.matmul(
                                    o, ones[0:1, 0:128],
                                    bias_slot.ap()[0:1, s * 512:(s + 1) * 512],
                                    start=False, stop=True))(),
                     reads=[bias_slot_b[oc % 4], CB["c_ones"]],
                     writes=[pbuf[psn]])
                evac(psn, j, oc)

    kh, kh_b = A.alloc("kh", [128, NTT, E], BF16, nbufs=NTT)
    escr, (escr_b,) = A.alloc("escr", [128, 2, 512], F32, nbufs=1)
    escr2, (escr2_b,) = A.alloc("escr2", [128, 2, 512], F32, nbufs=1)

    def evac_qk(psn, j, oc):
        sl = j % 2
        p.op("vector", (lambda s=ps[psn].ap()[:, 0:512], q=sl:
                        lambda e: e.tensor_scalar(
                            escr.ap()[:, q, :], s, 0.0, None, ALU.min))(),
             reads=[pbuf[psn]], writes=[escr_b])
        p.op("scalar", (lambda q=sl: lambda e: e.activation(
            escr2.ap()[:, q, :], escr.ap()[:, q, :], AF.Exp))(),
             reads=[escr_b], writes=[escr2_b])
        p.op("vector", (lambda o=kh.ap()[:, j, oc * 512:(oc + 1) * 512],
                        s=ps[psn].ap()[:, 0:512], q=sl:
                        lambda e: e.scalar_tensor_tensor(
                            o, escr2.ap()[:, q, :], -1.0, s,
                            ALU.add, ALU.max))(),
             reads=[escr2_b, pbuf[psn]], writes=[kh_b[j]])

    proj_pass(Wqk, din["bqk"], evac_qk)
    A.free("escr", "escr2")
    vt, vt_b = A.alloc("vt", [128, NTT, E], BF16, nbufs=NTT)

    def evac_v(psn, j, oc):
        p.op("scalar", (lambda o=vt.ap()[:, j, oc * 512:(oc + 1) * 512],
                        s=ps[psn].ap()[:, 0:512]:
                        lambda e: e.activation(o, s, AF.Silu))(),
             reads=[pbuf[psn]], writes=[vt_b[j]])

    proj_pass(Wv, din["bv"], evac_v)

    kvsb, (kvsb_b,) = A.alloc("kvsb", [128, E], F32)
    kmsk, (kmsk_b,) = A.alloc("kmsk", [128, E], F32)
    for hg in range(4):
        psn = PSN4[hg]
        for hh in range(4):
            h = hg * 4 + hh
            for j in range(NTT):
                t_mm(psn, (hh * 128, hh * 128 + 128),
                     kh.ap()[:, j, h * 128:(h + 1) * 128],
                     vt.ap()[:, j, h * 128:(h + 1) * 128],
                     j == 0, j == NTT - 1, [kh_b[j], vt_b[j]])
        p.op("vector", (lambda o=kvsb.ap()[:, hg * 512:(hg + 1) * 512],
                        s=ps[psn].ap(): lambda e: e.tensor_copy(o, s))(),
             reads=[pbuf[psn]], writes=[kvsb_b])
    p.op("vector", lambda e: e.tensor_scalar(kmsk.ap(), kvsb.ap(),
                                             selb[:, 0:1], None, ALU.mult),
         reads=[kvsb_b, CB["c_selb"]], writes=[kmsk_b])
    p.dma("sync", lambda e: e.dma_start(out=kv_in[:, 0:E], in_=kmsk.ap()),
          reads=[kmsk_b], stream="kvw")
    p.op("vector", lambda e: e.tensor_scalar(kmsk.ap(), kvsb.ap(),
                                             selb[:, 1:2], None, ALU.mult),
         reads=[kvsb_b, CB["c_selb"]], writes=[kmsk_b])
    p.dma("sync", lambda e: e.dma_start(out=kv_in[:, E:2 * E], in_=kmsk.ap()),
          reads=[kmsk_b], stream="kvw")
    kv_out_b = Buf("kv_out")
    p.coll(lambda e: e.collective_compute(
        "AllReduce", ALU.add, replica_groups=[list(range(NC))],
        ins=[kv_in.opt()], outs=[kv_out.opt()]),
        writes=[kv_out_b], deps=[p.stream_dep("kvw")])
    A.free("kvsb", "kmsk")

    A.free("vt")
    khT, khT_bt = A.alloc("khT", [128, 16, TPC], BF16, nbufs=NTT)
    for j in range(NTT):
        pe_transpose_tile(khT.ap(), khT_bt[j], kh.ap()[:, j, :],
                          [kh_b[j]], j)

    A.free("kh")
    usl, usl_b = A.alloc("usl", [128, 2, E], F32, nbufs=2)

    def evac_u(psn, j, oc):
        sl = j % 2
        p.op("scalar", (lambda o=usl.ap()[:, sl, 0:512],
                        s=ps[psn].ap()[:, 0:512]:
                        lambda e: e.activation(o, s, AF.Silu))(),
             reads=[pbuf[psn]], writes=[usl_b[sl]])
        p.dma("sync", (lambda s=sl, jj=j, occ=oc: lambda e: e.dma_start(
            out=u_dram[jj * 128:(jj + 1) * 128, occ * 512:(occ + 1) * 512],
            in_=usl.ap()[:, s, 0:512]))(),
            reads=[usl_b[sl]], stream="uw")

    proj_pass(Wu, din["bu"], evac_u)

    dbg_dram("u", u_dram, [TPC, E], F32, "uw")
    kvf, (kvf_b,) = A.alloc("kvf", [128, E], F32)
    kvlo, (kvlo_b,) = A.alloc("kvlo", [128, E], F32)
    kvbf, (kvbf_b,) = A.alloc("kvbf", [128, E], BF16)
    p.dma("sync", lambda e: e.dma_start(out=kvlo.ap(), in_=kv_out[:, 0:E]),
          reads=[kv_out_b], writes=[kvlo_b])
    p.dma("sync", lambda e: e.dma_start(out=kvf.ap(), in_=kv_out[:, E:2 * E]),
          reads=[kv_out_b], writes=[kvf_b])
    p.op("vector", lambda e: e.tensor_scalar(kvlo.ap(), kvlo.ap(),
                                             selb[:, 0:1], None, ALU.mult),
         reads=[kvlo_b, CB["c_selb"]], writes=[kvlo_b])
    p.op("vector", lambda e: e.tensor_scalar(kvf.ap(), kvf.ap(),
                                             selb[:, 1:2], None, ALU.mult),
         reads=[kvf_b, CB["c_selb"]], writes=[kvf_b])
    p.op("vector", lambda e: e.tensor_add(kvf.ap(), kvf.ap(), kvlo.ap()),
         reads=[kvf_b, kvlo_b], writes=[kvf_b])
    p.op("vector", lambda e: e.tensor_copy(kvbf.ap(), kvf.ap()),
         reads=[kvf_b], writes=[kvbf_b])
    dbg_store("kv", [128, E], kvf.ap(), [kvf_b])
    A.free("kvlo", "kvf")

    A.free("xT")
    zT, zT_bt = A.alloc("zT", [128, 16, TPC], BF16, nbufs=NTT)
    lnsc, (lnsc_b,) = A.alloc("lnsc", [128, E], F32)
    zsl, zsl_b = A.alloc("zsl", [128, 2, E], BF16, nbufs=2)
    g_bh, g_bb = bcast_load("g_b", ln_g, F32)
    b_bh, b_bb = bcast_load("b_b", ln_b, F32)
    g_b = g_bh.ap(); b_b = b_bh.ap()
    uw_dep = p.stream_dep("uw")
    for j in range(NTT):
        sl = j % 2
        for hg in range(4):
            psn = PSN4[hg]
            for hh in range(4):
                h = hg * 4 + hh
                t_mm(psn, (hh * 128, hh * 128 + 128),
                     khT.ap()[:, h, j * 128:(j + 1) * 128],
                     kvbf.ap()[:, h * 128:(h + 1) * 128],
                     True, True, [khT_bt[j], kvbf_b])
            p.op("scalar", (lambda o=lnsc.ap()[:, hg * 512:(hg + 1) * 512],
                            s=ps[psn].ap(): lambda e: e.copy(o, s))(),
                 reads=[pbuf[psn]], writes=[lnsc_b])
        st, sb = ln_stats(lnsc.ap(), [lnsc_b], LN_EPS, sl)
        p.op("vector", (lambda st=st: lambda e: e.tensor_scalar(
            lnsc.ap(), lnsc.ap(), st["rstd"], st["t2"], ALU.mult,
            ALU.subtract))(),
             reads=[lnsc_b, sb], writes=[lnsc_b])
        p.op("gpsimd", lambda e: e.tensor_mul(lnsc.ap(), lnsc.ap(), g_b),
             reads=[g_bb], writes=[lnsc_b])
        p.op("gpsimd", lambda e: e.tensor_add(lnsc.ap(), lnsc.ap(), b_b),
             reads=[b_bb], writes=[lnsc_b])
        p.dma("sync", (lambda s=sl, jj=j: lambda e: e.dma_start(
            out=usl.ap()[:, s, :], in_=u_dram[jj * 128:(jj + 1) * 128, :]))(),
            writes=[usl_b[sl]], deps=[uw_dep])
        p.op("vector", (lambda s=sl: lambda e: e.tensor_mul(
            zsl.ap()[:, s, :], lnsc.ap(), usl.ap()[:, s, :]))(),
             reads=[lnsc_b, usl_b[sl]], writes=[zsl_b[sl]])
        pe_transpose_tile(zT.ap(), zT_bt[j], zsl.ap()[:, sl, :],
                          [zsl_b[sl]], j)
    A.free("usl", "lnsc", "kvbf", "g_b", "b_b")

    A.free("khT", "zsl")
    y2sl, y2sl_b = A.alloc("y2sl", [128, 2, 512], F32, nbufs=2)

    def evac_y2(psn, j, oc):
        sl = j % 2
        p.op("vector", (lambda o=y2sl.ap()[:, sl, :],
                        s=ps[psn].ap()[:, 0:512]:
                        lambda e: e.tensor_copy(o, s))(),
             reads=[pbuf[psn]], writes=[y2sl_b[sl]])
        p.dma("sync", (lambda s=sl, jj=j, occ=oc: lambda e: e.dma_start(
            out=y2_dram[jj * 128:(jj + 1) * 128, occ * 512:(occ + 1) * 512],
            in_=y2sl.ap()[:, s, :]))(),
            reads=[y2sl_b[sl]], stream="y2w")

    for oc in range(4):
        slot = oc % 2
        load_wquarter(slot, Wo, oc)
        load_biasq(oc % 4, din["bo"], oc)
        for j in range(NTT):
            psn = "B" if j % 2 == 0 else "C"
            for kc in range(16):
                t_mm(psn, (0, 512),
                     zT.ap()[:, kc, j * 128:(j + 1) * 128],
                     wqk_buf.ap()[:, slot, kc, :],
                     kc == 0, False, [zT_bt[j], wqk_b[slot]])
            p.op("tensor", (lambda o=ps[psn].ap()[:, 0:512], s=oc % 4:
                            lambda e: e.matmul(
                                o, ones[0:1, 0:128],
                                bias_slot.ap()[0:1, s * 512:(s + 1) * 512],
                                start=False, stop=True))(),
                 reads=[bias_slot_b[oc % 4], CB["c_ones"]], writes=[pbuf[psn]])
            evac_y2(psn, j, oc)

    dbg_dram("y2", y2_dram, [TPC, E], F32, "y2w")
    A.free("zT", "wqk", "y2sl")
    hln, hln_b = A.alloc("hln", [128, NTT, SELW], BF16, nbufs=NTT)
    xin2, xin2_b = A.alloc("xin2", [128, 2, E], F32, nbufs=2)
    lnsc2, lnsc2_bs = A.alloc("lnsc2", [128, 2, E], F32, nbufs=2)
    xr, xr_b = A.alloc("xr", [128, 2, E], F32, nbufs=2)
    posex_b = hln_b[0]
    p.dma("gpsimd", lambda e: e.dma_start(
        out=hln.ap()[:, :, 2048:2050], in_=din["c_posex"]),
        writes=list(hln_b))
    p.op("vector", lambda e: e.memset(hln.ap()[:, :, 2051:SELW], 0.0),
         writes=list(hln_b))
    hlnpad_op = p.ops[-1]
    wr_bh, wr_bb = bcast_load("wr_b", w_router, F32)
    wr_b = wr_bh.ap()
    p.op("vector", lambda e: e.reduce_sum(sbt["wsum"].ap(), wr_b, axis=AX.X),
         reads=[wr_bb], writes=[stat_b])
    y2_dep = p.stream_dep("y2w")
    for j in range(NTT):
        sl = j % 2
        p.dma("sync", (lambda s=sl, jj=j: lambda e: e.dma_start(
            out=xin2.ap()[:, s, :], in_=x_in[jj * 128:(jj + 1) * 128, :]))(),
            writes=[xin2_b[sl]])
        p.dma("scalar", (lambda s=sl, jj=j: lambda e: e.dma_start(
            out=xr.ap()[:, s, :], in_=y2_dram[jj * 128:(jj + 1) * 128, :]))(),
            writes=[xr_b[sl]], deps=[y2_dep])
        p.op("vector", (lambda s=sl: lambda e: e.tensor_add(
            xr.ap()[:, s, :], xr.ap()[:, s, :], xin2.ap()[:, s, :]))(),
             reads=[xr_b[sl], xin2_b[sl]], writes=[xr_b[sl]])
        st, sb = ln_stats(xr.ap()[:, sl, :], [xr_b[sl]], LN_EPS, sl)
        p.op("vector", (lambda s=sl, jj=j, st=st: lambda e: e.tensor_scalar(
            hln.ap()[:, jj, 0:E], xr.ap()[:, s, :], st["rstd"], st["t2"],
            ALU.mult, ALU.subtract))(),
             reads=[xr_b[sl], sb], writes=[hln_b[j]])
        p.op("gpsimd", (lambda s=sl, o=lnsc2.ap()[:, sl, :]:
                        lambda e: e.tensor_mul(o, xr.ap()[:, s, :], wr_b))(),
             reads=[xr_b[sl], wr_bb], writes=[lnsc2_bs[sl]])
        p.op("vector", (lambda st=st, i=lnsc2.ap()[:, sl, :]:
                        lambda e: e.reduce_sum(st["t1"], i, axis=AX.X))(),
             reads=[lnsc2_bs[sl]], writes=[sb])
        p.op("vector", (lambda st=st: lambda e: e.tensor_mul(
            st["t3"], st["mu"], sbt["wsum"].ap()))(), reads=[sb], writes=[sb])
        p.op("vector", (lambda st=st: lambda e: e.tensor_sub(
            st["t1"], st["t1"], st["t3"]))(), reads=[sb], writes=[sb])
        p.op("vector", (lambda jj=j, st=st: lambda e: e.tensor_mul(
            scsb[:, jj:jj + 1], st["t1"], st["rstd"]))(),
             reads=[sb], writes=[sc_b])
        p.op("vector", (lambda jj=j: lambda e: e.tensor_copy(
            hln.ap()[:, jj, 2050:2051], scsb[:, jj:jj + 1]))(),
             reads=[sc_b], writes=[hln_b[j]])
        p.dma("sync", (lambda s=sl, jj=j: lambda e: e.dma_start(
            out=xres_dram[jj * 128:(jj + 1) * 128, :], in_=xr.ap()[:, s, :]))(),
            reads=[xr_b[sl]], stream="xrw")
        p.dma("scalar", (lambda jj=j: lambda e: e.dma_start(
            out=hln_dram[jj * 128:(jj + 1) * 128, :],
            in_=hln.ap()[:, jj, 0:E]))(),
            reads=[hln_b[j]], stream="hlw")

    dbg_dram("xres", xres_dram, [TPC, E], F32, "xrw")
    dbg_dram("hlnd", hln_dram, [TPC, E], BF16, "hlw")
    scT, (scT_b,) = A.alloc("scT", [NTT, 128], F32)
    t_transpose("S", (0, NTT), (0, 128), scsb, identf, [sc_b], CB["c_identf"])
    p.op("vector", lambda e: e.tensor_copy(scT.ap(),
                                           ps["S"].ap()[0:NTT, 0:128]),
         reads=[pbuf["S"]], writes=[scT_b])
    p.dma("sync", lambda e: e.dma_start(out=sc_in, in_=scT.ap()),
          reads=[scT_b], stream="scw")
    scout_b = Buf("sc_out")
    p.coll(lambda e: e.collective_compute(
        "AllGather", ALU.bypass, replica_groups=[list(range(NC))],
        ins=[sc_in.opt()], outs=[sc_out.opt()]),
        writes=[scout_b], deps=[p.stream_dep("scw")])

    dbg_dram("scout", sc_out, [NC * NTT, 128], F32, None, reads=[scout_b])
    scb, (scb_b,) = A.alloc("scb", [128, T], F32)
    p.dma("sync", lambda e: e.dma_start(
        out=scb.ap(), in_=sc_out.rearrange("(o a) b -> o (a b)", o=1).to_broadcast(
            [128, T])), reads=[scout_b], writes=[scb_b])
    stab64, (stab64_b,) = A.alloc("stab64", [64, 128], F32)
    p.dma("scalar", lambda e: e.dma_start(out=stab64.ap(), in_=sc_out),
          reads=[scout_b], writes=[stab64_b])
    tab, (tab_b,) = A.alloc("tab", [128, 64], F32)
    t_transpose("S", (0, 128), (0, 64), stab64.ap(), identf[0:64, 0:64], [stab64_b], CB["c_identf"])
    p.op("vector", lambda e: e.tensor_copy(tab.ap(), ps["S"].ap()[:, 0:64]),
         reads=[pbuf["S"]], writes=[tab_b])

    gs, (gs_b,) = A.alloc("gs", [1, 16], F32)
    gsl = gs.ap()
    junk, (junk_b,) = A.alloc("junk", [128, 4096], BF16)
    cnt2, (cnt2_b,) = A.alloc("cnt2", [128, 2], F32)
    tcol, (tcol_b,) = A.alloc("tcol", [128, 4], F32)
    sq64, (sq64_b,) = A.alloc("sq64", [128, 64], F32)

    p.op("scalar", lambda e: e.activation(sq64.ap(), tab.ap(), AF.Square),
         reads=[tab_b], writes=[sq64_b])
    t_mm("S", (0, 64), onecol, sq64.ap(), True, True,
         [sq64_b, CB["c_onecol"]], rows=(0, 1))
    p.op("vector", lambda e: e.tensor_copy(sq64.ap()[0:1, 0:64],
                                           ps["S"].ap()[0:1, 0:64]),
         reads=[pbuf["S"]], writes=[sq64_b])
    p.op("vector", lambda e: e.reduce_sum(
        gsl[0:1, 8:10], sq64.ap()[0:1, 0:64].rearrange("a (b c) -> a b c", b=2),
        axis=AX.X), reads=[sq64_b], writes=[gs_b])
    p.op("scalar", lambda e: e.activation(gsl[0:1, 2:4], gsl[0:1, 8:10],
                                          AF.Sqrt), reads=[gs_b], writes=[gs_b])
    p.op("vector", lambda e: e.tensor_scalar_add(gsl[0:1, 2:4],
                                                 gsl[0:1, 2:4], 1.0),
         reads=[gs_b], writes=[gs_b])
    p.op("vector", lambda e: e.tensor_scalar_mul(gsl[0:1, 0:2],
                                                 gsl[0:1, 2:4], -1.0),
         reads=[gs_b], writes=[gs_b])
    p.op("vector", lambda e: e.tensor_scalar_mul(gsl[0:1, 2:4],
                                                 gsl[0:1, 2:4], 2.0 / 129.0),
         reads=[gs_b], writes=[gs_b])

    iota = consts["c_iota"].ap()
    for rnd in range(4):
        t_mm("S", (0, 4), ones[0:1, 0:128], gsl[0:1, 0:4], True, True,
             [gs_b, CB["c_ones"]])
        p.op("vector", lambda e: e.tensor_copy(tcol.ap(),
                                               ps["S"].ap()[:, 0:4]),
             reads=[pbuf["S"]], writes=[tcol_b])
        p.op("vector", lambda e: e.scalar_tensor_tensor(
            tcol.ap()[:, 0:1], tcol.ap()[:, 2:3], iota[:, 0:1],
            tcol.ap()[:, 0:1], ALU.mult, ALU.add),
             reads=[tcol_b, CB["c_iota"]], writes=[tcol_b])
        p.op("vector", lambda e: e.scalar_tensor_tensor(
            tcol.ap()[:, 1:2], tcol.ap()[:, 3:4], iota[:, 0:1],
            tcol.ap()[:, 1:2], ALU.mult, ALU.add),
             reads=[tcol_b, CB["c_iota"]], writes=[tcol_b])
        for bb in range(2):
            p.op("vector", (lambda b2=bb: lambda e: e.tensor_scalar(
                junk.ap(), scb.ap()[:, b2 * 4096:(b2 + 1) * 4096],
                tcol.ap()[:, b2:b2 + 1], None, ALU.is_gt))(),
                 reads=[scb_b, tcol_b], writes=[junk_b])
            p.op("vector", (lambda b2=bb: lambda e: e.reduce_sum(
                cnt2.ap()[:, b2:b2 + 1], junk.ap(), axis=AX.X))(),
                 reads=[junk_b], writes=[cnt2_b])
        p.op("vector", lambda e: e.tensor_scalar(
            cnt2.ap(), cnt2.ap(), float(KSEL), None, ALU.is_ge),
             reads=[cnt2_b], writes=[cnt2_b])
        t_mm("S", (0, 2), onecol, cnt2.ap(), True, True,
             [cnt2_b, CB["c_onecol"]], rows=(0, 1))
        p.op("vector", lambda e: e.tensor_copy(gsl[0:1, 4:6],
                                               ps["S"].ap()[0:1, 0:2]),
             reads=[pbuf["S"]], writes=[gs_b])
        p.op("vector", lambda e: e.tensor_mul(gsl[0:1, 6:8], gsl[0:1, 4:6],
                                              gsl[0:1, 2:4]),
             reads=[gs_b], writes=[gs_b])
        p.op("vector", lambda e: e.tensor_add(gsl[0:1, 0:2], gsl[0:1, 0:2],
                                              gsl[0:1, 6:8]),
             reads=[gs_b], writes=[gs_b])
        p.op("vector", lambda e: e.tensor_scalar_mul(gsl[0:1, 2:4],
                                                     gsl[0:1, 2:4],
                                                     1.0 / 129.0),
             reads=[gs_b], writes=[gs_b])

    t_mm("S", (0, 2), ones[0:1, 0:128], gsl[0:1, 0:2], True, True,
         [gs_b, CB["c_ones"]])
    p.op("vector", lambda e: e.tensor_copy(tcol.ap()[:, 0:2],
                                           ps["S"].ap()[:, 0:2]),
         reads=[pbuf["S"]], writes=[tcol_b])
    p.op("vector", lambda e: e.tensor_mul(tcol.ap()[:, 2:4],
                                          tcol.ap()[:, 0:2], selb),
         reads=[tcol_b, CB["c_selb"]], writes=[tcol_b])
    p.op("vector", lambda e: e.reduce_sum(
        st_t1, tcol.ap()[:, 2:4], axis=AX.X), reads=[tcol_b], writes=[stat_b])

    mloc, (mloc_b,) = A.alloc("mloc", [128, NTT], F32)
    pexc, (pexc_b,) = A.alloc("pexc", [128, NTT], F32)
    lrow, (lrow_b,) = A.alloc("lrow", [1, 4 * NTT], F32)
    sidx, (sidx_b,) = A.alloc("sidx", [128, NTT], F32)
    sidx_i, (sidxi_b,) = A.alloc("sidxi", [128, NTT], I32)
    p.op("vector", lambda e: e.tensor_scalar(mloc.ap(), scsb, st_t1, None,
                                             ALU.is_gt),
         reads=[sc_b, stat_b], writes=[mloc_b])
    t_mm("S", (0, NTT), consts["c_sut"].ap(), mloc.ap(), True, True,
         [mloc_b, CB["c_sut"]])
    p.op("vector", lambda e: e.tensor_copy(pexc.ap(),
                                           ps["S"].ap()[:, 0:NTT]),
         reads=[pbuf["S"]], writes=[pexc_b])
    t_mm("S", (64, 64 + NTT), onecol, mloc.ap(), True, True,
         [mloc_b, CB["c_onecol"]], rows=(0, 1))
    p.op("vector", lambda e: e.tensor_copy(lrow.ap()[0:1, 0:NTT],
                                           ps["S"].ap()[0:1, 64:64 + NTT]),
         reads=[pbuf["S"]], writes=[lrow_b])
    p.op("vector", lambda e: e.tensor_copy(lrow.ap()[0:1, 8:16],
                                           lrow.ap()[0:1, 0:8]),
         reads=[lrow_b], writes=[lrow_b])
    p.op("vector", lambda e: e.tensor_add(lrow.ap()[0:1, 9:16],
                                          lrow.ap()[0:1, 9:16],
                                          lrow.ap()[0:1, 0:7]),
         reads=[lrow_b], writes=[lrow_b])
    p.op("vector", lambda e: e.tensor_copy(lrow.ap()[0:1, 16:24],
                                           lrow.ap()[0:1, 8:16]),
         reads=[lrow_b], writes=[lrow_b])
    p.op("vector", lambda e: e.tensor_add(lrow.ap()[0:1, 18:24],
                                          lrow.ap()[0:1, 18:24],
                                          lrow.ap()[0:1, 8:14]),
         reads=[lrow_b], writes=[lrow_b])
    p.op("vector", lambda e: e.tensor_copy(lrow.ap()[0:1, 24:32],
                                           lrow.ap()[0:1, 16:24]),
         reads=[lrow_b], writes=[lrow_b])
    p.op("vector", lambda e: e.tensor_add(lrow.ap()[0:1, 28:32],
                                          lrow.ap()[0:1, 28:32],
                                          lrow.ap()[0:1, 16:20]),
         reads=[lrow_b], writes=[lrow_b])
    p.op("vector", lambda e: e.tensor_sub(lrow.ap()[0:1, 24:32],
                                          lrow.ap()[0:1, 24:32],
                                          lrow.ap()[0:1, 0:8]),
         reads=[lrow_b], writes=[lrow_b])
    gmask = sq64
    gmask_b = sq64_b
    for bb in range(2):
        p.op("vector", (lambda b2=bb: lambda e: e.tensor_scalar(
            gmask.ap()[:, b2 * 32:(b2 + 1) * 32],
            tab.ap()[:, b2 * 32:(b2 + 1) * 32],
            tcol.ap()[:, b2:b2 + 1], None, ALU.is_gt))(),
             reads=[tab_b, tcol_b], writes=[gmask_b])
    t_mm("S", (0, 64), onecol, gmask.ap(), True, True,
         [gmask_b, CB["c_onecol"]], rows=(0, 1))
    p.op("vector", lambda e: e.tensor_copy(gmask.ap()[0:1, 0:64],
                                           ps["S"].ap()[0:1, 0:64]),
         reads=[pbuf["S"]], writes=[gmask_b])
    p.op("vector", lambda e: e.tensor_mul(gmask.ap()[0:1, 0:64],
                                          gmask.ap()[0:1, 0:64],
                                          consts["c_beforem"].ap()),
         reads=[gmask_b, CB["c_beforem"]], writes=[gmask_b])
    p.op("vector", lambda e: e.reduce_sum(lrow.ap()[0:1, 8:9],
                                          gmask.ap()[0:1, 0:64], axis=AX.X),
         reads=[gmask_b], writes=[lrow_b])
    p.op("vector", lambda e: e.tensor_scalar(lrow.ap()[0:1, 24:32],
                                             lrow.ap()[0:1, 24:32],
                                             lrow.ap()[0:1, 8:9], None,
                                             ALU.add),
         reads=[lrow_b], writes=[lrow_b])
    t_mm("S", (0, NTT), ones[0:1, 0:128], lrow.ap()[0:1, 24:32], True, True,
         [lrow_b, CB["c_ones"]])
    p.op("vector", lambda e: e.tensor_add(sidx.ap(), pexc.ap(),
                                          ps["S"].ap()[:, 0:NTT]),
         reads=[pexc_b, pbuf["S"]], writes=[sidx_b])
    p.op("vector", lambda e: e.tensor_scalar(pexc.ap(), sidx.ap(),
                                             float(KSEL), None, ALU.is_lt),
         reads=[sidx_b], writes=[pexc_b])
    p.op("vector", lambda e: e.tensor_mul(mloc.ap(), mloc.ap(), pexc.ap()),
         reads=[mloc_b, pexc_b], writes=[mloc_b])
    p.op("vector", lambda e: e.tensor_scalar(sidx.ap(), sidx.ap(),
                                             consts["c_b512"].ap()[:, 0:1],
                                             None, ALU.add),
         reads=[sidx_b, CB["c_b512"]], writes=[sidx_b])
    p.op("vector", lambda e: e.tensor_scalar(pexc.ap(), mloc.ap(), -BIG, BIG,
                                             ALU.mult, ALU.add),
         reads=[mloc_b], writes=[pexc_b])
    p.op("vector", lambda e: e.tensor_add(sidx.ap(), sidx.ap(), pexc.ap()),
         reads=[sidx_b, pexc_b], writes=[sidx_b])
    p.op("vector", lambda e: e.tensor_copy(sidx_i.ap(), sidx.ap()),
         reads=[sidx_b], writes=[sidxi_b])
    dbg_store("sidx", [128, NTT], sidx.ap(), [sidx_b])

    selz_dep = p.stream_dep("selz")
    for j in range(NTT):
        p.dma("gpsimd", (lambda jj=j: lambda e: e.indirect_dma_start(
            out=sel_in, out_offset=bass.IndirectOffsetOnAxis(
                sidx_i.ap()[:, jj:jj + 1], 0),
            in_=hln.ap()[:, jj, :], in_offset=None,
            bounds_check=S - 1, oob_is_err=False))(),
            reads=[hln_b[j], sidxi_b], stream="selw",
            deps=[selz_dep])
    selout_b = Buf("sel_out")
    p.coll(lambda e: e.collective_compute(
        "AllReduce", ALU.add, replica_groups=[list(range(NC))],
        ins=[sel_in.opt()], outs=[sel_out.opt()]),
        writes=[selout_b], deps=[p.stream_dep("selw")])

    dbg_dram("selout", sel_out, [S, SELW], BF16, None, reads=[selout_b])
    A.free("scb", "junk", "stab64", "sq64", "xin2", "xr", "lnsc2", "wr_b",
           "sq")
    selr, selr_b = A.alloc("selr", [128, NTT, SELW], BF16, nbufs=NTT)
    p.dma("sync", lambda e: e.dma_start(
        out=selr.ap(), in_=sel_out.rearrange("(j p) c -> p j c", p=128)),
        reads=[selout_b], writes=list(selr_b))
    gates, (gates_b,) = A.alloc("gates", [128, NTT, 1], F32)
    p.op("scalar", lambda e: e.activation(gates.ap(), selr.ap()[:, :, 2050:2051],
                                          AF.Sigmoid),
         reads=list(selr_b), writes=[gates_b])
    posf, (posf_b,) = A.alloc("posf", [128, NTT], F32)
    pos2, (pos2_b,) = A.alloc("pos2", [128, NTT], F32)
    p.op("vector", lambda e: e.tensor_copy(
        posf.ap().rearrange("p (a o) -> p a o", o=1), selr.ap()[:, :, 2048:2049]),
         reads=list(selr_b), writes=[posf_b])
    p.op("vector", lambda e: e.tensor_copy(
        pos2.ap().rearrange("p (a o) -> p a o", o=1), selr.ap()[:, :, 2049:2050]),
         reads=list(selr_b), writes=[pos2_b])
    p.op("vector", lambda e: e.scalar_tensor_tensor(
        posf.ap(), posf.ap(), 64.0, pos2.ap(), ALU.mult, ALU.add),
         reads=[posf_b, pos2_b], writes=[posf_b])
    t_transpose("S", (0, NTT), (0, 128), posf.ap(), identf, [posf_b],
                CB["c_identf"])
    p.op("vector", lambda e: e.tensor_copy(scT.ap(),
                                           ps["S"].ap()[0:NTT, 0:128]),
         reads=[pbuf["S"]], writes=[scT_b])
    p.dma("sync", lambda e: e.dma_start(out=pos_dram, in_=scT.ap()),
          reads=[scT_b], stream="posw")
    posb, (posb_b,) = A.alloc("posb", [128, S], F32)
    p.dma("sync", lambda e: e.dma_start(
        out=posb.ap(), in_=pos_dram.rearrange("(o a) b -> o (a b)", o=1).to_broadcast(
            [128, S])), writes=[posb_b], deps=[p.stream_dep("posw")])
    cosf, (cosf_b,) = A.alloc("cosf", [128, S], F32)
    sinf, (sinf_b,) = A.alloc("sinf", [128, S], F32)
    acyc, (acyc_b,) = A.alloc("acyc", [128, S], F32)
    rr, (rr_b,) = A.alloc("rr", [128, S], F32)
    p.op("vector", lambda e: e.tensor_scalar(
        acyc.ap(), posb.ap(), consts["c_invf"].ap()[:, 0:1], None, ALU.mult),
         reads=[posb_b, CB["c_invf"]], writes=[acyc_b])
    p.op("vector", lambda e: e.tensor_scalar(rr.ap(), acyc.ap(), MAGIC, MAGIC,
                                             ALU.add, ALU.subtract),
         reads=[acyc_b], writes=[rr_b])
    p.op("vector", lambda e: e.tensor_sub(rr.ap(), acyc.ap(), rr.ap()),
         reads=[acyc_b, rr_b], writes=[rr_b])
    p.op("scalar", lambda e: e.activation(sinf.ap(), rr.ap(), AF.Sin,
                                          scale=TWOPI),
         reads=[rr_b], writes=[sinf_b])
    p.op("vector", lambda e: e.tensor_scalar_add(acyc.ap(), acyc.ap(), 0.25),
         reads=[acyc_b], writes=[acyc_b])
    p.op("vector", lambda e: e.tensor_scalar(rr.ap(), acyc.ap(), MAGIC, MAGIC,
                                             ALU.add, ALU.subtract),
         reads=[acyc_b], writes=[rr_b])
    p.op("vector", lambda e: e.tensor_sub(rr.ap(), acyc.ap(), rr.ap()),
         reads=[acyc_b, rr_b], writes=[rr_b])
    p.op("scalar", lambda e: e.activation(cosf.ap(), rr.ap(), AF.Sin,
                                          scale=TWOPI),
         reads=[rr_b], writes=[cosf_b])
    A.free("acyc", "rr", "posb", "posf", "pos2")

    hrow, hrow_b = A.alloc("hrow", [128, 2, E], BF16, nbufs=2)
    hT, hT_bt = A.alloc("hT", [128, 16, S], BF16, nbufs=NTT)
    _sqfa, (_sqfa_b,) = A.alloc("sqfa", [128, E], F32)
    _sqfb, (_sqfb_b,) = A.alloc("sqfb", [128, E], F32)
    sqf_aps = [_sqfa.ap(), _sqfb.ap()]
    sqf_bs = [_sqfa_b, _sqfb_b]
    ln1bh, ln1bb = bcast_load("ln1_b", ln1w, BF16)
    ln1b = ln1bh.ap()

    def rms_stats(src_ap, src_bufs, sl=0):
        st = ST(sl); sb = stat_bs[sl]
        sqv = sqf_aps[sl]; sq_b = sqf_bs[sl]
        p.op("scalar", (lambda s=src_ap, o=sqv: lambda e: e.activation(
            o, s, AF.Square))(), reads=list(src_bufs), writes=[sq_b])
        p.op("vector", (lambda st=st, i=sqv: lambda e: e.reduce_sum(
            st["t1"], i, axis=AX.X))(), reads=[sq_b], writes=[sb])
        p.op("vector", (lambda st=st: lambda e: e.tensor_scalar(
            st["t1"], st["t1"], 1.0 / E, RMS_EPS, ALU.mult, ALU.add))(),
             reads=[sb], writes=[sb])
        p.op("scalar", (lambda st=st: lambda e: e.activation(
            st["t2"], st["t1"], AF.Sqrt))(), reads=[sb], writes=[sb])
        p.op("vector", (lambda st=st: lambda e: e.reciprocal(
            st["rstd"], st["t2"]))(), reads=[sb], writes=[sb])
        return st, sb

    for j in range(NTT):
        sl = j % 2
        st, sb = rms_stats(selr.ap()[:, j, 0:E], [selr_b[j]], sl)
        p.op("vector", (lambda s=sl, jj=j, st=st: lambda e: e.tensor_scalar(
            hrow.ap()[:, s, :], selr.ap()[:, jj, 0:E], st["rstd"], None,
            ALU.mult))(),
             reads=[selr_b[j], sb], writes=[hrow_b[sl]])
        p.op("gpsimd", (lambda s=sl: lambda e: e.tensor_mul(
            hrow.ap()[:, s, :], hrow.ap()[:, s, :], ln1b))(),
             reads=[ln1bb], writes=[hrow_b[sl]])
        pe_transpose_tile(hT.ap(), hT_bt[j], hrow.ap()[:, sl, :],
                          [hrow_b[sl]], j)
    A.free("ln1_b", "hrow")

    wq_sb, (wq_b,) = A.alloc("wq_sb", [128, 16, 256], BF16)
    wk_sb, (wk_b,) = A.alloc("wk_sb", [128, 16, 128], BF16)
    wvs_sb, (wvs_b,) = A.alloc("wvs_sb", [128, 16, 128], BF16)
    for hh, W, bobj in ((wq_sb, wq, wq_b), (wk_sb, wk, wk_b),
                        (wvs_sb, wv2, wvs_b)):
        p.dma("gpsimd", (lambda h=hh, WW=W: lambda e: e.dma_start(
            out=h.ap(), in_=WW.rearrange("(kc p) o -> p kc o", p=128)))(),
            writes=[bobj])
    qf, (qf_b,) = A.alloc("qf", [128, 2, S], F32)
    qbf, (qbf_b,) = A.alloc("qbf", [128, 2, S], BF16)
    kkf, (kkf_b,) = A.alloc("kkf", [128, S], F32)
    kkbf, (kkbf_b,) = A.alloc("kkbf", [128, S], BF16)
    rsc1, (rsc1_b,) = A.alloc("rsc1", [128, 512], F32)
    rsc2, (rsc2_b,) = A.alloc("rsc2", [128, 512], F32)
    permT = consts["c_permT"].ap()

    def rope_fm(dst_ap, dst_buf, src_ap, src_buf):
        for th in range(2):
            cols = (th * 512, (th + 1) * 512)
            t_mm("S", (0, 512), permT, src_ap[:, cols[0]:cols[1]],
                 True, True, [src_buf, CB["c_permT"]])
            p.op("vector", (lambda c=cols: lambda e: e.tensor_mul(
                rsc1.ap(), src_ap[:, c[0]:c[1]], cosf.ap()[:, c[0]:c[1]]))(),
                 reads=[src_buf, cosf_b], writes=[rsc1_b])
            p.op("vector", (lambda c=cols: lambda e: e.tensor_mul(
                rsc2.ap(), ps["S"].ap(), sinf.ap()[:, c[0]:c[1]]))(),
                 reads=[pbuf["S"], sinf_b], writes=[rsc2_b])
            p.op("vector", (lambda c=cols: lambda e: e.tensor_add(
                dst_ap[:, c[0]:c[1]], rsc1.ap(), rsc2.ap()))(),
                 reads=[rsc1_b, rsc2_b], writes=[dst_buf])

    for oh in range(2):
        for th in range(2):
            psn = PSN4[th]
            for kc in range(16):
                t_mm(psn, (0, 512),
                     wq_sb.ap()[:, kc, oh * 128:(oh + 1) * 128],
                     hT.ap()[:, kc, th * 512:(th + 1) * 512],
                     kc == 0, False,
                     [wq_b] + hT_bt[th * 4:(th + 1) * 4])
            p.op("tensor", (lambda o=ps[psn].ap()[:, 0:512], ohh=oh:
                            lambda e: e.matmul(
                                o, consts["r_fbq"].ap()[0:1,
                                                        ohh * 128:(ohh + 1) * 128],
                                ones[0:1, 0:512], start=False, stop=True))(),
                 reads=[CB["r_fbq"], CB["c_ones"]], writes=[pbuf[psn]])
            p.op("vector", (lambda o=qf.ap()[:, oh, th * 512:(th + 1) * 512],
                            s=ps[psn].ap(): lambda e: e.tensor_copy(o, s))(),
                 reads=[pbuf[psn]], writes=[qf_b])
        rope_fm(qbf.ap()[:, oh, :], qbf_b, qf.ap()[:, oh, :], qf_b)
    for th in range(2):
        psn = PSN4[th]
        for kc in range(16):
            t_mm(psn, (0, 512), wk_sb.ap()[:, kc, :],
                 hT.ap()[:, kc, th * 512:(th + 1) * 512],
                 kc == 0, False, [wk_b] + hT_bt[th * 4:(th + 1) * 4])
        p.op("tensor", (lambda o=ps[psn].ap()[:, 0:512]:
                        lambda e: e.matmul(
                            o, consts["r_fbk"].ap()[0:1, 0:128],
                            ones[0:1, 0:512], start=False, stop=True))(),
             reads=[CB["r_fbk"], CB["c_ones"]], writes=[pbuf[psn]])
        p.op("vector", (lambda o=kkf.ap()[:, th * 512:(th + 1) * 512],
                        s=ps[psn].ap(): lambda e: e.tensor_copy(o, s))(),
             reads=[pbuf[psn]], writes=[kkf_b])
    rope_fm(kkbf.ap(), kkbf_b, kkf.ap(), kkf_b)
    vbf, vbf_b = A.alloc("vbf", [128, NTT, 128], BF16, nbufs=NTT)
    for j in range(NTT):
        psn = "B" if j % 2 == 0 else "C"
        for kc in range(16):
            t_mm(psn, (0, 128), hT.ap()[:, kc, j * 128:(j + 1) * 128],
                 wvs_sb.ap()[:, kc, :], kc == 0, False, [wvs_b, hT_bt[j]])
        p.op("tensor", (lambda o=ps[psn].ap()[:, 0:128]:
                        lambda e: e.matmul(
                            o, ones[0:1, 0:128],
                            consts["r_fbv"].ap()[0:1, 0:128],
                            start=False, stop=True))(),
             reads=[CB["r_fbv"], CB["c_ones"]], writes=[pbuf[psn]])
        p.op("vector", (lambda o=vbf.ap()[:, j, :], s=ps[psn].ap()[:, 0:128]:
                        lambda e: e.tensor_copy(o, s))(),
             reads=[pbuf[psn]], writes=[vbf_b[j]])

    A.free("wq_sb", "wk_sb", "wvs_sb", "rsc1", "rsc2", "qf", "kkf")
    wo_sb, (wo_b,) = A.alloc("wo_sb", [128, 2, E], BF16)
    p.dma("gpsimd", lambda e: e.dma_start(
        out=wo_sb.ap(), in_=wo2.rearrange("(h p) o -> p h o", p=128)),
        writes=[wo_b])
    ao, (ao_b,) = A.alloc("ao", [128, 2, S], BF16)
    pex, (pex_b,) = A.alloc("pex", [128, S], BF16)
    attT, attT_b = A.alloc("attT", [128, NTT, 128], BF16, nbufs=NTT)
    tril = consts["c_tril"].ap()
    for qh in range(2):
        for r in range(NTT):
            nk = (r + 1) * 128
            for kb in range(r + 1):
                t_mm("A", (kb * 128, kb * 128 + 128),
                     qbf.ap()[:, qh, r * 128:(r + 1) * 128],
                     kkbf.ap()[:, kb * 128:(kb + 1) * 128],
                     True, True, [qbf_b, kkbf_b])
            p.op("vector", (lambda rr2=r: lambda e: e.tensor_add(
                ps["A"].ap()[:, rr2 * 128:(rr2 + 1) * 128],
                ps["A"].ap()[:, rr2 * 128:(rr2 + 1) * 128], tril))(),
                 reads=[CB["c_tril"], pbuf["A"]], writes=[pbuf["A"]])
            p.op("vector", (lambda n=nk: lambda e: e.reduce_max(
                st_t1, ps["A"].ap()[:, 0:n], axis=AX.X))(),
                 reads=[pbuf["A"]], writes=[stat_b])
            p.op("vector", lambda e: e.tensor_scalar_mul(st_t2, st_t1, -ISQD),
                 reads=[stat_b], writes=[stat_b])
            p.op("scalar", (lambda n=nk: lambda e: e.activation(
                pex.ap()[:, 0:n], ps["A"].ap()[:, 0:n], AF.Exp,
                bias=st_t2, scale=ISQD))(),
                 reads=[pbuf["A"], stat_b], writes=[pex_b])
            p.op("vector", (lambda n=nk: lambda e: e.reduce_sum(
                st_t1, pex.ap()[:, 0:n], axis=AX.X))(),
                 reads=[pex_b], writes=[stat_b])
            p.op("vector", lambda e: e.reciprocal(st_t3, st_t1),
                 reads=[stat_b], writes=[stat_b])
            p.op("vector", (lambda n=nk: lambda e: e.tensor_scalar(
                pex.ap()[:, 0:n], pex.ap()[:, 0:n], st_t3, None, ALU.mult))(),
                 reads=[pex_b, stat_b], writes=[pex_b])
            for kb in range(r + 1):
                t_transpose("T", (0, 128),
                            ((kb % 8) * 128, (kb % 8) * 128 + 128),
                            pex.ap()[:, kb * 128:(kb + 1) * 128], identb,
                            [pex_b], CB["c_identb"])
                p.op("vector", (lambda kbb=kb: lambda e: e.tensor_copy(
                    attT.ap()[:, kbb, :],
                    ps["T"].ap()[:, (kbb % 8) * 128:(kbb % 8) * 128 + 128]))(),
                     reads=[pbuf["T"]], writes=[attT_b[kb]])
            psn = "B"
            for kb in range(r + 1):
                t_mm(psn, (0, 128), vbf.ap()[:, kb, :], attT.ap()[:, kb, :],
                     kb == 0, kb == r, [vbf_b[kb], attT_b[kb]])
            p.op("vector", (lambda o=ao.ap()[:, qh, r * 128:(r + 1) * 128],
                            s=ps[psn].ap()[:, 0:128]:
                            lambda e: e.tensor_copy(o, s))(),
                 reads=[pbuf[psn]], writes=[ao_b])

    apbf, apbf_b = A.alloc("apbf", [128, 2, E], BF16, nbufs=2)
    for j in range(NTT):
        sl = j % 2
        for ec in range(4):
            psn = PSN4[ec]
            for h in range(2):
                t_mm(psn, (0, 512), ao.ap()[:, h, j * 128:(j + 1) * 128],
                     wo_sb.ap()[:, h, ec * 512:(ec + 1) * 512],
                     h == 0, h == 1, [ao_b, wo_b])
            p.op("vector", (lambda o=apbf.ap()[:, sl, ec * 512:(ec + 1) * 512],
                            s=ps[psn].ap(): lambda e: e.tensor_copy(o, s))(),
                 reads=[pbuf[psn]], writes=[apbf_b[sl]])
        p.dma("sync", (lambda s=sl, jj=j: lambda e: e.dma_start(
            out=ar2_in[jj * 128:(jj + 1) * 128, :], in_=apbf.ap()[:, s, :]))(),
            reads=[apbf_b[sl]], stream="a2w")
    ar2out_b = Buf("ar2_out")
    p.coll(lambda e: e.collective_compute(
        "AllReduce", ALU.add, replica_groups=[list(range(NC))],
        ins=[ar2_in.opt()], outs=[ar2_out.opt()]),
        writes=[ar2out_b], deps=[p.stream_dep("a2w")])
    A.free("wo_sb", "apbf")

    dbg_dram("a2", ar2_out, [S, E], BF16, None, reads=[ar2out_b])
    A.free("hT", "qbf", "kkbf", "vbf", "ao", "pex", "attT",
           "cosf", "sinf", "hln", "tab", "gs", "cnt2", "tcol", "mloc",
           "pexc", "lrow", "sidx", "scT")
    a2r, a2r_b = A.alloc("a2r", [128, NTT, E], BF16, nbufs=NTT)
    p.dma("sync", lambda e: e.dma_start(
        out=a2r.ap(), in_=ar2_out.rearrange("(j p) c -> p j c", p=128)),
        reads=[ar2out_b], writes=list(a2r_b))
    h2T, h2T_bt = A.alloc("h2T", [128, 16, S], BF16, nbufs=NTT)
    x1bf, x1bf_b = A.alloc("x1bf", [128, NTT, E], BF16, nbufs=NTT)
    _x1fa, (_x1fa_b,) = A.alloc("x1fa", [128, E], F32)
    _x1fb, (_x1fb_b,) = A.alloc("x1fb", [128, E], F32)
    x1f_aps = [_x1fa.ap(), _x1fb.ap()]
    x1f_bs = [_x1fa_b, _x1fb_b]
    h2row, h2row_b = A.alloc("h2row", [128, 2, E], BF16, nbufs=2)
    ln2bh, ln2bb = bcast_load("ln2_b", ln2w, BF16)
    ln2b = ln2bh.ap()
    for j in range(NTT):
        sl = j % 2
        p.op("vector", (lambda jj=j, s=sl: lambda e: e.tensor_add(
            x1f_aps[s], a2r.ap()[:, jj, :], selr.ap()[:, jj, 0:E]))(),
             reads=[a2r_b[j], selr_b[j]], writes=[x1f_bs[sl]])
        p.op("vector", (lambda jj=j, s=sl: lambda e: e.tensor_copy(
            x1bf.ap()[:, jj, :], x1f_aps[s]))(),
             reads=[x1f_bs[sl]], writes=[x1bf_b[j]])
        st, sb = rms_stats(x1f_aps[sl], [x1f_bs[sl]], sl)
        p.op("vector", (lambda s=sl, st=st: lambda e: e.tensor_scalar(
            h2row.ap()[:, s, :], x1f_aps[s], st["rstd"], None,
            ALU.mult))(),
             reads=[x1f_bs[sl], sb], writes=[h2row_b[sl]])
        p.op("gpsimd", (lambda s=sl: lambda e: e.tensor_mul(
            h2row.ap()[:, s, :], h2row.ap()[:, s, :], ln2b))(),
             reads=[ln2bb], writes=[h2row_b[sl]])
        pe_transpose_tile(h2T.ap(), h2T_bt[j], h2row.ap()[:, sl, :],
                          [h2row_b[sl]], j)
    A.free("a2r", "x1fa", "x1fb", "sqfa", "sqfb", "h2row", "ln2_b")

    wg_sb, wg_b = A.alloc("wg_sb", [128, 2, 16, 128], BF16, nbufs=2)
    wu_sb, wu_b = A.alloc("wu_sb", [128, 2, 16, 128], BF16, nbufs=2)
    mbf, (mbf_b,) = A.alloc("mbf", [128, NTT, S], BF16)
    gsc, (gsc_b,) = A.alloc("gsc", [128, 512], F32)
    for oc in range(NTT):
        slot = oc % 2
        p.dma("gpsimd", (lambda s=slot, occ=oc: lambda e: e.dma_start(
            out=wg_sb.ap()[:, s, :, :],
            in_=wgate[:, occ * 128:(occ + 1) * 128].rearrange(
                "(kc p) o -> p kc o", p=128)))(), writes=[wg_b[slot]])
        p.dma("gpsimd", (lambda s=slot, occ=oc: lambda e: e.dma_start(
            out=wu_sb.ap()[:, s, :, :],
            in_=wup[:, occ * 128:(occ + 1) * 128].rearrange(
                "(kc p) o -> p kc o", p=128)))(), writes=[wu_b[slot]])
        for th in range(2):
            for kc in range(16):
                t_mm("B", (0, 512), wg_sb.ap()[:, slot, kc, :],
                     h2T.ap()[:, kc, th * 512:(th + 1) * 512],
                     kc == 0, kc == 15,
                     [wg_b[slot]] + h2T_bt[th * 4:(th + 1) * 4])
                t_mm("C", (0, 512), wu_sb.ap()[:, slot, kc, :],
                     h2T.ap()[:, kc, th * 512:(th + 1) * 512],
                     kc == 0, kc == 15,
                     [wu_b[slot]] + h2T_bt[th * 4:(th + 1) * 4])
            p.op("scalar", lambda e: e.activation(gsc.ap(), ps["B"].ap(),
                                                  AF.Silu),
                 reads=[pbuf["B"]], writes=[gsc_b])
            p.op("vector", (lambda o=mbf.ap()[:, oc, th * 512:(th + 1) * 512]:
                            lambda e: e.tensor_mul(o, gsc.ap(),
                                                   ps["C"].ap()))(),
                 reads=[gsc_b, pbuf["C"]], writes=[mbf_b])

    A.free("h2T", "wg_sb", "wu_sb", "gsc")
    wd_sb, wd_b = A.alloc("wd_sb", [128, 2, NTT, 512], BF16, nbufs=2)
    mpq, mpq_b = A.alloc("mpq", [128, 2, 512], BF16, nbufs=2)
    for ec in range(4):
        slot = ec % 2
        p.dma("gpsimd", (lambda s=slot, e2=ec: lambda e: e.dma_start(
            out=wd_sb.ap()[:, s, :, :],
            in_=wdown[:, e2 * 512:(e2 + 1) * 512].rearrange(
                "(ic p) o -> p ic o", p=128)))(), writes=[wd_b[slot]])
        for j in range(NTT):
            psn = "B" if j % 2 == 0 else "C"
            for ic in range(NTT):
                t_mm(psn, (0, 512), mbf.ap()[:, ic, j * 128:(j + 1) * 128],
                     wd_sb.ap()[:, slot, ic, :],
                     ic == 0, ic == NTT - 1, [mbf_b, wd_b[slot]])
            msl = j % 2
            p.op("vector", (lambda o=mpq.ap()[:, msl, :], s=ps[psn].ap():
                            lambda e: e.tensor_copy(o, s))(),
                 reads=[pbuf[psn]], writes=[mpq_b[msl]])
            p.dma("sync", (lambda s=msl, jj=j, e2=ec: lambda e: e.dma_start(
                out=ar3_in[jj * 128:(jj + 1) * 128,
                           e2 * 512:(e2 + 1) * 512],
                in_=mpq.ap()[:, s, :]))(),
                reads=[mpq_b[msl]], stream="a3w")
    ar3out_b = Buf("ar3_out")
    p.coll(lambda e: e.collective_compute(
        "AllReduce", ALU.add, replica_groups=[list(range(NC))],
        ins=[ar3_in.opt()], outs=[ar3_out.opt()]),
        writes=[ar3out_b], deps=[p.stream_dep("a3w")])

    dbg_dram("a3", ar3_out, [S, E], BF16, None, reads=[ar3out_b])
    A.free("mbf", "wd_sb", "mpq")
    a3r, a3r_b = A.alloc("a3r", [128, NTT, E], BF16, nbufs=NTT)
    p.dma("sync", lambda e: e.dma_start(
        out=a3r.ap(), in_=ar3_out.rearrange("(j p) c -> p j c", p=128)),
        reads=[ar3out_b], writes=list(a3r_b))
    procf, (procf_b,) = A.alloc("procf", [128, E], F32)
    dlb, dlb_b = A.alloc("dlb", [128, 2, E], BF16, nbufs=2)
    for j in range(NTT):
        sl = j % 2
        p.op("vector", (lambda jj=j: lambda e: e.tensor_add(
            procf.ap(), a3r.ap()[:, jj, :], x1bf.ap()[:, jj, :]))(),
             reads=[a3r_b[j], x1bf_b[j]], writes=[procf_b])
        p.op("vector", (lambda jj=j: lambda e: e.tensor_sub(
            procf.ap(), procf.ap(), selr.ap()[:, jj, 0:E]))(),
             reads=[procf_b, selr_b[j]], writes=[procf_b])
        p.op("vector", (lambda s=sl, jj=j: lambda e: e.tensor_scalar(
            dlb.ap()[:, s, :], procf.ap(), gates.ap()[:, jj, 0:1], None,
            ALU.mult))(),
             reads=[procf_b, gates_b], writes=[dlb_b[sl]])
        p.dma("sync", (lambda s=sl, jj=j: lambda e: e.dma_start(
            out=delta_dram[jj * 128:(jj + 1) * 128, :],
            in_=dlb.ap()[:, s, :]))(),
            reads=[dlb_b[sl]], stream="dlw")
    dl_dep = p.stream_dep("dlw")
    A.free("a3r", "x1bf", "selr", "procf", "dlb")
    xrr, xrr_b = A.alloc("xrr", [128, 2, E], F32, nbufs=2)
    hlr, hlr_b = A.alloc("hlr", [128, 2, E], BF16, nbufs=2)
    dgt, dgt_b = A.alloc("dgt", [128, 2, E], BF16, nbufs=2)
    outf, outf_b = A.alloc("outf", [128, 2, E], F32, nbufs=2)
    dbg_dram("delta", delta_dram, [S, E], BF16, "dlw")
    xr_dep = p.stream_dep("xrw")
    hl_dep = p.stream_dep("hlw")
    for j in range(NTT):
        sl = j % 2
        p.dma("sync", (lambda s=sl, jj=j: lambda e: e.dma_start(
            out=xrr.ap()[:, s, :], in_=xres_dram[jj * 128:(jj + 1) * 128, :]))(),
            writes=[xrr_b[sl]], deps=[xr_dep])
        p.dma("scalar", (lambda s=sl, jj=j: lambda e: e.dma_start(
            out=hlr.ap()[:, s, :], in_=hln_dram[jj * 128:(jj + 1) * 128, :]))(),
            writes=[hlr_b[sl]], deps=[hl_dep])
        p.op("vector", (lambda s=sl: lambda e: e.memset(
            dgt.ap()[:, s, :], 0.0))(), writes=[dgt_b[sl]])
        p.dma("gpsimd", (lambda s=sl, jj=j: lambda e: e.indirect_dma_start(
            out=dgt.ap()[:, s, :], out_offset=None,
            in_=delta_dram, in_offset=bass.IndirectOffsetOnAxis(
                sidx_i.ap()[:, jj:jj + 1], 0),
            bounds_check=S - 1, oob_is_err=False))(),
            reads=[sidxi_b], writes=[dgt_b[sl]], deps=[dl_dep])
        p.op("vector", (lambda s=sl: lambda e: e.tensor_add(
            outf.ap()[:, s, :], xrr.ap()[:, s, :], hlr.ap()[:, s, :]))(),
             reads=[xrr_b[sl], hlr_b[sl]], writes=[outf_b[sl]])
        p.op("vector", (lambda s=sl: lambda e: e.tensor_add(
            outf.ap()[:, s, :], outf.ap()[:, s, :], dgt.ap()[:, s, :]))(),
             reads=[outf_b[sl], dgt_b[sl]], writes=[outf_b[sl]])
        p.dma("sync", (lambda s=sl, jj=j: lambda e: e.dma_start(
            out=y_out[jj * 128:(jj + 1) * 128, :], in_=outf.ap()[:, s, :]))(),
            reads=[outf_b[sl]], stream="yw")

    p.final_wait_streams(["yw"] + (["dbg"] if "dbg" in p.streams else []))
    p.finish()
    return nc, dbg_outs


def _shard_inputs(inputs):
    x = np.ascontiguousarray(np.asarray(inputs["L0_attn_x"],
                                        dtype=np.float32).reshape(T, E))
    per = []
    for c in range(NC):
        d = {
            "x": x[c * TPC:(c + 1) * TPC],
            "Wqk": inputs["L0_attn_Wqk"],
            "bqk": inputs["L0_attn_bqk"].reshape(1, E),
            "Wv": inputs["L0_attn_Wv"], "bv": inputs["L0_attn_bv"].reshape(1, E),
            "Wu": inputs["L0_attn_Wu"], "bu": inputs["L0_attn_bu"].reshape(1, E),
            "Wo": inputs["L0_attn_Wo"], "bo": inputs["L0_attn_bo"].reshape(1, E),
            "ln_g": inputs["L0_attn_ln_g"].reshape(1, E),
            "ln_b": inputs["L0_attn_ln_b"].reshape(1, E),
            "w_router": inputs["L0_ffn_w_router"].reshape(1, E),
            "wq": inputs["L0_ffn_wq"][:, 256 * c:256 * (c + 1)],
            "fbq": inputs["L0_ffn_bq"][256 * c:256 * (c + 1)].reshape(1, 256),
            "wk": inputs["L0_ffn_wk"][:, 128 * c:128 * (c + 1)],
            "fbk": inputs["L0_ffn_bk"][128 * c:128 * (c + 1)].reshape(1, 128),
            "wv2": inputs["L0_ffn_wv"][:, 128 * c:128 * (c + 1)],
            "fbv": inputs["L0_ffn_bv"][128 * c:128 * (c + 1)].reshape(1, 128),
            "wo2": inputs["L0_ffn_wo"][256 * c:256 * (c + 1), :],
            "wgate": inputs["L0_ffn_w_gate"][:, 1024 * c:1024 * (c + 1)],
            "wup": inputs["L0_ffn_w_up"][:, 1024 * c:1024 * (c + 1)],
            "wdown": inputs["L0_ffn_w_down"][1024 * c:1024 * (c + 1), :],
            "ln1w": inputs["L0_ffn_ln1_w"].reshape(1, E),
            "ln2w": inputs["L0_ffn_ln2_w"].reshape(1, E),
        }
        d = {k: np.ascontiguousarray(np.asarray(v, dtype=np.float32))
             for k, v in d.items()}
        d.update(_host_constants(c))
        per.append(d)
    return per


LAST_DBG = {}


def kernel(**inputs):
    nc, dbg = build_program()
    in_maps = _shard_inputs(inputs)
    res = run_bass_kernel_spmd(nc, in_maps, core_ids=list(range(NC)))
    global LAST_DBG
    LAST_DBG = {name: [np.asarray(res.results[c]["dbg_" + name])
                       for c in range(NC)] for name in dbg}
    out = np.concatenate([np.asarray(res.results[c]["y"]) for c in range(NC)],
                         axis=0)
    return out.reshape(B, N, E)


# version 09
# speedup vs baseline: 1.0980x; 1.0136x over previous
import math
import numpy as np
import ml_dtypes
import concourse.bass as bass
import concourse.mybir as mybir
from concourse.bass_utils import run_bass_kernel_spmd

F32 = mybir.dt.float32
BF16 = mybir.dt.bfloat16
I32 = mybir.dt.int32
AF = mybir.ActivationFunctionType
ALU = mybir.AluOpType
AX = mybir.AxisListType

B, N, E = 2, 4096, 2048
T = B * N
NC = 8
TPC = T // NC
NTT = TPC // 128
HEADS, DH = 16, 128
KSEL, S = 512, 1024
IF = 8192
LN_EPS = 1e-5
RMS_EPS = 1e-6
SELW = 2056
BIG = 1.0e6
MAGIC = 12582912.0
INV2PI = 1.0 / (2.0 * math.pi)
TWOPI = 2.0 * math.pi
ISQD = 1.0 / math.sqrt(DH)

import os
DEBUG = [s for s in os.environ.get("KDEBUG", "").split(",") if s]


ENGINES = ("tensor", "vector", "scalar", "gpsimd", "sync")


class _Op:
    __slots__ = ("idx", "eng", "emit", "kind", "sem", "val", "deps", "flag",
                 "eidx")

    def __init__(self, idx, eng, emit, kind):
        self.idx, self.eng, self.emit, self.kind = idx, eng, emit, kind
        self.sem = None
        self.val = None
        self.deps = []
        self.flag = False
        self.eidx = None


class Buf:
    __slots__ = ("name", "last_writer", "readers", "dma_sem", "dma_count")

    def __init__(self, name):
        self.name = name
        self.last_writer = None
        self.readers = []
        self.dma_sem = None
        self.dma_count = 0


class Stream:
    __slots__ = ("name", "sem", "count")

    def __init__(self, name, sem):
        self.name, self.sem, self.count = name, sem, 0


class Prog:
    def __init__(self, nc):
        self.nc = nc
        self.ops = []
        self.eng_ops = {e: [] for e in ENGINES}
        self.eng_sem = {e: nc.alloc_semaphore(f"esem_{e}") for e in
                        ("tensor", "vector", "scalar", "gpsimd")}
        self.streams = {}
        self.final_waits = []
        self.same_engine_implicit = False

    def stream(self, name):
        if name not in self.streams:
            self.streams[name] = Stream(name, self.nc.alloc_semaphore("ss_" + name))
        return self.streams[name]

    def stream_dep(self, name):
        st = self.stream(name)
        return ("rawsem", st.sem, 16 * st.count)

    def _record(self, eng, emit, kind, reads, writes, extra_deps):
        op = _Op(len(self.ops), eng, emit, kind)
        self.ops.append(op)
        self.eng_ops[eng].append(op)
        op.eidx = len(self.eng_ops[eng]) - 1
        for d in extra_deps:
            op.deps.append(d)
        for b in reads:
            if b.last_writer is not None:
                op.deps.append(b.last_writer)
            b.readers.append(op)
        for b in writes:
            if b.last_writer is not None and b.last_writer is not op:
                op.deps.append(b.last_writer)
            for r in b.readers:
                if r is not op:
                    op.deps.append(r)
            b.readers = []
            b.last_writer = op
        for i, d in enumerate(op.deps):
            if (not isinstance(d, tuple) and d.kind == "dma"
                    and isinstance(d.sem, Stream)):
                if d.sem.name == "trp":
                    op.deps[i] = ("rawsem", d.sem.sem, d.val)
                else:
                    op.deps[i] = ("rawsem", d.sem.sem, 16 * d.sem.count)
        return op

    def op(self, eng, emit, reads=(), writes=(), deps=()):
        assert eng != "sync"
        return self._record(eng, emit, "eng", reads, writes, deps)

    def dma(self, eng, emit, reads=(), writes=(), stream=None, deps=()):
        op = self._record(eng, emit, "dma", reads, writes, deps)
        if stream is not None:
            st = self.stream(stream)
            st.count += 1
            op.sem = st
            op.val = 16 * st.count
        else:
            assert len(writes) >= 1
            b = writes[0]
            if b.dma_sem is None:
                b.dma_sem = self.nc.alloc_semaphore("ds_" + b.name)
            if b.dma_count > 0:
                op.deps.append(("rawsem", b.dma_sem, 16 * b.dma_count))
            b.dma_count += 1
            op.sem = b
            op.val = 16 * b.dma_count
        return op

    def coll(self, emit, reads=(), writes=(), deps=()):
        op = self._record("gpsimd", emit, "coll", reads, writes, deps)
        op.sem = self.nc.alloc_semaphore(f"cs_{op.idx}")
        op.val = 1
        return op

    def final_wait_streams(self, names):
        for n in names:
            st = self.stream(n)
            self.final_waits.append((st.sem, 16 * st.count))

    def _completion(self, dep):
        if isinstance(dep, tuple):
            return dep[1], dep[2]
        if dep.kind == "eng":
            dep.flag = True
            return None, dep
        if dep.kind == "coll":
            return dep.sem, dep.val
        if isinstance(dep.sem, Stream):
            return dep.sem.sem, dep.val
        return dep.sem.dma_sem, dep.val

    def finish(self):
        nc = self.nc
        waitlists = []
        for op in self.ops:
            wl = []
            for dep in op.deps:
                if not isinstance(dep, tuple):
                    assert dep.idx < op.idx
                    if (dep.kind == "eng" and dep.eng == op.eng
                            and (self.same_engine_implicit
                                 or op.eng == "tensor")):
                        continue
                wl.append(self._completion(dep))
            waitlists.append(wl)
        cnt = {e: 0 for e in ENGINES}
        for op in self.ops:
            if op.kind == "eng" and op.flag:
                cnt[op.eng] += 1
                op.val = cnt[op.eng]
                op.sem = self.eng_sem[op.eng]
        resolved = []
        for wl in waitlists:
            resolved.append([(val.sem, val.val) if sem is None else (sem, val)
                             for sem, val in wl])
        know = {e: {} for e in ENGINES}
        for op in self.ops:
            k = know[op.eng]
            wl = []
            for sem, val in resolved[op.idx]:
                key = sem.num if hasattr(sem, "num") else id(sem)
                if k.get(key, -1) >= val:
                    continue
                k[key] = val
                wl.append((sem, val))
            resolved[op.idx] = wl
        with nc.Block() as block:
            def mk(engname):
                def run(eng):
                    for op in self.eng_ops[engname]:
                        for sem, val in resolved[op.idx]:
                            eng.wait_ge(sem, val)
                        inst = op.emit(eng)
                        if op.kind == "eng":
                            if op.flag:
                                inst.then_inc(op.sem, 1)
                        elif op.kind == "coll":
                            inst.then_inc(op.sem, 1)
                        else:
                            sem = (op.sem.sem if isinstance(op.sem, Stream)
                                   else op.sem.dma_sem)
                            inst.then_inc(sem, 16)
                    if engname == "sync":
                        for sem, val in self.final_waits:
                            eng.wait_ge(sem, val)
                return run
            block.tensor(mk("tensor"))
            block.vector(mk("vector"))
            block.scalar(mk("scalar"))
            block.gpsimd(mk("gpsimd"))
            block.sync(mk("sync"))


class Arena:

    def __init__(self, nc, base, size):
        self.nc = nc
        self.base = base
        self.size = size
        self.live = {}
        self.freed = []
        self.n = 0

    def alloc(self, name, shape, dtype, nbufs=1, align=512):
        per = int(np.prod(shape[1:])) * mybir.dt.size(dtype)
        per = (per + align - 1) // align * align
        spans = sorted((o, e) for (o, e, _) in self.live.values())
        holes = []
        cur = 0
        for (s, e) in spans:
            if s > cur:
                holes.append((cur, s))
            cur = max(cur, e)
        if cur < self.size:
            holes.append((cur, self.size))
        fit = [h for h in holes if h[1] - h[0] >= per]
        assert fit, \
            f"arena overflow: {name} needs {per}B, holes {holes}"
        off = min(fit, key=lambda h: h[1] - h[0])[0]
        war = []
        keep = []
        for (fo, fe, ops) in self.freed:
            if fo < off + per and off < fe:
                war.extend(ops)
                if fo < off:
                    keep.append((fo, off, ops))
                if fe > off + per:
                    keep.append((off + per, fe, ops))
            else:
                keep.append((fo, fe, ops))
        self.freed = keep
        bufs = [Buf(f"{name}_{i}") for i in range(nbufs)]
        for b in bufs:
            b.readers = list(war)
        self.n += 1
        h = self.nc.alloc_sbuf_tensor_at(f"ar{self.n}_{name}", list(shape),
                                         dtype, offset=self.base + off)
        self.live[name] = (off, off + per, bufs)
        return h, bufs

    def free(self, *names):
        for nm in names:
            off, end, bufs = self.live.pop(nm)
            ops = []
            for b in bufs:
                ops.extend(b.readers)
                if b.last_writer is not None:
                    ops.append(b.last_writer)
            self.freed.append((off, end, ops))


def _host_constants(core):
    b = core // 4
    rb = core % 4
    cst = {}
    cst["c_identf"] = np.eye(128, dtype=np.float32)
    cst["c_identb"] = np.eye(128, dtype=ml_dtypes.bfloat16)
    cst["c_sut"] = np.triu(np.ones((128, 128), np.float32), 1)
    cst["c_tril"] = np.where(np.tril(np.ones((128, 128), bool)), 0.0,
                             -1.0e30).astype(np.float32)
    P = np.zeros((128, 128), np.float32)
    for i in range(64):
        P[i, i + 64] = -1.0
        P[i + 64, i] = 1.0
    cst["c_permT"] = np.ascontiguousarray(P.T)
    cst["c_iota"] = np.arange(1, 129, dtype=np.float32).reshape(128, 1)
    cst["c_onecol"] = np.ones((128, 1), np.float32)
    inv = 1.0 / (10000.0 ** (np.arange(0, DH, 2, dtype=np.float32) / DH))
    invd = np.concatenate([inv, inv]).reshape(128, 1)
    cst["c_invf"] = (invd * INV2PI).astype(np.float32)
    cst["c_ones"] = np.ones((1, 512), np.float32)
    sbv = np.zeros((128, 2), np.float32)
    sbv[:, 0] = 1.0 - b
    sbv[:, 1] = b
    cst["c_selb"] = sbv
    f0 = b * 32 + rb * 8
    bm = np.zeros((1, 64), np.float32)
    bm[0, b * 32:f0] = 1.0
    cst["c_beforem"] = bm
    tl = np.arange(TPC)
    pos = rb * TPC + tl
    pe = np.zeros((128, NTT, 2), np.float32)
    pe[:, :, 0] = (pos // 64).reshape(NTT, 128).T
    pe[:, :, 1] = (pos % 64).reshape(NTT, 128).T
    cst["c_posex"] = pe.astype(ml_dtypes.bfloat16)
    cst["c_b512"] = np.full((128, 1), 512.0 * b, np.float32)
    return cst


_CONST_SPECS = None


def _const_specs():
    global _CONST_SPECS
    if _CONST_SPECS is None:
        c = _host_constants(0)
        _CONST_SPECS = {k: (list(v.shape),
                            BF16 if v.dtype == ml_dtypes.bfloat16 else F32)
                        for k, v in c.items()}
    return _CONST_SPECS


def build_program():
    nc = bass.Bass("TRN2", target_bir_lowering=False)
    p = Prog(nc)

    din = {}

    def dt_in(name, shape, dtype=F32):
        din[name] = nc.dram_tensor(name, list(shape), dtype,
                                   kind="ExternalInput").ap()
        return din[name]

    x_in = dt_in("x", [TPC, E])
    Wqk = dt_in("Wqk", [E, E]); dt_in("bqk", [1, E])
    Wv = dt_in("Wv", [E, E]); dt_in("bv", [1, E])
    Wu = dt_in("Wu", [E, E]); dt_in("bu", [1, E])
    Wo = dt_in("Wo", [E, E]); dt_in("bo", [1, E])
    ln_g = dt_in("ln_g", [1, E]); ln_b = dt_in("ln_b", [1, E])
    w_router = dt_in("w_router", [1, E])
    wq = dt_in("wq", [E, 256]); dt_in("fbq", [1, 256])
    wk = dt_in("wk", [E, 128]); dt_in("fbk", [1, 128])
    wv2 = dt_in("wv2", [E, 128]); dt_in("fbv", [1, 128])
    wo2 = dt_in("wo2", [256, E])
    wgate = dt_in("wgate", [E, 1024])
    wup = dt_in("wup", [E, 1024])
    wdown = dt_in("wdown", [1024, E])
    ln1w = dt_in("ln1w", [1, E]); ln2w = dt_in("ln2w", [1, E])
    for cname, (cshape, cdt) in _const_specs().items():
        dt_in(cname, cshape, cdt)

    y_out = nc.dram_tensor("y", [TPC, E], F32, kind="ExternalOutput").ap()

    def dscr(name, shape, dtype, shared=False):
        return nc.dram_tensor(name, list(shape), dtype, kind="Internal",
                              addr_space="Shared" if shared else "Local").ap()

    tr_dram = dscr("tr_dram", [TPC, E], BF16)
    tr_dram2 = dscr("tr_dram2", [TPC, E], BF16)
    kh_dram = dscr("kh_dram", [TPC, E], BF16)
    u_dram = dscr("u_dram", [TPC, E], F32)
    y2_dram = dscr("y2_dram", [TPC, E], F32)
    kv_in = dscr("kv_in", [128, 2 * E], F32)
    kv_out = dscr("kv_out", [128, 2 * E], F32)
    sc_in = dscr("sc_in", [NTT, 128], F32)
    sc_out = dscr("sc_out", [NC * NTT, 128], F32)
    xres_dram = dscr("xres_dram", [TPC, E], F32)
    hln_dram = dscr("hln_dram", [TPC, E], BF16)
    sel_in = dscr("sel_in", [S, SELW], BF16)
    sel_out = dscr("sel_out", [S, SELW], BF16)
    pos_dram = dscr("pos_dram", [NTT, 128], F32)
    ar2_in = dscr("ar2_in", [S, E], BF16)
    ar2_out = dscr("ar2_out", [S, E], BF16)
    ar3_in = dscr("ar3_in", [S, E], BF16)
    ar3_out = dscr("ar3_out", [S, E], BF16)
    delta_dram = dscr("delta_dram", [S, E], BF16)

    dbg_outs = {}

    def dbg_dram(name, dram_ap, shape, dtype, stream_name=None, reads=()):
        if name not in DEBUG:
            return
        d = nc.dram_tensor("dbg_" + name, list(shape), dtype,
                           kind="ExternalOutput").ap()
        dbg_outs[name] = d
        deps = [p.stream_dep(stream_name)] if stream_name else []
        p.dma("scalar", (lambda dd=d, s=dram_ap: lambda e: e.dma_start(
            out=dd, in_=s))(), reads=list(reads), stream="dbg", deps=deps)

    def dbg_store(name, shape, src_ap, reads, dtype=F32):
        if name not in DEBUG:
            return
        d = nc.dram_tensor("dbg_" + name, list(shape), dtype,
                           kind="ExternalOutput").ap()
        dbg_outs[name] = d
        p.dma("sync", (lambda dd=d, s=src_ap: lambda e: e.dma_start(
            out=dd, in_=s))(), reads=list(reads), stream="dbg")

    sbt = {}

    def salloc(name, shape, dtype):
        sbt[name] = nc.alloc_sbuf_tensor("s_" + name, list(shape), dtype)
        return sbt[name]

    consts = {}
    CB = {}
    for cname, (cshape, cdt) in _const_specs().items():
        hh = salloc(cname, cshape, cdt)
        consts[cname] = hh
        CB[cname] = Buf(cname)
        q = "gpsimd" if cdt == BF16 else "sync"
        p.dma(q, (lambda h=hh, a=din[cname]:
                  lambda e: e.dma_start(out=h.ap(), in_=a))(),
              writes=[CB[cname]], stream="cload")
    def bcast_load(nm, srcrow, dt_):
        hh, (bb,) = A.alloc(nm, [128, E], dt_)
        q = "gpsimd" if dt_ == BF16 else "sync"
        p.dma(q, (lambda h=hh, a=srcrow:
                  lambda e: e.dma_start(out=h.ap(),
                                        in_=a.to_broadcast([128, E])))(),
              writes=[bb])
        return hh, bb
    for nm, src, w in (("r_fbq", din["fbq"], 256), ("r_fbk", din["fbk"], 128),
                       ("r_fbv", din["fbv"], 128)):
        hh = salloc(nm, [1, w], F32)
        consts[nm] = hh
        CB[nm] = Buf(nm)
        p.dma("sync", (lambda h=hh, a=src:
                       lambda e: e.dma_start(out=h.ap(), in_=a))(),
              writes=[CB[nm]], stream="cload")
    bias_slot = salloc("bias_slot", [1, 4 * 512], F32)
    bias_slot_b = [Buf(f"bias{i}") for i in range(4)]

    salloc("stats", [128, 2, 8], F32)
    salloc("wsum", [128, 1], F32)
    salloc("scores", [128, NTT], F32)
    _stap = sbt["stats"].ap()

    def ST(sl):
        a = _stap[:, sl, :]
        return {"mu": a[:, 0:1], "rstd": a[:, 1:2], "t1": a[:, 2:3],
                "t2": a[:, 3:4], "t3": a[:, 4:5]}
    stat_bs = [Buf("stat0"), Buf("stat1")]
    st_mu = ST(0)["mu"]; st_rstd = ST(0)["rstd"]
    st_t1 = ST(0)["t1"]; st_t2 = ST(0)["t2"]; st_t3 = ST(0)["t3"]
    stat_b = stat_bs[0]
    scsb = sbt["scores"].ap()
    sc_b = Buf("scores")

    ones = consts["c_ones"].ap()
    onecol = consts["c_onecol"].ap()
    selb = consts["c_selb"].ap()
    identb = consts["c_identb"].ap()
    identf = consts["c_identf"].ap()


    ps = {}
    ps["A"] = nc.alloc_psum_tensor("psA", [128, 1024], F32)
    for nm in ("B", "C", "D", "E2"):
        ps[nm] = nc.alloc_psum_tensor("ps" + nm, [128, 512], F32)
    ps["T"] = nc.alloc_psum_tensor("psT", [128, 1024], BF16)
    ps["S"] = nc.alloc_psum_tensor("psS", [128, 512], F32)
    pbuf = {k: Buf("ps" + k) for k in ps}
    PSN4 = ("B", "C", "D", "E2")

    avail = nc.sbuf_bytes_remaining
    arena_size = (avail - 12 * 1024) // 512 * 512
    backing = nc.alloc_sbuf_tensor("arena_backing", [128, arena_size],
                                   mybir.dt.uint8)
    arena_base = nc.lookup_mloc(backing).addr
    A = Arena(nc, arena_base, arena_size)

    def t_mm(psname, cols, lhsT, rhs, start, stop, reads, rows=(0, 128)):
        o = ps[psname].ap()[rows[0]:rows[1], cols[0]:cols[1]]
        p.op("tensor",
             (lambda o=o, l=lhsT, r=rhs, st=start, sp=stop:
              lambda e: e.matmul(o, l, r, start=st, stop=sp))(),
             reads=list(reads), writes=[pbuf[psname]])

    def t_transpose(psname, rows, cols, in_ap, ident_ap, reads, iread):
        o = ps[psname].ap()[rows[0]:rows[1], cols[0]:cols[1]]
        p.op("tensor",
             (lambda o=o, i=in_ap, idn=ident_ap:
              lambda e: e.transpose(o, i, idn))(),
             reads=list(reads) + [iread], writes=[pbuf[psname]])


    def pe_transpose_tile(dstT_ap, dst_buf, src_ap, src_bufs, j, nch=16):
        for g in range(nch // 8):
            for kk in range(8):
                kc = g * 8 + kk
                t_transpose("T", (0, 128), (kk * 128, kk * 128 + 128),
                            src_ap[:, kc * 128:(kc + 1) * 128], identb,
                            list(src_bufs), CB["c_identb"])
            p.op("vector", (lambda gg=g, d=dstT_ap, jj=j:
                            lambda e: e.tensor_copy(
                                d[:, gg * 8:(gg + 1) * 8,
                                  jj * 128:(jj + 1) * 128],
                                ps["T"].ap().rearrange(
                                    "p (a b) -> p a b", a=8)))(),
                 reads=[pbuf["T"]], writes=[dst_buf])

    def emit_transposes(dst_ap_fn, dst_bufs, src_dram, nchunks, dep):
        st = p.stream("trp")
        for kc in range(nchunks):
            deps = [dep, ("rawsem", st.sem, 16 * st.count)]
            p.dma("sync", (lambda k=kc: lambda e: e.dma_start(
                out=dst_ap_fn(k), in_=src_dram[:, k * 128:(k + 1) * 128],
                transpose=True))(),
                writes=[dst_bufs[kc]], stream="trp", deps=deps)

    zt, (zt_b,) = A.alloc("ztile", [128, SELW], BF16)
    p.op("vector", lambda e: e.memset(zt.ap(), 0.0), writes=[zt_b])
    for j in range(NTT):
        p.dma("scalar", (lambda jj=j: lambda e: e.dma_start(
            out=sel_in[jj * 128:(jj + 1) * 128, :], in_=zt.ap()))(),
            reads=[zt_b], stream="selz")
    A.free("ztile")

    xin, xin_b = A.alloc("xin", [128, 2, E], F32, nbufs=2)
    xln, xln_b = A.alloc("xln", [128, 2, E], BF16, nbufs=2)
    sq, sq_bs = A.alloc("sq", [128, 2, E], F32, nbufs=2)
    xT, xT_bt = A.alloc("xT", [128, 16, TPC], BF16, nbufs=NTT)

    def ln_stats(src_ap, src_bufs, eps, sl=0):
        st = ST(sl); sb = stat_bs[sl]; sqv = sq.ap()[:, sl, :]
        sq_b = sq_bs[sl]
        p.op("vector", (lambda s=src_ap, o=st["t1"]: lambda e: e.reduce_sum(
            o, s, axis=AX.X))(), reads=list(src_bufs), writes=[sb])
        p.op("vector", (lambda st=st: lambda e: e.tensor_scalar_mul(
            st["mu"], st["t1"], 1.0 / E))(), reads=[sb], writes=[sb])
        p.op("scalar", (lambda s=src_ap, o=sqv: lambda e: e.activation(
            o, s, AF.Square))(), reads=list(src_bufs), writes=[sq_b])
        p.op("vector", (lambda st=st, i=sqv: lambda e: e.reduce_sum(
            st["t2"], i, axis=AX.X))(), reads=[sq_b], writes=[sb])
        p.op("vector", (lambda st=st, ep=eps: lambda e: e.tensor_scalar(
            st["t2"], st["t2"], 1.0 / E, ep, ALU.mult, ALU.add))(),
             reads=[sb], writes=[sb])
        p.op("vector", (lambda st=st: lambda e: e.tensor_mul(
            st["t1"], st["mu"], st["mu"]))(), reads=[sb], writes=[sb])
        p.op("vector", (lambda st=st: lambda e: e.tensor_sub(
            st["t2"], st["t2"], st["t1"]))(), reads=[sb], writes=[sb])
        p.op("scalar", (lambda st=st: lambda e: e.activation(
            st["t1"], st["t2"], AF.Sqrt))(), reads=[sb], writes=[sb])
        p.op("vector", (lambda st=st: lambda e: e.reciprocal(
            st["rstd"], st["t1"]))(), reads=[sb], writes=[sb])
        p.op("vector", (lambda st=st: lambda e: e.tensor_mul(
            st["t2"], st["mu"], st["rstd"]))(), reads=[sb], writes=[sb])
        return st, sb

    for j in range(NTT):
        sl = j % 2
        p.dma("sync", (lambda s=sl, jj=j: lambda e: e.dma_start(
            out=xin.ap()[:, s, :], in_=x_in[jj * 128:(jj + 1) * 128, :]))(),
            writes=[xin_b[sl]])
        st, sb = ln_stats(xin.ap()[:, sl, :], [xin_b[sl]], LN_EPS, sl)
        p.op("vector", (lambda s=sl, st=st: lambda e: e.tensor_scalar(
            xln.ap()[:, s, :], xin.ap()[:, s, :], st["rstd"], st["t2"],
            ALU.mult, ALU.subtract))(),
             reads=[xin_b[sl], sb], writes=[xln_b[sl]])
        pe_transpose_tile(xT.ap(), xT_bt[j], xln.ap()[:, sl, :],
                          [xln_b[sl]], j)
    A.free("xin", "xln")

    wqk_buf, wqk_b = A.alloc("wqk", [128, 2, 16, 512], BF16, nbufs=2)

    def load_wquarter(slot, W, oc):
        src = W[:, oc * 512:(oc + 1) * 512].rearrange("(kc p) o -> p kc o",
                                                      p=128)
        p.dma("gpsimd", (lambda s=slot, ss=src: lambda e: e.dma_start(
            out=wqk_buf.ap()[:, s, :, :], in_=ss))(), writes=[wqk_b[slot]])

    def load_biasq(slot, brow, oc):
        p.dma("sync", (lambda s=slot, b=brow, occ=oc: lambda e: e.dma_start(
            out=bias_slot.ap()[0:1, s * 512:(s + 1) * 512],
            in_=b[0:1, occ * 512:(occ + 1) * 512]))(),
            writes=[bias_slot_b[slot]])

    def proj_pass(W, brow, evac):
        for oc in range(4):
            slot = oc % 2
            load_wquarter(slot, W, oc)
            load_biasq(oc % 4, brow, oc)
            for j in range(NTT):
                psn = "B" if j % 2 == 0 else "C"
                for kc in range(16):
                    t_mm(psn, (0, 512),
                         xT.ap()[:, kc, j * 128:(j + 1) * 128],
                         wqk_buf.ap()[:, slot, kc, :],
                         kc == 0, False, [xT_bt[j], wqk_b[slot]])
                p.op("tensor", (lambda o=ps[psn].ap()[:, 0:512], s=oc % 4:
                                lambda e: e.matmul(
                                    o, ones[0:1, 0:128],
                                    bias_slot.ap()[0:1, s * 512:(s + 1) * 512],
                                    start=False, stop=True))(),
                     reads=[bias_slot_b[oc % 4], CB["c_ones"]],
                     writes=[pbuf[psn]])
                evac(psn, j, oc)

    kh, kh_b = A.alloc("kh", [128, NTT, E], BF16, nbufs=NTT)
    escr, (escr_b,) = A.alloc("escr", [128, 2, 512], F32, nbufs=1)
    escr2, (escr2_b,) = A.alloc("escr2", [128, 2, 512], F32, nbufs=1)

    def evac_qk(psn, j, oc):
        sl = j % 2
        p.op("vector", (lambda s=ps[psn].ap()[:, 0:512], q=sl:
                        lambda e: e.tensor_scalar(
                            escr.ap()[:, q, :], s, 0.0, None, ALU.min))(),
             reads=[pbuf[psn]], writes=[escr_b])
        p.op("scalar", (lambda q=sl: lambda e: e.activation(
            escr2.ap()[:, q, :], escr.ap()[:, q, :], AF.Exp))(),
             reads=[escr_b], writes=[escr2_b])
        p.op("vector", (lambda o=kh.ap()[:, j, oc * 512:(oc + 1) * 512],
                        s=ps[psn].ap()[:, 0:512], q=sl:
                        lambda e: e.scalar_tensor_tensor(
                            o, escr2.ap()[:, q, :], -1.0, s,
                            ALU.add, ALU.max))(),
             reads=[escr2_b, pbuf[psn]], writes=[kh_b[j]])

    proj_pass(Wqk, din["bqk"], evac_qk)
    A.free("escr", "escr2")
    vt, vt_b = A.alloc("vt", [128, NTT, E], BF16, nbufs=NTT)

    def evac_v(psn, j, oc):
        p.op("scalar", (lambda o=vt.ap()[:, j, oc * 512:(oc + 1) * 512],
                        s=ps[psn].ap()[:, 0:512]:
                        lambda e: e.activation(o, s, AF.Silu))(),
             reads=[pbuf[psn]], writes=[vt_b[j]])

    proj_pass(Wv, din["bv"], evac_v)

    kvsb, (kvsb_b,) = A.alloc("kvsb", [128, E], F32)
    kmsk, (kmsk_b,) = A.alloc("kmsk", [128, E], F32)
    for hg in range(4):
        psn = PSN4[hg]
        for hh in range(4):
            h = hg * 4 + hh
            for j in range(NTT):
                t_mm(psn, (hh * 128, hh * 128 + 128),
                     kh.ap()[:, j, h * 128:(h + 1) * 128],
                     vt.ap()[:, j, h * 128:(h + 1) * 128],
                     j == 0, j == NTT - 1, [kh_b[j], vt_b[j]])
        p.op("vector", (lambda o=kvsb.ap()[:, hg * 512:(hg + 1) * 512],
                        s=ps[psn].ap(): lambda e: e.tensor_copy(o, s))(),
             reads=[pbuf[psn]], writes=[kvsb_b])
    p.op("vector", lambda e: e.tensor_scalar(kmsk.ap(), kvsb.ap(),
                                             selb[:, 0:1], None, ALU.mult),
         reads=[kvsb_b, CB["c_selb"]], writes=[kmsk_b])
    p.dma("sync", lambda e: e.dma_start(out=kv_in[:, 0:E], in_=kmsk.ap()),
          reads=[kmsk_b], stream="kvw")
    p.op("vector", lambda e: e.tensor_scalar(kmsk.ap(), kvsb.ap(),
                                             selb[:, 1:2], None, ALU.mult),
         reads=[kvsb_b, CB["c_selb"]], writes=[kmsk_b])
    p.dma("sync", lambda e: e.dma_start(out=kv_in[:, E:2 * E], in_=kmsk.ap()),
          reads=[kmsk_b], stream="kvw")
    kv_out_b = Buf("kv_out")
    p.coll(lambda e: e.collective_compute(
        "AllReduce", ALU.add, replica_groups=[list(range(NC))],
        ins=[kv_in.opt()], outs=[kv_out.opt()]),
        writes=[kv_out_b], deps=[p.stream_dep("kvw")])
    A.free("kvsb", "kmsk")

    A.free("vt")
    khT, khT_bt = A.alloc("khT", [128, 16, TPC], BF16, nbufs=NTT)
    for j in range(NTT):
        pe_transpose_tile(khT.ap(), khT_bt[j], kh.ap()[:, j, :],
                          [kh_b[j]], j)

    A.free("kh")
    usl, usl_b = A.alloc("usl", [128, 2, E], F32, nbufs=2)

    def evac_u(psn, j, oc):
        sl = j % 2
        p.op("scalar", (lambda o=usl.ap()[:, sl, 0:512],
                        s=ps[psn].ap()[:, 0:512]:
                        lambda e: e.activation(o, s, AF.Silu))(),
             reads=[pbuf[psn]], writes=[usl_b[sl]])
        p.dma("sync", (lambda s=sl, jj=j, occ=oc: lambda e: e.dma_start(
            out=u_dram[jj * 128:(jj + 1) * 128, occ * 512:(occ + 1) * 512],
            in_=usl.ap()[:, s, 0:512]))(),
            reads=[usl_b[sl]], stream="uw")

    proj_pass(Wu, din["bu"], evac_u)

    dbg_dram("u", u_dram, [TPC, E], F32, "uw")
    kvf, (kvf_b,) = A.alloc("kvf", [128, E], F32)
    kvlo, (kvlo_b,) = A.alloc("kvlo", [128, E], F32)
    kvbf, (kvbf_b,) = A.alloc("kvbf", [128, E], BF16)
    p.dma("sync", lambda e: e.dma_start(out=kvlo.ap(), in_=kv_out[:, 0:E]),
          reads=[kv_out_b], writes=[kvlo_b])
    p.dma("sync", lambda e: e.dma_start(out=kvf.ap(), in_=kv_out[:, E:2 * E]),
          reads=[kv_out_b], writes=[kvf_b])
    p.op("vector", lambda e: e.tensor_scalar(kvlo.ap(), kvlo.ap(),
                                             selb[:, 0:1], None, ALU.mult),
         reads=[kvlo_b, CB["c_selb"]], writes=[kvlo_b])
    p.op("vector", lambda e: e.tensor_scalar(kvf.ap(), kvf.ap(),
                                             selb[:, 1:2], None, ALU.mult),
         reads=[kvf_b, CB["c_selb"]], writes=[kvf_b])
    p.op("vector", lambda e: e.tensor_add(kvf.ap(), kvf.ap(), kvlo.ap()),
         reads=[kvf_b, kvlo_b], writes=[kvf_b])
    p.op("vector", lambda e: e.tensor_copy(kvbf.ap(), kvf.ap()),
         reads=[kvf_b], writes=[kvbf_b])
    dbg_store("kv", [128, E], kvf.ap(), [kvf_b])
    A.free("kvlo", "kvf")

    A.free("xT")
    zT, zT_bt = A.alloc("zT", [128, 16, TPC], BF16, nbufs=NTT)
    lnsc, (lnsc_b,) = A.alloc("lnsc", [128, E], F32)
    zsl, zsl_b = A.alloc("zsl", [128, 2, E], BF16, nbufs=2)
    g_bh, g_bb = bcast_load("g_b", ln_g, F32)
    b_bh, b_bb = bcast_load("b_b", ln_b, F32)
    g_b = g_bh.ap(); b_b = b_bh.ap()
    uw_dep = p.stream_dep("uw")
    for j in range(NTT):
        sl = j % 2
        for hg in range(4):
            psn = PSN4[hg]
            for hh in range(4):
                h = hg * 4 + hh
                t_mm(psn, (hh * 128, hh * 128 + 128),
                     khT.ap()[:, h, j * 128:(j + 1) * 128],
                     kvbf.ap()[:, h * 128:(h + 1) * 128],
                     True, True, [khT_bt[j], kvbf_b])
            p.op("scalar", (lambda o=lnsc.ap()[:, hg * 512:(hg + 1) * 512],
                            s=ps[psn].ap(): lambda e: e.copy(o, s))(),
                 reads=[pbuf[psn]], writes=[lnsc_b])
        st, sb = ln_stats(lnsc.ap(), [lnsc_b], LN_EPS, sl)
        p.op("vector", (lambda st=st: lambda e: e.tensor_scalar(
            lnsc.ap(), lnsc.ap(), st["rstd"], st["t2"], ALU.mult,
            ALU.subtract))(),
             reads=[lnsc_b, sb], writes=[lnsc_b])
        p.op("gpsimd", lambda e: e.tensor_mul(lnsc.ap(), lnsc.ap(), g_b),
             reads=[g_bb], writes=[lnsc_b])
        p.op("gpsimd", lambda e: e.tensor_add(lnsc.ap(), lnsc.ap(), b_b),
             reads=[b_bb], writes=[lnsc_b])
        p.dma("sync", (lambda s=sl, jj=j: lambda e: e.dma_start(
            out=usl.ap()[:, s, :], in_=u_dram[jj * 128:(jj + 1) * 128, :]))(),
            writes=[usl_b[sl]], deps=[uw_dep])
        p.op("vector", (lambda s=sl: lambda e: e.tensor_mul(
            zsl.ap()[:, s, :], lnsc.ap(), usl.ap()[:, s, :]))(),
             reads=[lnsc_b, usl_b[sl]], writes=[zsl_b[sl]])
        pe_transpose_tile(zT.ap(), zT_bt[j], zsl.ap()[:, sl, :],
                          [zsl_b[sl]], j)
    A.free("usl", "lnsc", "kvbf", "g_b", "b_b")

    A.free("khT", "zsl")
    y2sl, y2sl_b = A.alloc("y2sl", [128, 2, 512], F32, nbufs=2)

    def evac_y2(psn, j, oc):
        sl = j % 2
        p.op("vector", (lambda o=y2sl.ap()[:, sl, :],
                        s=ps[psn].ap()[:, 0:512]:
                        lambda e: e.tensor_copy(o, s))(),
             reads=[pbuf[psn]], writes=[y2sl_b[sl]])
        p.dma("sync", (lambda s=sl, jj=j, occ=oc: lambda e: e.dma_start(
            out=y2_dram[jj * 128:(jj + 1) * 128, occ * 512:(occ + 1) * 512],
            in_=y2sl.ap()[:, s, :]))(),
            reads=[y2sl_b[sl]], stream="y2w")

    for oc in range(4):
        slot = oc % 2
        load_wquarter(slot, Wo, oc)
        load_biasq(oc % 4, din["bo"], oc)
        for j in range(NTT):
            psn = "B" if j % 2 == 0 else "C"
            for kc in range(16):
                t_mm(psn, (0, 512),
                     zT.ap()[:, kc, j * 128:(j + 1) * 128],
                     wqk_buf.ap()[:, slot, kc, :],
                     kc == 0, False, [zT_bt[j], wqk_b[slot]])
            p.op("tensor", (lambda o=ps[psn].ap()[:, 0:512], s=oc % 4:
                            lambda e: e.matmul(
                                o, ones[0:1, 0:128],
                                bias_slot.ap()[0:1, s * 512:(s + 1) * 512],
                                start=False, stop=True))(),
                 reads=[bias_slot_b[oc % 4], CB["c_ones"]], writes=[pbuf[psn]])
            evac_y2(psn, j, oc)

    dbg_dram("y2", y2_dram, [TPC, E], F32, "y2w")
    A.free("zT", "wqk", "y2sl")
    hln, hln_b = A.alloc("hln", [128, NTT, SELW], BF16, nbufs=NTT)
    xin2, xin2_b = A.alloc("xin2", [128, 2, E], F32, nbufs=2)
    lnsc2, lnsc2_bs = A.alloc("lnsc2", [128, 2, E], F32, nbufs=2)
    xr, xr_b = A.alloc("xr", [128, 2, E], F32, nbufs=2)
    posex_b = hln_b[0]
    p.dma("gpsimd", lambda e: e.dma_start(
        out=hln.ap()[:, :, 2048:2050], in_=din["c_posex"]),
        writes=list(hln_b))
    p.op("vector", lambda e: e.memset(hln.ap()[:, :, 2051:SELW], 0.0),
         writes=list(hln_b))
    hlnpad_op = p.ops[-1]
    wr_bh, wr_bb = bcast_load("wr_b", w_router, F32)
    wr_b = wr_bh.ap()
    p.op("vector", lambda e: e.reduce_sum(sbt["wsum"].ap(), wr_b, axis=AX.X),
         reads=[wr_bb], writes=[stat_b])
    y2_dep = p.stream_dep("y2w")
    for j in range(NTT):
        sl = j % 2
        p.dma("sync", (lambda s=sl, jj=j: lambda e: e.dma_start(
            out=xin2.ap()[:, s, :], in_=x_in[jj * 128:(jj + 1) * 128, :]))(),
            writes=[xin2_b[sl]])
        p.dma("scalar", (lambda s=sl, jj=j: lambda e: e.dma_start(
            out=xr.ap()[:, s, :], in_=y2_dram[jj * 128:(jj + 1) * 128, :]))(),
            writes=[xr_b[sl]], deps=[y2_dep])
        p.op("vector", (lambda s=sl: lambda e: e.tensor_add(
            xr.ap()[:, s, :], xr.ap()[:, s, :], xin2.ap()[:, s, :]))(),
             reads=[xr_b[sl], xin2_b[sl]], writes=[xr_b[sl]])
        st, sb = ln_stats(xr.ap()[:, sl, :], [xr_b[sl]], LN_EPS, sl)
        p.op("vector", (lambda s=sl, jj=j, st=st: lambda e: e.tensor_scalar(
            hln.ap()[:, jj, 0:E], xr.ap()[:, s, :], st["rstd"], st["t2"],
            ALU.mult, ALU.subtract))(),
             reads=[xr_b[sl], sb], writes=[hln_b[j]])
        p.op("gpsimd", (lambda s=sl, o=lnsc2.ap()[:, sl, :]:
                        lambda e: e.tensor_mul(o, xr.ap()[:, s, :], wr_b))(),
             reads=[xr_b[sl], wr_bb], writes=[lnsc2_bs[sl]])
        p.op("vector", (lambda st=st, i=lnsc2.ap()[:, sl, :]:
                        lambda e: e.reduce_sum(st["t1"], i, axis=AX.X))(),
             reads=[lnsc2_bs[sl]], writes=[sb])
        p.op("vector", (lambda st=st: lambda e: e.tensor_mul(
            st["t3"], st["mu"], sbt["wsum"].ap()))(), reads=[sb], writes=[sb])
        p.op("vector", (lambda st=st: lambda e: e.tensor_sub(
            st["t1"], st["t1"], st["t3"]))(), reads=[sb], writes=[sb])
        p.op("vector", (lambda jj=j, st=st: lambda e: e.tensor_mul(
            scsb[:, jj:jj + 1], st["t1"], st["rstd"]))(),
             reads=[sb], writes=[sc_b])
        p.op("vector", (lambda jj=j: lambda e: e.tensor_copy(
            hln.ap()[:, jj, 2050:2051], scsb[:, jj:jj + 1]))(),
             reads=[sc_b], writes=[hln_b[j]])
        p.dma("sync", (lambda s=sl, jj=j: lambda e: e.dma_start(
            out=xres_dram[jj * 128:(jj + 1) * 128, :], in_=xr.ap()[:, s, :]))(),
            reads=[xr_b[sl]], stream="xrw")
        p.dma("scalar", (lambda jj=j: lambda e: e.dma_start(
            out=hln_dram[jj * 128:(jj + 1) * 128, :],
            in_=hln.ap()[:, jj, 0:E]))(),
            reads=[hln_b[j]], stream="hlw")

    dbg_dram("xres", xres_dram, [TPC, E], F32, "xrw")
    dbg_dram("hlnd", hln_dram, [TPC, E], BF16, "hlw")
    scT, (scT_b,) = A.alloc("scT", [NTT, 128], F32)
    t_transpose("S", (0, NTT), (0, 128), scsb, identf, [sc_b], CB["c_identf"])
    p.op("vector", lambda e: e.tensor_copy(scT.ap(),
                                           ps["S"].ap()[0:NTT, 0:128]),
         reads=[pbuf["S"]], writes=[scT_b])
    p.dma("sync", lambda e: e.dma_start(out=sc_in, in_=scT.ap()),
          reads=[scT_b], stream="scw")
    scout_b = Buf("sc_out")
    p.coll(lambda e: e.collective_compute(
        "AllGather", ALU.bypass, replica_groups=[list(range(NC))],
        ins=[sc_in.opt()], outs=[sc_out.opt()]),
        writes=[scout_b], deps=[p.stream_dep("scw")])

    dbg_dram("scout", sc_out, [NC * NTT, 128], F32, None, reads=[scout_b])
    scb, (scb_b,) = A.alloc("scb", [128, T], F32)
    p.dma("sync", lambda e: e.dma_start(
        out=scb.ap(), in_=sc_out.rearrange("(o a) b -> o (a b)", o=1).to_broadcast(
            [128, T])), reads=[scout_b], writes=[scb_b])
    stab64, (stab64_b,) = A.alloc("stab64", [64, 128], F32)
    p.dma("scalar", lambda e: e.dma_start(out=stab64.ap(), in_=sc_out),
          reads=[scout_b], writes=[stab64_b])
    tab, (tab_b,) = A.alloc("tab", [128, 64], F32)
    t_transpose("S", (0, 128), (0, 64), stab64.ap(), identf[0:64, 0:64], [stab64_b], CB["c_identf"])
    p.op("vector", lambda e: e.tensor_copy(tab.ap(), ps["S"].ap()[:, 0:64]),
         reads=[pbuf["S"]], writes=[tab_b])

    gs, (gs_b,) = A.alloc("gs", [1, 16], F32)
    gsl = gs.ap()
    junk, (junk_b,) = A.alloc("junk", [128, 4096], BF16)
    cnt2, (cnt2_b,) = A.alloc("cnt2", [128, 2], F32)
    tcol, (tcol_b,) = A.alloc("tcol", [128, 4], F32)
    sq64, (sq64_b,) = A.alloc("sq64", [128, 64], F32)

    p.op("scalar", lambda e: e.activation(sq64.ap(), tab.ap(), AF.Square),
         reads=[tab_b], writes=[sq64_b])
    t_mm("S", (0, 64), onecol, sq64.ap(), True, True,
         [sq64_b, CB["c_onecol"]], rows=(0, 1))
    p.op("vector", lambda e: e.tensor_copy(sq64.ap()[0:1, 0:64],
                                           ps["S"].ap()[0:1, 0:64]),
         reads=[pbuf["S"]], writes=[sq64_b])
    p.op("vector", lambda e: e.reduce_sum(
        gsl[0:1, 8:10], sq64.ap()[0:1, 0:64].rearrange("a (b c) -> a b c", b=2),
        axis=AX.X), reads=[sq64_b], writes=[gs_b])
    p.op("scalar", lambda e: e.activation(gsl[0:1, 2:4], gsl[0:1, 8:10],
                                          AF.Sqrt), reads=[gs_b], writes=[gs_b])
    p.op("vector", lambda e: e.tensor_scalar_add(gsl[0:1, 2:4],
                                                 gsl[0:1, 2:4], 1.0),
         reads=[gs_b], writes=[gs_b])
    p.op("vector", lambda e: e.tensor_scalar_mul(gsl[0:1, 0:2],
                                                 gsl[0:1, 2:4], -1.0),
         reads=[gs_b], writes=[gs_b])
    p.op("vector", lambda e: e.tensor_scalar_mul(gsl[0:1, 2:4],
                                                 gsl[0:1, 2:4], 2.0 / 129.0),
         reads=[gs_b], writes=[gs_b])

    iota = consts["c_iota"].ap()
    for rnd in range(4):
        t_mm("S", (0, 4), ones[0:1, 0:128], gsl[0:1, 0:4], True, True,
             [gs_b, CB["c_ones"]])
        p.op("vector", lambda e: e.tensor_copy(tcol.ap(),
                                               ps["S"].ap()[:, 0:4]),
             reads=[pbuf["S"]], writes=[tcol_b])
        p.op("vector", lambda e: e.scalar_tensor_tensor(
            tcol.ap()[:, 0:1], tcol.ap()[:, 2:3], iota[:, 0:1],
            tcol.ap()[:, 0:1], ALU.mult, ALU.add),
             reads=[tcol_b, CB["c_iota"]], writes=[tcol_b])
        p.op("vector", lambda e: e.scalar_tensor_tensor(
            tcol.ap()[:, 1:2], tcol.ap()[:, 3:4], iota[:, 0:1],
            tcol.ap()[:, 1:2], ALU.mult, ALU.add),
             reads=[tcol_b, CB["c_iota"]], writes=[tcol_b])
        for bb in range(2):
            p.op("vector", (lambda b2=bb: lambda e: e.tensor_scalar(
                junk.ap(), scb.ap()[:, b2 * 4096:(b2 + 1) * 4096],
                tcol.ap()[:, b2:b2 + 1], None, ALU.is_gt))(),
                 reads=[scb_b, tcol_b], writes=[junk_b])
            p.op("vector", (lambda b2=bb: lambda e: e.reduce_sum(
                cnt2.ap()[:, b2:b2 + 1], junk.ap(), axis=AX.X))(),
                 reads=[junk_b], writes=[cnt2_b])
        p.op("vector", lambda e: e.tensor_scalar(
            cnt2.ap(), cnt2.ap(), float(KSEL), None, ALU.is_ge),
             reads=[cnt2_b], writes=[cnt2_b])
        t_mm("S", (0, 2), onecol, cnt2.ap(), True, True,
             [cnt2_b, CB["c_onecol"]], rows=(0, 1))
        p.op("vector", lambda e: e.tensor_copy(gsl[0:1, 4:6],
                                               ps["S"].ap()[0:1, 0:2]),
             reads=[pbuf["S"]], writes=[gs_b])
        p.op("vector", lambda e: e.tensor_mul(gsl[0:1, 6:8], gsl[0:1, 4:6],
                                              gsl[0:1, 2:4]),
             reads=[gs_b], writes=[gs_b])
        p.op("vector", lambda e: e.tensor_add(gsl[0:1, 0:2], gsl[0:1, 0:2],
                                              gsl[0:1, 6:8]),
             reads=[gs_b], writes=[gs_b])
        p.op("vector", lambda e: e.tensor_scalar_mul(gsl[0:1, 2:4],
                                                     gsl[0:1, 2:4],
                                                     1.0 / 129.0),
             reads=[gs_b], writes=[gs_b])

    t_mm("S", (0, 2), ones[0:1, 0:128], gsl[0:1, 0:2], True, True,
         [gs_b, CB["c_ones"]])
    p.op("vector", lambda e: e.tensor_copy(tcol.ap()[:, 0:2],
                                           ps["S"].ap()[:, 0:2]),
         reads=[pbuf["S"]], writes=[tcol_b])
    p.op("vector", lambda e: e.tensor_mul(tcol.ap()[:, 2:4],
                                          tcol.ap()[:, 0:2], selb),
         reads=[tcol_b, CB["c_selb"]], writes=[tcol_b])
    p.op("vector", lambda e: e.reduce_sum(
        st_t1, tcol.ap()[:, 2:4], axis=AX.X), reads=[tcol_b], writes=[stat_b])

    mloc, (mloc_b,) = A.alloc("mloc", [128, NTT], F32)
    pexc, (pexc_b,) = A.alloc("pexc", [128, NTT], F32)
    lrow, (lrow_b,) = A.alloc("lrow", [1, 4 * NTT], F32)
    sidx, (sidx_b,) = A.alloc("sidx", [128, NTT], F32)
    sidx_i, (sidxi_b,) = A.alloc("sidxi", [128, NTT], I32)
    p.op("vector", lambda e: e.tensor_scalar(mloc.ap(), scsb, st_t1, None,
                                             ALU.is_gt),
         reads=[sc_b, stat_b], writes=[mloc_b])
    t_mm("S", (0, NTT), consts["c_sut"].ap(), mloc.ap(), True, True,
         [mloc_b, CB["c_sut"]])
    p.op("vector", lambda e: e.tensor_copy(pexc.ap(),
                                           ps["S"].ap()[:, 0:NTT]),
         reads=[pbuf["S"]], writes=[pexc_b])
    t_mm("S", (64, 64 + NTT), onecol, mloc.ap(), True, True,
         [mloc_b, CB["c_onecol"]], rows=(0, 1))
    p.op("vector", lambda e: e.tensor_copy(lrow.ap()[0:1, 0:NTT],
                                           ps["S"].ap()[0:1, 64:64 + NTT]),
         reads=[pbuf["S"]], writes=[lrow_b])
    p.op("vector", lambda e: e.tensor_copy(lrow.ap()[0:1, 8:16],
                                           lrow.ap()[0:1, 0:8]),
         reads=[lrow_b], writes=[lrow_b])
    p.op("vector", lambda e: e.tensor_add(lrow.ap()[0:1, 9:16],
                                          lrow.ap()[0:1, 9:16],
                                          lrow.ap()[0:1, 0:7]),
         reads=[lrow_b], writes=[lrow_b])
    p.op("vector", lambda e: e.tensor_copy(lrow.ap()[0:1, 16:24],
                                           lrow.ap()[0:1, 8:16]),
         reads=[lrow_b], writes=[lrow_b])
    p.op("vector", lambda e: e.tensor_add(lrow.ap()[0:1, 18:24],
                                          lrow.ap()[0:1, 18:24],
                                          lrow.ap()[0:1, 8:14]),
         reads=[lrow_b], writes=[lrow_b])
    p.op("vector", lambda e: e.tensor_copy(lrow.ap()[0:1, 24:32],
                                           lrow.ap()[0:1, 16:24]),
         reads=[lrow_b], writes=[lrow_b])
    p.op("vector", lambda e: e.tensor_add(lrow.ap()[0:1, 28:32],
                                          lrow.ap()[0:1, 28:32],
                                          lrow.ap()[0:1, 16:20]),
         reads=[lrow_b], writes=[lrow_b])
    p.op("vector", lambda e: e.tensor_sub(lrow.ap()[0:1, 24:32],
                                          lrow.ap()[0:1, 24:32],
                                          lrow.ap()[0:1, 0:8]),
         reads=[lrow_b], writes=[lrow_b])
    gmask = sq64
    gmask_b = sq64_b
    for bb in range(2):
        p.op("vector", (lambda b2=bb: lambda e: e.tensor_scalar(
            gmask.ap()[:, b2 * 32:(b2 + 1) * 32],
            tab.ap()[:, b2 * 32:(b2 + 1) * 32],
            tcol.ap()[:, b2:b2 + 1], None, ALU.is_gt))(),
             reads=[tab_b, tcol_b], writes=[gmask_b])
    t_mm("S", (0, 64), onecol, gmask.ap(), True, True,
         [gmask_b, CB["c_onecol"]], rows=(0, 1))
    p.op("vector", lambda e: e.tensor_copy(gmask.ap()[0:1, 0:64],
                                           ps["S"].ap()[0:1, 0:64]),
         reads=[pbuf["S"]], writes=[gmask_b])
    p.op("vector", lambda e: e.tensor_mul(gmask.ap()[0:1, 0:64],
                                          gmask.ap()[0:1, 0:64],
                                          consts["c_beforem"].ap()),
         reads=[gmask_b, CB["c_beforem"]], writes=[gmask_b])
    p.op("vector", lambda e: e.reduce_sum(lrow.ap()[0:1, 8:9],
                                          gmask.ap()[0:1, 0:64], axis=AX.X),
         reads=[gmask_b], writes=[lrow_b])
    p.op("vector", lambda e: e.tensor_scalar(lrow.ap()[0:1, 24:32],
                                             lrow.ap()[0:1, 24:32],
                                             lrow.ap()[0:1, 8:9], None,
                                             ALU.add),
         reads=[lrow_b], writes=[lrow_b])
    t_mm("S", (0, NTT), ones[0:1, 0:128], lrow.ap()[0:1, 24:32], True, True,
         [lrow_b, CB["c_ones"]])
    p.op("vector", lambda e: e.tensor_add(sidx.ap(), pexc.ap(),
                                          ps["S"].ap()[:, 0:NTT]),
         reads=[pexc_b, pbuf["S"]], writes=[sidx_b])
    p.op("vector", lambda e: e.tensor_scalar(pexc.ap(), sidx.ap(),
                                             float(KSEL), None, ALU.is_lt),
         reads=[sidx_b], writes=[pexc_b])
    p.op("vector", lambda e: e.tensor_mul(mloc.ap(), mloc.ap(), pexc.ap()),
         reads=[mloc_b, pexc_b], writes=[mloc_b])
    p.op("vector", lambda e: e.tensor_scalar(sidx.ap(), sidx.ap(),
                                             consts["c_b512"].ap()[:, 0:1],
                                             None, ALU.add),
         reads=[sidx_b, CB["c_b512"]], writes=[sidx_b])
    p.op("vector", lambda e: e.tensor_scalar(pexc.ap(), mloc.ap(), -BIG, BIG,
                                             ALU.mult, ALU.add),
         reads=[mloc_b], writes=[pexc_b])
    p.op("vector", lambda e: e.tensor_add(sidx.ap(), sidx.ap(), pexc.ap()),
         reads=[sidx_b, pexc_b], writes=[sidx_b])
    p.op("vector", lambda e: e.tensor_copy(sidx_i.ap(), sidx.ap()),
         reads=[sidx_b], writes=[sidxi_b])
    dbg_store("sidx", [128, NTT], sidx.ap(), [sidx_b])

    selz_dep = p.stream_dep("selz")
    for j in range(NTT):
        p.dma("gpsimd", (lambda jj=j: lambda e: e.indirect_dma_start(
            out=sel_in, out_offset=bass.IndirectOffsetOnAxis(
                sidx_i.ap()[:, jj:jj + 1], 0),
            in_=hln.ap()[:, jj, :], in_offset=None,
            bounds_check=S - 1, oob_is_err=False))(),
            reads=[hln_b[j], sidxi_b], stream="selw",
            deps=[selz_dep])
    selout_b = Buf("sel_out")
    p.coll(lambda e: e.collective_compute(
        "AllReduce", ALU.add, replica_groups=[list(range(NC))],
        ins=[sel_in.opt()], outs=[sel_out.opt()]),
        writes=[selout_b], deps=[p.stream_dep("selw")])

    dbg_dram("selout", sel_out, [S, SELW], BF16, None, reads=[selout_b])
    A.free("scb", "junk", "stab64", "sq64", "xin2", "xr", "lnsc2", "wr_b",
           "sq")
    selr, selr_b = A.alloc("selr", [128, NTT, SELW], BF16, nbufs=NTT)
    p.dma("sync", lambda e: e.dma_start(
        out=selr.ap(), in_=sel_out.rearrange("(j p) c -> p j c", p=128)),
        reads=[selout_b], writes=list(selr_b))
    gates, (gates_b,) = A.alloc("gates", [128, NTT, 1], F32)
    p.op("scalar", lambda e: e.activation(gates.ap(), selr.ap()[:, :, 2050:2051],
                                          AF.Sigmoid),
         reads=list(selr_b), writes=[gates_b])
    posf, (posf_b,) = A.alloc("posf", [128, NTT], F32)
    pos2, (pos2_b,) = A.alloc("pos2", [128, NTT], F32)
    p.op("vector", lambda e: e.tensor_copy(
        posf.ap().rearrange("p (a o) -> p a o", o=1), selr.ap()[:, :, 2048:2049]),
         reads=list(selr_b), writes=[posf_b])
    p.op("vector", lambda e: e.tensor_copy(
        pos2.ap().rearrange("p (a o) -> p a o", o=1), selr.ap()[:, :, 2049:2050]),
         reads=list(selr_b), writes=[pos2_b])
    p.op("vector", lambda e: e.scalar_tensor_tensor(
        posf.ap(), posf.ap(), 64.0, pos2.ap(), ALU.mult, ALU.add),
         reads=[posf_b, pos2_b], writes=[posf_b])
    t_transpose("S", (0, NTT), (0, 128), posf.ap(), identf, [posf_b],
                CB["c_identf"])
    p.op("vector", lambda e: e.tensor_copy(scT.ap(),
                                           ps["S"].ap()[0:NTT, 0:128]),
         reads=[pbuf["S"]], writes=[scT_b])
    p.dma("sync", lambda e: e.dma_start(out=pos_dram, in_=scT.ap()),
          reads=[scT_b], stream="posw")
    posb, (posb_b,) = A.alloc("posb", [128, S], F32)
    p.dma("sync", lambda e: e.dma_start(
        out=posb.ap(), in_=pos_dram.rearrange("(o a) b -> o (a b)", o=1).to_broadcast(
            [128, S])), writes=[posb_b], deps=[p.stream_dep("posw")])
    cosf, (cosf_b,) = A.alloc("cosf", [128, S], F32)
    sinf, (sinf_b,) = A.alloc("sinf", [128, S], F32)
    acyc, (acyc_b,) = A.alloc("acyc", [128, S], F32)
    rr, (rr_b,) = A.alloc("rr", [128, S], F32)
    p.op("vector", lambda e: e.tensor_scalar(
        acyc.ap(), posb.ap(), consts["c_invf"].ap()[:, 0:1], None, ALU.mult),
         reads=[posb_b, CB["c_invf"]], writes=[acyc_b])
    p.op("vector", lambda e: e.tensor_scalar(rr.ap(), acyc.ap(), MAGIC, MAGIC,
                                             ALU.add, ALU.subtract),
         reads=[acyc_b], writes=[rr_b])
    p.op("vector", lambda e: e.tensor_sub(rr.ap(), acyc.ap(), rr.ap()),
         reads=[acyc_b, rr_b], writes=[rr_b])
    p.op("scalar", lambda e: e.activation(sinf.ap(), rr.ap(), AF.Sin,
                                          scale=TWOPI),
         reads=[rr_b], writes=[sinf_b])
    p.op("vector", lambda e: e.tensor_scalar_add(acyc.ap(), acyc.ap(), 0.25),
         reads=[acyc_b], writes=[acyc_b])
    p.op("vector", lambda e: e.tensor_scalar(rr.ap(), acyc.ap(), MAGIC, MAGIC,
                                             ALU.add, ALU.subtract),
         reads=[acyc_b], writes=[rr_b])
    p.op("vector", lambda e: e.tensor_sub(rr.ap(), acyc.ap(), rr.ap()),
         reads=[acyc_b, rr_b], writes=[rr_b])
    p.op("scalar", lambda e: e.activation(cosf.ap(), rr.ap(), AF.Sin,
                                          scale=TWOPI),
         reads=[rr_b], writes=[cosf_b])
    A.free("acyc", "rr", "posb", "posf", "pos2")

    hrow, hrow_b = A.alloc("hrow", [128, 2, E], BF16, nbufs=2)
    hT, hT_bt = A.alloc("hT", [128, 16, S], BF16, nbufs=NTT)
    _sqfa, (_sqfa_b,) = A.alloc("sqfa", [128, E], F32)
    _sqfb, (_sqfb_b,) = A.alloc("sqfb", [128, E], F32)
    sqf_aps = [_sqfa.ap(), _sqfb.ap()]
    sqf_bs = [_sqfa_b, _sqfb_b]
    ln1bh, ln1bb = bcast_load("ln1_b", ln1w, BF16)
    ln1b = ln1bh.ap()

    def rms_stats(src_ap, src_bufs, sl=0):
        st = ST(sl); sb = stat_bs[sl]
        sqv = sqf_aps[sl]; sq_b = sqf_bs[sl]
        p.op("scalar", (lambda s=src_ap, o=sqv: lambda e: e.activation(
            o, s, AF.Square))(), reads=list(src_bufs), writes=[sq_b])
        p.op("vector", (lambda st=st, i=sqv: lambda e: e.reduce_sum(
            st["t1"], i, axis=AX.X))(), reads=[sq_b], writes=[sb])
        p.op("vector", (lambda st=st: lambda e: e.tensor_scalar(
            st["t1"], st["t1"], 1.0 / E, RMS_EPS, ALU.mult, ALU.add))(),
             reads=[sb], writes=[sb])
        p.op("scalar", (lambda st=st: lambda e: e.activation(
            st["t2"], st["t1"], AF.Sqrt))(), reads=[sb], writes=[sb])
        p.op("vector", (lambda st=st: lambda e: e.reciprocal(
            st["rstd"], st["t2"]))(), reads=[sb], writes=[sb])
        return st, sb

    for j in range(NTT):
        sl = j % 2
        st, sb = rms_stats(selr.ap()[:, j, 0:E], [selr_b[j]], sl)
        p.op("vector", (lambda s=sl, jj=j, st=st: lambda e: e.tensor_scalar(
            hrow.ap()[:, s, :], selr.ap()[:, jj, 0:E], st["rstd"], None,
            ALU.mult))(),
             reads=[selr_b[j], sb], writes=[hrow_b[sl]])
        p.op("gpsimd", (lambda s=sl: lambda e: e.tensor_mul(
            hrow.ap()[:, s, :], hrow.ap()[:, s, :], ln1b))(),
             reads=[ln1bb], writes=[hrow_b[sl]])
        pe_transpose_tile(hT.ap(), hT_bt[j], hrow.ap()[:, sl, :],
                          [hrow_b[sl]], j)
    A.free("ln1_b", "hrow")

    wq_sb, (wq_b,) = A.alloc("wq_sb", [128, 16, 256], BF16)
    wk_sb, (wk_b,) = A.alloc("wk_sb", [128, 16, 128], BF16)
    wvs_sb, (wvs_b,) = A.alloc("wvs_sb", [128, 16, 128], BF16)
    for hh, W, bobj in ((wq_sb, wq, wq_b), (wk_sb, wk, wk_b),
                        (wvs_sb, wv2, wvs_b)):
        p.dma("gpsimd", (lambda h=hh, WW=W: lambda e: e.dma_start(
            out=h.ap(), in_=WW.rearrange("(kc p) o -> p kc o", p=128)))(),
            writes=[bobj])
    qf, (qf_b,) = A.alloc("qf", [128, 2, S], F32)
    qbf, (qbf_b,) = A.alloc("qbf", [128, 2, S], BF16)
    kkf, (kkf_b,) = A.alloc("kkf", [128, S], F32)
    kkbf, (kkbf_b,) = A.alloc("kkbf", [128, S], BF16)
    rsc1, (rsc1_b,) = A.alloc("rsc1", [128, 512], F32)
    rsc2, (rsc2_b,) = A.alloc("rsc2", [128, 512], F32)
    permT = consts["c_permT"].ap()

    def rope_fm(dst_ap, dst_buf, src_ap, src_buf):
        for th in range(2):
            cols = (th * 512, (th + 1) * 512)
            t_mm("S", (0, 512), permT, src_ap[:, cols[0]:cols[1]],
                 True, True, [src_buf, CB["c_permT"]])
            p.op("vector", (lambda c=cols: lambda e: e.tensor_mul(
                rsc1.ap(), src_ap[:, c[0]:c[1]], cosf.ap()[:, c[0]:c[1]]))(),
                 reads=[src_buf, cosf_b], writes=[rsc1_b])
            p.op("vector", (lambda c=cols: lambda e: e.tensor_mul(
                rsc2.ap(), ps["S"].ap(), sinf.ap()[:, c[0]:c[1]]))(),
                 reads=[pbuf["S"], sinf_b], writes=[rsc2_b])
            p.op("vector", (lambda c=cols: lambda e: e.tensor_add(
                dst_ap[:, c[0]:c[1]], rsc1.ap(), rsc2.ap()))(),
                 reads=[rsc1_b, rsc2_b], writes=[dst_buf])

    for oh in range(2):
        for th in range(2):
            psn = PSN4[th]
            for kc in range(16):
                t_mm(psn, (0, 512),
                     wq_sb.ap()[:, kc, oh * 128:(oh + 1) * 128],
                     hT.ap()[:, kc, th * 512:(th + 1) * 512],
                     kc == 0, False,
                     [wq_b] + hT_bt[th * 4:(th + 1) * 4])
            p.op("tensor", (lambda o=ps[psn].ap()[:, 0:512], ohh=oh:
                            lambda e: e.matmul(
                                o, consts["r_fbq"].ap()[0:1,
                                                        ohh * 128:(ohh + 1) * 128],
                                ones[0:1, 0:512], start=False, stop=True))(),
                 reads=[CB["r_fbq"], CB["c_ones"]], writes=[pbuf[psn]])
            p.op("vector", (lambda o=qf.ap()[:, oh, th * 512:(th + 1) * 512],
                            s=ps[psn].ap(): lambda e: e.tensor_copy(o, s))(),
                 reads=[pbuf[psn]], writes=[qf_b])
        rope_fm(qbf.ap()[:, oh, :], qbf_b, qf.ap()[:, oh, :], qf_b)
    for th in range(2):
        psn = PSN4[th]
        for kc in range(16):
            t_mm(psn, (0, 512), wk_sb.ap()[:, kc, :],
                 hT.ap()[:, kc, th * 512:(th + 1) * 512],
                 kc == 0, False, [wk_b] + hT_bt[th * 4:(th + 1) * 4])
        p.op("tensor", (lambda o=ps[psn].ap()[:, 0:512]:
                        lambda e: e.matmul(
                            o, consts["r_fbk"].ap()[0:1, 0:128],
                            ones[0:1, 0:512], start=False, stop=True))(),
             reads=[CB["r_fbk"], CB["c_ones"]], writes=[pbuf[psn]])
        p.op("vector", (lambda o=kkf.ap()[:, th * 512:(th + 1) * 512],
                        s=ps[psn].ap(): lambda e: e.tensor_copy(o, s))(),
             reads=[pbuf[psn]], writes=[kkf_b])
    rope_fm(kkbf.ap(), kkbf_b, kkf.ap(), kkf_b)
    vbf, vbf_b = A.alloc("vbf", [128, NTT, 128], BF16, nbufs=NTT)
    for j in range(NTT):
        psn = "B" if j % 2 == 0 else "C"
        for kc in range(16):
            t_mm(psn, (0, 128), hT.ap()[:, kc, j * 128:(j + 1) * 128],
                 wvs_sb.ap()[:, kc, :], kc == 0, False, [wvs_b, hT_bt[j]])
        p.op("tensor", (lambda o=ps[psn].ap()[:, 0:128]:
                        lambda e: e.matmul(
                            o, ones[0:1, 0:128],
                            consts["r_fbv"].ap()[0:1, 0:128],
                            start=False, stop=True))(),
             reads=[CB["r_fbv"], CB["c_ones"]], writes=[pbuf[psn]])
        p.op("vector", (lambda o=vbf.ap()[:, j, :], s=ps[psn].ap()[:, 0:128]:
                        lambda e: e.tensor_copy(o, s))(),
             reads=[pbuf[psn]], writes=[vbf_b[j]])

    A.free("wq_sb", "wk_sb", "wvs_sb", "rsc1", "rsc2", "qf", "kkf")
    wo_sb, (wo_b,) = A.alloc("wo_sb", [128, 2, E], BF16)
    p.dma("gpsimd", lambda e: e.dma_start(
        out=wo_sb.ap(), in_=wo2.rearrange("(h p) o -> p h o", p=128)),
        writes=[wo_b])
    ao, (ao_b,) = A.alloc("ao", [128, 2, S], BF16)
    pex, (pex_b,) = A.alloc("pex", [128, S], BF16)
    attT, attT_b = A.alloc("attT", [128, NTT, 128], BF16, nbufs=NTT)
    tril = consts["c_tril"].ap()
    for qh in range(2):
        for r in range(NTT):
            nk = (r + 1) * 128
            for kb in range(r + 1):
                t_mm("A", (kb * 128, kb * 128 + 128),
                     qbf.ap()[:, qh, r * 128:(r + 1) * 128],
                     kkbf.ap()[:, kb * 128:(kb + 1) * 128],
                     True, True, [qbf_b, kkbf_b])
            p.op("vector", (lambda rr2=r: lambda e: e.tensor_add(
                ps["A"].ap()[:, rr2 * 128:(rr2 + 1) * 128],
                ps["A"].ap()[:, rr2 * 128:(rr2 + 1) * 128], tril))(),
                 reads=[CB["c_tril"], pbuf["A"]], writes=[pbuf["A"]])
            p.op("vector", (lambda n=nk: lambda e: e.reduce_max(
                st_t1, ps["A"].ap()[:, 0:n], axis=AX.X))(),
                 reads=[pbuf["A"]], writes=[stat_b])
            p.op("vector", lambda e: e.tensor_scalar_mul(st_t2, st_t1, -ISQD),
                 reads=[stat_b], writes=[stat_b])
            p.op("scalar", (lambda n=nk: lambda e: e.activation(
                pex.ap()[:, 0:n], ps["A"].ap()[:, 0:n], AF.Exp,
                bias=st_t2, scale=ISQD))(),
                 reads=[pbuf["A"], stat_b], writes=[pex_b])
            p.op("vector", (lambda n=nk: lambda e: e.reduce_sum(
                st_t1, pex.ap()[:, 0:n], axis=AX.X))(),
                 reads=[pex_b], writes=[stat_b])
            p.op("vector", lambda e: e.reciprocal(st_t3, st_t1),
                 reads=[stat_b], writes=[stat_b])
            p.op("vector", (lambda n=nk: lambda e: e.tensor_scalar(
                pex.ap()[:, 0:n], pex.ap()[:, 0:n], st_t3, None, ALU.mult))(),
                 reads=[pex_b, stat_b], writes=[pex_b])
            for kb in range(r + 1):
                t_transpose("T", (0, 128),
                            ((kb % 8) * 128, (kb % 8) * 128 + 128),
                            pex.ap()[:, kb * 128:(kb + 1) * 128], identb,
                            [pex_b], CB["c_identb"])
                p.op("vector", (lambda kbb=kb: lambda e: e.tensor_copy(
                    attT.ap()[:, kbb, :],
                    ps["T"].ap()[:, (kbb % 8) * 128:(kbb % 8) * 128 + 128]))(),
                     reads=[pbuf["T"]], writes=[attT_b[kb]])
            psn = "B"
            for kb in range(r + 1):
                t_mm(psn, (0, 128), vbf.ap()[:, kb, :], attT.ap()[:, kb, :],
                     kb == 0, kb == r, [vbf_b[kb], attT_b[kb]])
            p.op("vector", (lambda o=ao.ap()[:, qh, r * 128:(r + 1) * 128],
                            s=ps[psn].ap()[:, 0:128]:
                            lambda e: e.tensor_copy(o, s))(),
                 reads=[pbuf[psn]], writes=[ao_b])

    apbf, apbf_b = A.alloc("apbf", [128, 2, E], BF16, nbufs=2)
    for j in range(NTT):
        sl = j % 2
        for ec in range(4):
            psn = PSN4[ec]
            for h in range(2):
                t_mm(psn, (0, 512), ao.ap()[:, h, j * 128:(j + 1) * 128],
                     wo_sb.ap()[:, h, ec * 512:(ec + 1) * 512],
                     h == 0, h == 1, [ao_b, wo_b])
            p.op("vector", (lambda o=apbf.ap()[:, sl, ec * 512:(ec + 1) * 512],
                            s=ps[psn].ap(): lambda e: e.tensor_copy(o, s))(),
                 reads=[pbuf[psn]], writes=[apbf_b[sl]])
        p.dma("sync", (lambda s=sl, jj=j: lambda e: e.dma_start(
            out=ar2_in[jj * 128:(jj + 1) * 128, :], in_=apbf.ap()[:, s, :]))(),
            reads=[apbf_b[sl]], stream="a2w")
    ar2out_b = Buf("ar2_out")
    p.coll(lambda e: e.collective_compute(
        "AllReduce", ALU.add, replica_groups=[list(range(NC))],
        ins=[ar2_in.opt()], outs=[ar2_out.opt()]),
        writes=[ar2out_b], deps=[p.stream_dep("a2w")])
    A.free("wo_sb", "apbf")

    dbg_dram("a2", ar2_out, [S, E], BF16, None, reads=[ar2out_b])
    A.free("hT", "qbf", "kkbf", "vbf", "ao", "pex", "attT",
           "cosf", "sinf", "hln", "tab", "gs", "cnt2", "tcol", "mloc",
           "pexc", "lrow", "sidx", "scT")
    a2r, a2r_b = A.alloc("a2r", [128, NTT, E], BF16, nbufs=NTT)
    p.dma("sync", lambda e: e.dma_start(
        out=a2r.ap(), in_=ar2_out.rearrange("(j p) c -> p j c", p=128)),
        reads=[ar2out_b], writes=list(a2r_b))
    h2T, h2T_bt = A.alloc("h2T", [128, 16, S], BF16, nbufs=NTT)
    x1bf, x1bf_b = A.alloc("x1bf", [128, NTT, E], BF16, nbufs=NTT)
    _x1fa, (_x1fa_b,) = A.alloc("x1fa", [128, E], F32)
    _x1fb, (_x1fb_b,) = A.alloc("x1fb", [128, E], F32)
    x1f_aps = [_x1fa.ap(), _x1fb.ap()]
    x1f_bs = [_x1fa_b, _x1fb_b]
    h2row, h2row_b = A.alloc("h2row", [128, 2, E], BF16, nbufs=2)
    ln2bh, ln2bb = bcast_load("ln2_b", ln2w, BF16)
    ln2b = ln2bh.ap()
    for j in range(NTT):
        sl = j % 2
        p.op("vector", (lambda jj=j, s=sl: lambda e: e.tensor_add(
            x1f_aps[s], a2r.ap()[:, jj, :], selr.ap()[:, jj, 0:E]))(),
             reads=[a2r_b[j], selr_b[j]], writes=[x1f_bs[sl]])
        p.op("vector", (lambda jj=j, s=sl: lambda e: e.tensor_copy(
            x1bf.ap()[:, jj, :], x1f_aps[s]))(),
             reads=[x1f_bs[sl]], writes=[x1bf_b[j]])
        st, sb = rms_stats(x1f_aps[sl], [x1f_bs[sl]], sl)
        p.op("vector", (lambda s=sl, st=st: lambda e: e.tensor_scalar(
            h2row.ap()[:, s, :], x1f_aps[s], st["rstd"], None,
            ALU.mult))(),
             reads=[x1f_bs[sl], sb], writes=[h2row_b[sl]])
        p.op("gpsimd", (lambda s=sl: lambda e: e.tensor_mul(
            h2row.ap()[:, s, :], h2row.ap()[:, s, :], ln2b))(),
             reads=[ln2bb], writes=[h2row_b[sl]])
        pe_transpose_tile(h2T.ap(), h2T_bt[j], h2row.ap()[:, sl, :],
                          [h2row_b[sl]], j)
    A.free("a2r", "x1fa", "x1fb", "sqfa", "sqfb", "h2row", "ln2_b")

    wg_sb, wg_b = A.alloc("wg_sb", [128, 2, 16, 128], BF16, nbufs=2)
    wu_sb, wu_b = A.alloc("wu_sb", [128, 2, 16, 128], BF16, nbufs=2)
    mbf, (mbf_b,) = A.alloc("mbf", [128, NTT, S], BF16)
    gsc, (gsc_b,) = A.alloc("gsc", [128, 512], F32)
    for oc in range(NTT):
        slot = oc % 2
        p.dma("gpsimd", (lambda s=slot, occ=oc: lambda e: e.dma_start(
            out=wg_sb.ap()[:, s, :, :],
            in_=wgate[:, occ * 128:(occ + 1) * 128].rearrange(
                "(kc p) o -> p kc o", p=128)))(), writes=[wg_b[slot]])
        p.dma("gpsimd", (lambda s=slot, occ=oc: lambda e: e.dma_start(
            out=wu_sb.ap()[:, s, :, :],
            in_=wup[:, occ * 128:(occ + 1) * 128].rearrange(
                "(kc p) o -> p kc o", p=128)))(), writes=[wu_b[slot]])
        for th in range(2):
            for kc in range(16):
                t_mm("B", (0, 512), wg_sb.ap()[:, slot, kc, :],
                     h2T.ap()[:, kc, th * 512:(th + 1) * 512],
                     kc == 0, kc == 15,
                     [wg_b[slot]] + h2T_bt[th * 4:(th + 1) * 4])
                t_mm("C", (0, 512), wu_sb.ap()[:, slot, kc, :],
                     h2T.ap()[:, kc, th * 512:(th + 1) * 512],
                     kc == 0, kc == 15,
                     [wu_b[slot]] + h2T_bt[th * 4:(th + 1) * 4])
            p.op("scalar", lambda e: e.activation(gsc.ap(), ps["B"].ap(),
                                                  AF.Silu),
                 reads=[pbuf["B"]], writes=[gsc_b])
            p.op("vector", (lambda o=mbf.ap()[:, oc, th * 512:(th + 1) * 512]:
                            lambda e: e.tensor_mul(o, gsc.ap(),
                                                   ps["C"].ap()))(),
                 reads=[gsc_b, pbuf["C"]], writes=[mbf_b])

    A.free("h2T", "wg_sb", "wu_sb", "gsc")
    wd_sb, wd_b = A.alloc("wd_sb", [128, 2, NTT, 512], BF16, nbufs=2)
    mpq, mpq_b = A.alloc("mpq", [128, 2, 512], BF16, nbufs=2)
    for ec in range(4):
        slot = ec % 2
        p.dma("gpsimd", (lambda s=slot, e2=ec: lambda e: e.dma_start(
            out=wd_sb.ap()[:, s, :, :],
            in_=wdown[:, e2 * 512:(e2 + 1) * 512].rearrange(
                "(ic p) o -> p ic o", p=128)))(), writes=[wd_b[slot]])
        for j in range(NTT):
            psn = "B" if j % 2 == 0 else "C"
            for ic in range(NTT):
                t_mm(psn, (0, 512), mbf.ap()[:, ic, j * 128:(j + 1) * 128],
                     wd_sb.ap()[:, slot, ic, :],
                     ic == 0, ic == NTT - 1, [mbf_b, wd_b[slot]])
            msl = j % 2
            p.op("vector", (lambda o=mpq.ap()[:, msl, :], s=ps[psn].ap():
                            lambda e: e.tensor_copy(o, s))(),
                 reads=[pbuf[psn]], writes=[mpq_b[msl]])
            p.dma("sync", (lambda s=msl, jj=j, e2=ec: lambda e: e.dma_start(
                out=ar3_in[jj * 128:(jj + 1) * 128,
                           e2 * 512:(e2 + 1) * 512],
                in_=mpq.ap()[:, s, :]))(),
                reads=[mpq_b[msl]], stream="a3w")
    ar3out_b = Buf("ar3_out")
    p.coll(lambda e: e.collective_compute(
        "AllReduce", ALU.add, replica_groups=[list(range(NC))],
        ins=[ar3_in.opt()], outs=[ar3_out.opt()]),
        writes=[ar3out_b], deps=[p.stream_dep("a3w")])

    dbg_dram("a3", ar3_out, [S, E], BF16, None, reads=[ar3out_b])
    A.free("mbf", "wd_sb", "mpq")
    a3r, a3r_b = A.alloc("a3r", [128, NTT, E], BF16, nbufs=NTT)
    p.dma("sync", lambda e: e.dma_start(
        out=a3r.ap(), in_=ar3_out.rearrange("(j p) c -> p j c", p=128)),
        reads=[ar3out_b], writes=list(a3r_b))
    procf, (procf_b,) = A.alloc("procf", [128, E], F32)
    dlb, dlb_b = A.alloc("dlb", [128, 2, E], BF16, nbufs=2)
    for j in range(NTT):
        sl = j % 2
        p.op("vector", (lambda jj=j: lambda e: e.tensor_add(
            procf.ap(), a3r.ap()[:, jj, :], x1bf.ap()[:, jj, :]))(),
             reads=[a3r_b[j], x1bf_b[j]], writes=[procf_b])
        p.op("vector", (lambda jj=j: lambda e: e.tensor_sub(
            procf.ap(), procf.ap(), selr.ap()[:, jj, 0:E]))(),
             reads=[procf_b, selr_b[j]], writes=[procf_b])
        p.op("vector", (lambda s=sl, jj=j: lambda e: e.tensor_scalar(
            dlb.ap()[:, s, :], procf.ap(), gates.ap()[:, jj, 0:1], None,
            ALU.mult))(),
             reads=[procf_b, gates_b], writes=[dlb_b[sl]])
        p.dma("sync", (lambda s=sl, jj=j: lambda e: e.dma_start(
            out=delta_dram[jj * 128:(jj + 1) * 128, :],
            in_=dlb.ap()[:, s, :]))(),
            reads=[dlb_b[sl]], stream="dlw")
    dl_dep = p.stream_dep("dlw")
    A.free("a3r", "x1bf", "selr", "procf", "dlb")
    xrr, xrr_b = A.alloc("xrr", [128, 2, E], F32, nbufs=2)
    hlr, hlr_b = A.alloc("hlr", [128, 2, E], BF16, nbufs=2)
    dgt, dgt_b = A.alloc("dgt", [128, 2, E], BF16, nbufs=2)
    outf, outf_b = A.alloc("outf", [128, 2, E], F32, nbufs=2)
    dbg_dram("delta", delta_dram, [S, E], BF16, "dlw")
    xr_dep = p.stream_dep("xrw")
    hl_dep = p.stream_dep("hlw")
    for j in range(NTT):
        sl = j % 2
        p.dma("sync", (lambda s=sl, jj=j: lambda e: e.dma_start(
            out=xrr.ap()[:, s, :], in_=xres_dram[jj * 128:(jj + 1) * 128, :]))(),
            writes=[xrr_b[sl]], deps=[xr_dep])
        p.dma("scalar", (lambda s=sl, jj=j: lambda e: e.dma_start(
            out=hlr.ap()[:, s, :], in_=hln_dram[jj * 128:(jj + 1) * 128, :]))(),
            writes=[hlr_b[sl]], deps=[hl_dep])
        p.op("vector", (lambda s=sl: lambda e: e.memset(
            dgt.ap()[:, s, :], 0.0))(), writes=[dgt_b[sl]])
        p.dma("gpsimd", (lambda s=sl, jj=j: lambda e: e.indirect_dma_start(
            out=dgt.ap()[:, s, :], out_offset=None,
            in_=delta_dram, in_offset=bass.IndirectOffsetOnAxis(
                sidx_i.ap()[:, jj:jj + 1], 0),
            bounds_check=S - 1, oob_is_err=False))(),
            reads=[sidxi_b], writes=[dgt_b[sl]], deps=[dl_dep])
        p.op("vector", (lambda s=sl: lambda e: e.tensor_add(
            outf.ap()[:, s, :], xrr.ap()[:, s, :], hlr.ap()[:, s, :]))(),
             reads=[xrr_b[sl], hlr_b[sl]], writes=[outf_b[sl]])
        p.op("vector", (lambda s=sl: lambda e: e.tensor_add(
            outf.ap()[:, s, :], outf.ap()[:, s, :], dgt.ap()[:, s, :]))(),
             reads=[outf_b[sl], dgt_b[sl]], writes=[outf_b[sl]])
        p.dma("sync", (lambda s=sl, jj=j: lambda e: e.dma_start(
            out=y_out[jj * 128:(jj + 1) * 128, :], in_=outf.ap()[:, s, :]))(),
            reads=[outf_b[sl]], stream="yw")

    p.final_wait_streams(["yw"] + (["dbg"] if "dbg" in p.streams else []))
    p.finish()
    return nc, dbg_outs


def _shard_inputs(inputs):
    x = np.ascontiguousarray(np.asarray(inputs["L0_attn_x"],
                                        dtype=np.float32).reshape(T, E))
    per = []
    for c in range(NC):
        d = {
            "x": x[c * TPC:(c + 1) * TPC],
            "Wqk": inputs["L0_attn_Wqk"],
            "bqk": inputs["L0_attn_bqk"].reshape(1, E),
            "Wv": inputs["L0_attn_Wv"], "bv": inputs["L0_attn_bv"].reshape(1, E),
            "Wu": inputs["L0_attn_Wu"], "bu": inputs["L0_attn_bu"].reshape(1, E),
            "Wo": inputs["L0_attn_Wo"], "bo": inputs["L0_attn_bo"].reshape(1, E),
            "ln_g": inputs["L0_attn_ln_g"].reshape(1, E),
            "ln_b": inputs["L0_attn_ln_b"].reshape(1, E),
            "w_router": inputs["L0_ffn_w_router"].reshape(1, E),
            "wq": inputs["L0_ffn_wq"][:, 256 * c:256 * (c + 1)],
            "fbq": inputs["L0_ffn_bq"][256 * c:256 * (c + 1)].reshape(1, 256),
            "wk": inputs["L0_ffn_wk"][:, 128 * c:128 * (c + 1)],
            "fbk": inputs["L0_ffn_bk"][128 * c:128 * (c + 1)].reshape(1, 128),
            "wv2": inputs["L0_ffn_wv"][:, 128 * c:128 * (c + 1)],
            "fbv": inputs["L0_ffn_bv"][128 * c:128 * (c + 1)].reshape(1, 128),
            "wo2": inputs["L0_ffn_wo"][256 * c:256 * (c + 1), :],
            "wgate": inputs["L0_ffn_w_gate"][:, 1024 * c:1024 * (c + 1)],
            "wup": inputs["L0_ffn_w_up"][:, 1024 * c:1024 * (c + 1)],
            "wdown": inputs["L0_ffn_w_down"][1024 * c:1024 * (c + 1), :],
            "ln1w": inputs["L0_ffn_ln1_w"].reshape(1, E),
            "ln2w": inputs["L0_ffn_ln2_w"].reshape(1, E),
        }
        d = {k: np.ascontiguousarray(np.asarray(v, dtype=np.float32))
             for k, v in d.items()}
        d.update(_host_constants(c))
        per.append(d)
    return per


LAST_DBG = {}


def kernel(**inputs):
    nc, dbg = build_program()
    in_maps = _shard_inputs(inputs)
    res = run_bass_kernel_spmd(nc, in_maps, core_ids=list(range(NC)))
    global LAST_DBG
    LAST_DBG = {name: [np.asarray(res.results[c]["dbg_" + name])
                       for c in range(NC)] for name in dbg}
    out = np.concatenate([np.asarray(res.results[c]["y"]) for c in range(NC)],
                         axis=0)
    return out.reshape(B, N, E)
